# version 09
# speedup vs baseline: 1.1426x; 1.0109x over previous
import math
import numpy as np
import ml_dtypes
import concourse.bass as bass
import concourse.bacc as bacc
import concourse.mybir as mybir
from concourse.bass_utils import run_bass_kernel_spmd

F32 = mybir.dt.float32
BF16 = mybir.dt.bfloat16
AF = mybir.ActivationFunctionType
ALU = mybir.AluOpType
AX = mybir.AxisListType.X

B = 2
T = 8192
C = 1024
H = 16
N = 64
NC = 8
TPC = 2048
NT = TPC // 128
LNEPS = 1e-5
GNEPS = 0.00064
GAMMA = 0.96875
LOG_GAMMA = math.log(GAMMA)
CHUNK = 128
NCHUNK = T // CHUNK

bf = ml_dtypes.bfloat16


def _bass():
    return bass.Bass("TRN2", target_bir_lowering=False, detect_race_conditions=False)


def _run(nc, in_maps):
    mybir.codegen_inst_isa_subclasses(nc)
    return run_bass_kernel_spmd(nc, in_maps, core_ids=list(range(NC)))



class SemSeq:

    def __init__(self, stack, nc, name, inc, total):
        self.k = max(1, -(-(inc * total) // 240))
        self.sems = [stack.enter_context(nc.semaphore(f"{name}{j}")) for j in range(self.k)]
        self.inc = inc

    def attach(self, inst, i):
        inst.then_inc(self.sems[i % self.k], self.inc)
        return inst

    def wait(self, eng, i):
        if i < 0:
            return
        for j in range(self.k):
            cnt = (i - j) // self.k + 1 if i >= j else 0
            if cnt > 0:
                eng.wait_ge(self.sems[j], self.inc * cnt)


def emit_stats1024(v, xv, scr, sm, var, mean):
    v.tensor_tensor(scr, xv, xv, ALU.mult)
    v.reduce_sum(sm, xv, AX)
    v.drain()
    v.reduce_sum(var, scr, AX)
    v.tensor_scalar(mean, sm, 1.0 / 1024.0, None, ALU.mult)
    v.drain()
    v.tensor_scalar(var, var, 1.0 / 1024.0, None, ALU.mult)
    v.tensor_tensor(sm, mean, mean, ALU.mult)
    v.drain()
    v.tensor_tensor(var, var, sm, ALU.subtract)
    v.drain()


def emit_invsq(v, var, ur, vt):
    v.tensor_scalar(ur, var, float(LNEPS), None, ALU.add)
    v.drain()
    v.reciprocal(vt, ur)
    v.drain()
    v.tensor_tensor(vt, vt, var, ALU.mult)
    v.drain()
    v.tensor_scalar(vt, vt, float(LNEPS), None, ALU.add)
    v.drain()
    v.tensor_tensor(vt, vt, ur, ALU.mult)
    v.drain()
    v.reciprocal(ur, vt)
    v.drain()


def emit_inv(v, var, ur):
    v.tensor_scalar(ur, var, float(LNEPS), None, ALU.add)
    v.drain()
    v.reciprocal(ur, ur)
    v.drain()




def build_pA():
    nc = _bass()
    x = nc.dram_tensor("x", [TPC, C], F32, kind="ExternalInput").ap()
    lnpw = nc.dram_tensor("lnpw", [128, C], F32, kind="ExternalInput").ap()
    lnpb = nc.dram_tensor("lnpb", [128, C], F32, kind="ExternalInput").ap()
    xp = nc.dram_tensor("xp", [TPC, C], F32, kind="ExternalOutput").ap()
    z2 = nc.dram_tensor("z2", [TPC, C], BF16, kind="ExternalOutput").ap()

    from contextlib import ExitStack
    with ExitStack() as stack:
        e = stack.enter_context
        xt = e(nc.sbuf_tensor("xt", [128, 2 * C], F32))
        wt = e(nc.sbuf_tensor("wt", [128, C], F32))
        bt = e(nc.sbuf_tensor("bt", [128, C], F32))
        xpt = e(nc.sbuf_tensor("xpt", [128, 2 * C], F32))
        z2t = e(nc.sbuf_tensor("z2t", [128, 2 * C], BF16))
        scr = e(nc.sbuf_tensor("scr", [128, C], F32))
        sm = e(nc.sbuf_tensor("sm", [128, 1], F32))
        mean = e(nc.sbuf_tensor("mean", [128, 1], F32))
        var = e(nc.sbuf_tensor("var", [128, 1], F32))
        ur1 = e(nc.sbuf_tensor("ur1", [128, 1], F32))
        ur2 = e(nc.sbuf_tensor("ur2", [128, 1], F32))
        vt = e(nc.sbuf_tensor("vt", [128, 1], F32))
        sl = SemSeq(stack, nc, "sl", 16, NT)
        so = SemSeq(stack, nc, "so", 16, 2 * NT)
        sb = e(nc.semaphore("sb"))
        sv = e(nc.semaphore("sv"))
        sa = e(nc.semaphore("sa"))
        block = e(nc.Block())

        @block.sync
        def _(sync):
            for i in range(NT):
                if i >= 2:
                    sync.wait_ge(sv, 4 * (i - 2) + 3)
                sl.attach(sync.dma_start(out=xt[:, (i % 2) * C:(i % 2 + 1) * C],
                                         in_=x[128 * i:128 * (i + 1), :]), i)

        @block.vector
        def _(vector):
            for i in range(NT):
                sli = slice((i % 2) * C, (i % 2 + 1) * C)
                sl.wait(vector, i)
                if i == 0:
                    vector.wait_ge(sb, 32)
                if i >= 2:
                    so.wait(vector, 2 * (i - 2) + 1)
                xv = xt[:, sli]
                emit_stats1024(vector, xv, scr[:], sm[:], var[:], mean[:])
                emit_invsq(vector, var[:], ur1[:], vt[:])
                vector.engine_nop().then_inc(sv, 1)
                vector.wait_ge(sa, 2 * i + 1)
                vector.tensor_scalar(xpt[:, sli], xv, mean[:], ur1[:], ALU.subtract, ALU.mult)
                vector.drain()
                vector.tensor_tensor(xpt[:, sli], xpt[:, sli], wt[:], ALU.mult)
                vector.drain()
                vector.tensor_tensor(xpt[:, sli], xpt[:, sli], bt[:], ALU.add).then_inc(sv, 1)
                vector.drain()
                emit_stats1024(vector, xpt[:, sli], scr[:], sm[:], var[:], mean[:])
                emit_inv(vector, var[:], ur2[:])
                vector.engine_nop().then_inc(sv, 1)
                vector.wait_ge(sa, 2 * i + 2)
                vector.tensor_scalar(z2t[:, sli], xpt[:, sli], mean[:], ur2[:],
                                     ALU.subtract, ALU.mult).then_inc(sv, 1)

        @block.scalar
        def _(scalar):
            scalar.preload_act_table(AF.Sqrt)
            for i in range(NT):
                scalar.wait_ge(sv, 4 * i + 1)
                scalar.activation(ur1[:], ur1[:], AF.Sqrt).then_inc(sa, 1)
                scalar.wait_ge(sv, 4 * i + 3)
                scalar.activation(ur2[:], ur2[:], AF.Sqrt).then_inc(sa, 1)

        @block.gpsimd
        def _(gpsimd):
            gpsimd.dma_start(out=wt[:], in_=lnpw).then_inc(sb, 16)
            gpsimd.dma_start(out=bt[:], in_=lnpb).then_inc(sb, 16)
            for i in range(NT):
                gpsimd.wait_ge(sv, 4 * i + 2)
                so.attach(gpsimd.dma_start(out=xp[128 * i:128 * (i + 1), :],
                                           in_=xpt[:, (i % 2) * C:(i % 2 + 1) * C]), 2 * i)
                gpsimd.wait_ge(sv, 4 * i + 4)
                so.attach(gpsimd.dma_start(out=z2[128 * i:128 * (i + 1), :],
                                           in_=z2t[:, (i % 2) * C:(i % 2 + 1) * C]), 2 * i + 1)
            so.wait(gpsimd, 2 * NT - 1)
    return nc


def run_pA(inputs):
    nc = build_pA()
    x = np.asarray(inputs["L0_attn_x"], np.float32).reshape(B * T, C)
    lnpw = np.asarray(inputs["L0_attn_ln_pre_w"], np.float32)
    lnpb = np.asarray(inputs["L0_attn_ln_pre_b"], np.float32)
    lnpw = np.ascontiguousarray(np.broadcast_to(lnpw[None, :], (128, C)))
    lnpb = np.ascontiguousarray(np.broadcast_to(lnpb[None, :], (128, C)))
    in_maps = [{"x": np.ascontiguousarray(x[TPC * c:TPC * (c + 1)]),
                "lnpw": lnpw, "lnpb": lnpb} for c in range(NC)]
    res = _run(nc, in_maps)
    xp = np.concatenate([np.asarray(r["xp"]) for r in res.results], 0)
    z2 = np.concatenate([np.asarray(r["z2"]) for r in res.results], 0)
    return xp, z2




SL = 1024
NSTRIP = T // SL
CPS = SL // CHUNK
PB_STAGE = 7




def _finalize_pb(block, prog):
    @block.sync
    def _(sy):
        for f in prog["S"]:
            f(sy)

    @block.tensor
    def _(t):
        for f in prog["T"]:
            f(t)

    @block.vector
    def _(v):
        for f in prog["V"]:
            f(v)

    @block.scalar
    def _(a):
        for f in prog["A"]:
            f(a)

    @block.gpsimd
    def _(g):
        for f in prog["G"]:
            f(g)


def build_pB():
    nc = _bass()
    z2T = nc.dram_tensor("z2T", [C, T + 1], BF16, kind="ExternalInput").ap()
    wcat = nc.dram_tensor("wcat", [C, 1024], F32, kind="ExternalInput").ap()
    mixcat = nc.dram_tensor("mixcat", [128, 48], F32, kind="ExternalInput").ap()
    lnwb = nc.dram_tensor("lnwb", [128, 16], F32, kind="ExternalInput").ap()
    w2sI = nc.dram_tensor("w2sI", [64, 256], F32, kind="ExternalInput").ap()
    a2sI = nc.dram_tensor("a2sI", [64, 256], F32, kind="ExternalInput").ap()
    g2sI = nc.dram_tensor("g2sI", [128, 256], F32, kind="ExternalInput").ap()
    smallc = nc.dram_tensor("smallc", [128, 10], F32, kind="ExternalInput").ap()
    gnwb = nc.dram_tensor("gnwb", [128, 512], F32, kind="ExternalInput").ap()
    og = nc.dram_tensor("og", [T, 256], BF16, kind="ExternalOutput").ap()
    vbx = nc.dram_tensor("vbx", [1, 256], F32, kind="Internal").ap()

    SLH = SL // 2
    NTT = SL // 128
    from contextlib import ExitStack
    with ExitStack() as stack:
        e = stack.enter_context
        mixc = e(nc.sbuf_tensor("mixc", [128, 48], F32))
        lnt = e(nc.sbuf_tensor("lnt", [128, 16], F32))
        smt = e(nc.sbuf_tensor("smt", [128, 10], F32))
        onemka = e(nc.sbuf_tensor("onemka", [128, 2], F32))
        gnt = e(nc.sbuf_tensor("gnt", [128, 512], F32))
        sab = e(nc.sbuf_tensor("sab", [128, 96], F32))
        w2t = e(nc.sbuf_tensor("w2t", [64, 256], BF16))
        a2t = e(nc.sbuf_tensor("a2t", [128, 256], BF16))
        g2t = e(nc.sbuf_tensor("g2t", [128, 256], BF16))
        biasc = e(nc.sbuf_tensor("biasc", [128, 6], F32))
        vbrow = e(nc.sbuf_tensor("vbrow", [1, 256], F32))
        vbrowB = e(nc.sbuf_tensor("vbrowB", [128, 256], F32))
        vbrowS = e(nc.sbuf_tensor("vbrowS", [128, 256], BF16))
        lnbbB = e(nc.sbuf_tensor("lnbbB", [128, 8 * 128], BF16))
        onesF = e(nc.sbuf_tensor("onesF", [128, 128], F32))
        onesB = e(nc.sbuf_tensor("onesB", [128, 128], BF16))
        blkones = e(nc.sbuf_tensor("blkones", [128, 128], BF16))
        ident = e(nc.sbuf_tensor("ident", [128, 128], BF16))
        mlt = e(nc.sbuf_tensor("mlt", [128, 128], F32))
        mgt = e(nc.sbuf_tensor("mgt", [128, 128], F32))
        mle = e(nc.sbuf_tensor("mle", [128, 128], F32))
        ind64 = e(nc.sbuf_tensor("ind64", [128, 64], BF16))
        wtmpb = e(nc.sbuf_tensor("wtmpb", [128, 8 * 1024], BF16))
        wzs = e(nc.sbuf_tensor("wzs", [128, 8 * 1024], BF16))
        wz = wtmpb
        s0f = e(nc.sbuf_tensor("s0f", [128, 128], F32))
        s0b = e(nc.sbuf_tensor("s0b", [128, 128], BF16))
        sTmp = e(nc.sbuf_tensor("sTmp", [128, 128], F32))
        zst = e(nc.sbuf_tensor("zst", [128, 8 * (SL + 1)], BF16))
        rS = e(nc.sbuf_tensor("rS", [128, 2 * SL], F32))
        kS = e(nc.sbuf_tensor("kS", [128, 2 * SL], F32))
        dsig = e(nc.sbuf_tensor("dsig", [128, 2 * SL], F32))
        asig = e(nc.sbuf_tensor("asig", [128, 2 * SL], F32))
        zwza = e(nc.sbuf_tensor("zwza", [128, SL], BF16))
        zg = e(nc.sbuf_tensor("zg", [128, SL], BF16))
        vS = e(nc.sbuf_tensor("vS", [128, 8 * 256], BF16))
        gS = e(nc.sbuf_tensor("gS", [128, 8 * 256], BF16))
        ca = e(nc.sbuf_tensor("ca", [128, 2 * SL], F32))
        cprev = e(nc.sbuf_tensor("cprev", [128, 2 * SL], F32))
        cinv = e(nc.sbuf_tensor("cinv", [128, 2 * SL], F32))
        SCR1 = e(nc.sbuf_tensor("SCR1", [128, 2 * SL], F32))
        SCR2 = e(nc.sbuf_tensor("SCR2", [128, 2 * SL], F32))
        SCB1 = e(nc.sbuf_tensor("SCB1", [128, 2 * SL], BF16))
        atil = e(nc.sbuf_tensor("atil", [128, 2 * SL], BF16))
        btil = e(nc.sbuf_tensor("btil", [128, 2 * SL], BF16))
        ktil = e(nc.sbuf_tensor("ktil", [128, 2 * SL], BF16))
        rtil = e(nc.sbuf_tensor("rtil", [128, 2 * SL], BF16))
        rkpb = e(nc.sbuf_tensor("rkpb", [128, 2 * SL], BF16))
        ogS = e(nc.sbuf_tensor("ogS", [128, 8 * 256], BF16))
        mtS = e(nc.sbuf_tensor("mtS", [128, 512], BF16))
        mS = e(nc.sbuf_tensor("mS", [128, 512], BF16))
        gtS = e(nc.sbuf_tensor("gtS", [128, 512], BF16))
        d1S = e(nc.sbuf_tensor("d1S", [128, 512], BF16))
        d2S = e(nc.sbuf_tensor("d2S", [128, 512], BF16))
        pwT = [e(nc.sbuf_tensor(f"pwT{j}", [128, 512], BF16)) for j in range(2)]
        pwU = [e(nc.sbuf_tensor(f"pwU{j}", [128, 512], BF16)) for j in range(2)]
        uB = e(nc.sbuf_tensor("uB", [128, 256], BF16))
        bkT = e(nc.sbuf_tensor("bkT", [128, 512], BF16))
        gnm = e(nc.sbuf_tensor("gnm", [128, 4], F32))
        gns = e(nc.sbuf_tensor("gns", [128, 4], F32))
        gnr = e(nc.sbuf_tensor("gnr", [128, 4], F32))
        rksum = e(nc.sbuf_tensor("rksum", [128, 4], F32))
        ogchunk = e(nc.sbuf_tensor("ogchunk", [128, 256], F32))
        oq = e(nc.sbuf_tensor("oq", [128, 256], F32))
        pbS = [e(nc.psum_tensor(f"pbS{j}", [128, 512], F32)) for j in range(2)]
        pbV = [e(nc.psum_tensor("pbV0", [128, 256], F32))]
        pbP = [e(nc.psum_tensor(f"pbP{j}", [128, 128], F32)) for j in range(2)]
        pbQ = [e(nc.psum_tensor(f"pbQ{j}", [128, 64], F32)) for j in range(2)]
        pbX = [e(nc.psum_tensor("pbX0", [128, 128], BF16))]

        sqD = SemSeq(stack, nc, "sqD", 16, NSTRIP + 8)
        sqO = SemSeq(stack, nc, "sqO", 16, NSTRIP + 2)
        sqT = SemSeq(stack, nc, "sqT", 1, 8500)
        sqV = SemSeq(stack, nc, "sqV", 1, 6500)
        sqA = SemSeq(stack, nc, "sqA", 1, 3500)
        sqG = SemSeq(stack, nc, "sqG", 16, 12)
        sqC = SemSeq(stack, nc, "sqC", 1, 8)
        block = e(nc.Block())

        tick = {"T": 0, "V": 0, "A": 0, "G": 0, "D": 0, "O": 0, "C": 0}
        seqs = {"T": sqT, "V": sqV, "A": sqA, "G": sqG, "D": sqD, "O": sqO, "C": sqC}
        prog = {"S": [], "T": [], "V": [], "A": [], "G": []}

        def emit(engn, fn):
            prog[engn].append(fn)

        def sig(stream, fn):
            i = tick[stream]
            tick[stream] += 1
            engn = {"D": "S", "O": "G", "C": "G"}.get(stream, stream)
            sq = seqs[stream]

            def g(eng, fn=fn, i=i, sq=sq):
                sq.attach(fn(eng), i)
            emit(engn, g)
            return i + 1

        def wt(engn, stream, count):
            if count <= 0:
                return
            i = count - 1
            sq = seqs[stream]
            if stream in ("D", "O", "G"):
                def g(eng, sq=sq, i=i):
                    sq.wait(eng, i)
            else:
                def g(eng, sq=sq, i=i):
                    eng.wait_ge(sq.sems[i % sq.k], sq.inc * (i // sq.k + 1))
            emit(engn, g)

        def run(engn, fn):
            emit(engn, fn)

        pools = {"S": pbS, "V": pbV, "P": pbP, "Q": pbQ, "X": pbX}
        rot = {k: 0 for k in pools}
        blast = {}

        def grp(pool, mmfn, estream, efn, waits=()):
            bank = pools[pool][rot[pool] % len(pools[pool])]
            rot[pool] += 1
            bl = blast.get(bank.name)
            if bl:
                wt("T", bl[0], bl[1])
            for (stm, m) in waits:
                wt("T", stm, m)
            mT = sig("T", lambda t, mmfn=mmfn, bank=bank: mmfn(t, bank))
            wt(estream, "T", mT)
            mE = sig(estream, lambda v, efn=efn, bank=bank: efn(v, bank))
            blast[bank.name] = (estream, mE)
            return mT, mE

        def grp2(pool, mmfns, estream, efns, waits=()):
            mT = mE = 0
            for i, (f, ef) in enumerate(zip(mmfns, efns)):
                mT, mE = grp(pool, f, estream, ef, waits if i == 0 else ())
            return mT, mE

        mD = 0
        for dst, src in ((mixc, mixcat), (lnt, lnwb), (smt, smallc), (gnt, gnwb)):
            mD = sig("D", lambda s, dst=dst, src=src: s.dma_start(out=dst[:], in_=src))
        mD_small = mD
        mG = sig("G", lambda g: g.dma_start(
            out=wtmpb[:].rearrange("p (c j) -> p c j", c=8),
            in_=wcat.rearrange("(c p) j -> p c j", p=128)))
        mG = sig("G", lambda g: g.dma_start(out=w2t[:], in_=w2sI))
        mG = sig("G", lambda g: g.dma_start(out=a2t[64:128, :], in_=a2sI))
        mG = sig("G", lambda g: g.dma_start(out=g2t[:], in_=g2sI))
        mG_w = mG

        run("A", lambda a: a.preload_act_table(AF.Sigmoid))
        run("A", lambda a: a.drain())

        if PB_STAGE < 0.4:
            wt("G", "D", mD_small)
            sig("C", lambda g: g.drain())
            _finalize_pb(block, prog)
            return nc

        def const_build(v):
            v.memset(onesF[:], 1.0)
            v.memset(onesB[:], 1.0)
            v.memset(blkones[:], 0.0)
            v.memset(ind64[:], 0.0)
            v.memset(s0f[:], 0.0)
            v.memset(s0b[:], 0.0)
            v.drain()
            v.memset(blkones[0:64, 0:64], 1.0)
            v.memset(blkones[64:128, 64:128], 1.0)
            v.memset(ind64[0:64, 0:1], 1.0)
            v.memset(ind64[64:128, 1:2], 1.0)
            return v.drain()
        mV_const = sig("V", const_build)

        if PB_STAGE < 0.6:
            wt("G", "V", mV_const)
            sig("C", lambda g: g.drain())
            _finalize_pb(block, prog)
            return nc

        wt("G", "V", mV_const)

        def mask_build(g):
            g.affine_select(mlt[:], onesF[:], [[1, 128]], ALU.is_ge, 0.0,
                            base=-1, channel_multiplier=-1)
            g.affine_select(mgt[:], onesF[:], [[-1, 128]], ALU.is_ge, 0.0,
                            base=-1, channel_multiplier=1)
            g.affine_select(mle[:], onesF[:], [[1, 128]], ALU.is_ge, 0.0,
                            base=0, channel_multiplier=-1)
            g.affine_select(ident[:], onesB[:], [[-1, 128]], ALU.is_ge, 0.0,
                            base=0, channel_multiplier=1)
            g.drain()
            g.affine_select(ident[:], ident[:], [[1, 128]], ALU.is_ge, 0.0,
                            base=0, channel_multiplier=-1)
            return g.drain()
        mC_mask = sig("C", mask_build)

        if PB_STAGE < 0.8:
            wt("G", "C", mC_mask)
            sig("C", lambda g: g.drain())
            _finalize_pb(block, prog)
            return nc

        wt("V", "D", mD_small)
        wt("V", "G", mG_w)

        def wprep(v):
            for m in range(6):
                for c in range(8):
                    col = (m * 8 + c) * 2
                    v.tensor_tensor(sab[:, col + 1:col + 2], lnt[:, c:c + 1],
                                    mixc[:, m * 8 + c:m * 8 + c + 1], ALU.mult)
            v.drain()
            for m in range(6):
                for c in range(8):
                    col = (m * 8 + c) * 2
                    v.tensor_tensor(sab[:, col:col + 1], lnt[:, c:c + 1],
                                    sab[:, col + 1:col + 2], ALU.subtract)
            v.tensor_scalar(onemka[:], smt[:, 6:8], -1.0, 1.0, ALU.mult, ALU.add)
            for c in range(8):
                v.tensor_scalar(lnbbB[:, 128 * c:128 * (c + 1)], onesF[:],
                                lnt[:, 8 + c:9 + c], None, ALU.mult)
            return v.drain()
        mV_wprep = sig("V", wprep)

        if PB_STAGE < 0.9:
            wt("G", "V", mV_wprep)
            sig("C", lambda g: g.drain())
            _finalize_pb(block, prog)
            return nc

        def bias_mm(oc):
            def f(t, bank, oc=oc):
                inst = None
                for c in range(8):
                    inst = t.matmul(bank[:, 0:64],
                                    wtmpb[:, c * 1024 + 128 * oc:c * 1024 + 128 * (oc + 1)],
                                    lnbbB[:, 128 * c:128 * c + 64],
                                    start=(c == 0), stop=(c == 7))
                return inst
            return f

        def bias_ev(oc):
            def f(a, bank, oc=oc):
                return a.activation(biasc[:, oc:oc + 1], bank[:, 0:1], AF.Identity)
            return f

        mT_bias = 0
        for oc in range(6):
            w = [("V", mV_wprep)] if oc == 0 else []
            mT_bias, mA_bias = grp("Q", bias_mm(oc), "A", bias_ev(oc), waits=w)

        def vb_mm(t, bank):
            inst = None
            for c in range(8):
                inst = t.matmul(bank[:, 0:256], lnbbB[:, 128 * c:128 * (c + 1)],
                                wtmpb[:, c * 1024 + 768:c * 1024 + 1024],
                                start=(c == 0), stop=(c == 7))
            return inst

        def vb_ev(v, bank):
            v.tensor_copy(vbrow[:], bank[0:1, 0:256])
            return v.drain()
        mT_vb, mV_vb = grp("V", vb_mm, "V", vb_ev)

        wt("G", "V", mV_vb)
        mO_vb = sig("O", lambda g: g.dma_start(out=vbx, in_=vbrow[:]))
        wt("G", "O", mO_vb)
        mG_vbB = sig("G", lambda g: g.dma_start(out=vbrowB[:], in_=vbx.partition_broadcast(128)))
        wt("V", "G", mG_vbB)
        wt("V", "T", mT_vb)

        def wscale(v):
            regions = [(0, 256, 0), (256, 512, 1), (512, 576, 2), (576, 640, 3),
                       (640, 768, 4), (768, 1024, 5)]
            v.tensor_scalar(vbrowS[:], vbrowB[:], 1.0 / 128.0, None, ALU.mult)
            for c in range(8):
                base = c * 1024
                for lo, hi, m in regions:
                    col = (m * 8 + c) * 2
                    v.tensor_scalar(wzs[:, base + lo:base + hi], wtmpb[:, base + lo:base + hi],
                                    sab[:, col + 1:col + 2], None, ALU.mult)
            v.drain()
            for c in range(8):
                base = c * 1024
                for lo, hi, m in regions:
                    col = (m * 8 + c) * 2
                    v.tensor_scalar(wtmpb[:, base + lo:base + hi], wtmpb[:, base + lo:base + hi],
                                    sab[:, col:col + 1], None, ALU.mult)
            return v.drain()
        mV_wz = sig("V", wscale)

        if PB_STAGE < 0.99:
            wt("G", "V", mV_wz)
            sig("C", lambda g: g.drain())
            _finalize_pb(block, prog)
            return nc

        OCD = [0, 128, 256, 384, 512, 640]
        mD_z = [0] * NSTRIP
        zdone = [0] * NSTRIP
        ogall = [0] * NSTRIP
        m_chT_last = 0
        mSb = mV_const
        m_v7 = 0
        for s in range(NSTRIP if PB_STAGE >= 1 else 0):
            t0 = SL * s
            if s > 0:
                wt("S", "T", zdone[s - 1])
            mD_z[s] = sig("D", lambda sy, t0=t0: sy.dma_start(
                out=zst[:].rearrange("p (c j) -> p c j", c=8),
                in_=z2T.rearrange("(c p) t -> p c t", p=128)[:, :, t0:t0 + SL + 1]))

            if s > 0:
                run("A", lambda a: a.preload_act_table(AF.Sigmoid))
                run("A", lambda a: a.drain())

            ev_zwza = ev_zg = 0
            for oc in range(6):
                for nt in range(2):
                    off = OCD[oc]

                    def s1mm(t, bank, off=off, nt=nt):
                        for c in range(8):
                            t.matmul(bank[:, 0:SLH], wz[:, c * 1024 + off:c * 1024 + off + 128],
                                     zst[:, c * (SL + 1) + 1 + SLH * nt:c * (SL + 1) + 1 + SLH * nt + SLH],
                                     start=(c == 0), stop=False)
                        inst = None
                        for c in range(8):
                            inst = t.matmul(bank[:, 0:SLH], wzs[:, c * 1024 + off:c * 1024 + off + 128],
                                            zst[:, c * (SL + 1) + SLH * nt:c * (SL + 1) + SLH * nt + SLH],
                                            start=False, stop=(c == 7))
                        return inst

                    if oc < 2:
                        def s1ev(a, bank, oc=oc, nt=nt):
                            return a.activation(rS[:, oc * SL + SLH * nt:oc * SL + SLH * nt + SLH],
                                                bank[:, 0:SLH], AF.Identity, bias=biasc[:, oc:oc + 1])
                    elif oc < 4:
                        def s1ev(a, bank, oc=oc, nt=nt):
                            return a.activation(kS[:, (oc - 2) * SL + SLH * nt:(oc - 2) * SL + SLH * nt + SLH],
                                                bank[:, 0:SLH], AF.Identity, bias=biasc[:, oc:oc + 1])
                    elif oc == 4:
                        def s1ev(a, bank, nt=nt):
                            a.activation(zwza[0:64, SLH * nt:SLH * nt + SLH], bank[0:64, 0:SLH],
                                         AF.Tanh, bias=biasc[0:64, 4:5])
                            return a.activation(zwza[64:128, SLH * nt:SLH * nt + SLH], bank[64:128, 0:SLH],
                                                AF.Identity, bias=biasc[64:128, 4:5])
                    else:
                        def s1ev(a, bank, nt=nt):
                            return a.activation(zg[:, SLH * nt:SLH * nt + SLH], bank[:, 0:SLH],
                                                AF.Sigmoid, bias=biasc[:, 5:6])
                    w = []
                    if oc == 0 and nt == 0:
                        w = [("D", mD_z[s]), ("V", mV_wz)]
                        if s > 0:
                            w.append(("V", ogall[s - 1]))
                    mT1g, mE1g = grp("S", s1mm, "A", s1ev, waits=w)
                    if oc == 4 and nt == 1:
                        ev_zwza = mE1g
                    if oc == 5 and nt == 1:
                        ev_zg = mE1g
            for tt in range(NTT):
                def vmm(t, bank, tt=tt):
                    for c in range(8):
                        t.matmul(bank[:, 0:256], zst[:, c * (SL + 1) + 1 + 128 * tt:c * (SL + 1) + 1 + 128 * tt + 128],
                                 wz[:, c * 1024 + 768:c * 1024 + 1024], start=(c == 0), stop=False)
                    for c in range(8):
                        t.matmul(bank[:, 0:256], zst[:, c * (SL + 1) + 128 * tt:c * (SL + 1) + 128 * tt + 128],
                                 wzs[:, c * 1024 + 768:c * 1024 + 1024], start=False, stop=False)
                    return t.matmul(bank[:, 0:256], onesB[:], vbrowS[:], start=False, stop=True)

                def vev(a, bank, tt=tt):
                    return a.activation(vS[:, 256 * tt:256 * (tt + 1)], bank[:, 0:256], AF.Identity)
                mTv, _ = grp("V", vmm, "A", vev)
                if tt == NTT - 1:
                    zdone[s] = mTv
            for nt in range(2):
                for mh in range(2):
                    def w2mm(t, bank, nt=nt, mh=mh):
                        return t.matmul(bank[:, 0:SLH], w2t[:, 128 * mh:128 * (mh + 1)],
                                        zwza[0:64, SLH * nt:SLH * (nt + 1)], start=True, stop=True)

                    def w2ev(a, bank, nt=nt, mh=mh):
                        return a.activation(dsig[:, mh * SL + SLH * nt:mh * SL + SLH * nt + SLH],
                                            bank[:, 0:SLH], AF.Sigmoid, bias=smt[:, mh:mh + 1])
                    w = [("A", ev_zwza)] if (nt == 0 and mh == 0) else []
                    grp("S", w2mm, "A", w2ev, waits=w)
            for nt in range(2):
                for mh in range(2):
                    def a2mm(t, bank, nt=nt, mh=mh):
                        return t.matmul(bank[:, 0:SLH], a2t[64:128, 128 * mh:128 * (mh + 1)],
                                        zwza[64:128, SLH * nt:SLH * (nt + 1)], start=True, stop=True)

                    def a2ev(a, bank, nt=nt, mh=mh):
                        return a.activation(asig[:, mh * SL + SLH * nt:mh * SL + SLH * nt + SLH],
                                            bank[:, 0:SLH], AF.Sigmoid, bias=smt[:, 2 + mh:3 + mh])
                    grp("S", a2mm, "A", a2ev)
            for tt in range(NTT):
                def gmm(t, bank, tt=tt):
                    return t.matmul(bank[:, 0:256], zg[:, 128 * tt:128 * (tt + 1)],
                                    g2t[:], start=True, stop=True)

                def gev(a, bank, tt=tt):
                    return a.activation(gS[:, 256 * tt:256 * (tt + 1)], bank[:, 0:256], AF.Identity)
                w = [("A", ev_zg)] if tt == 0 else []
                grp("V", gmm, "A", gev, waits=w)

            if PB_STAGE < 2:
                continue
            run("A", lambda a: a.preload_act_table(AF.Exp))
            run("A", lambda a: a.drain())

            def p2_act(a):
                a.drain()
                return a.activation(SCR1[:], dsig[:], AF.Exp, scale=-0.60653066)
            mA_d = sig("A", p2_act)
            run("A", lambda a: a.preload_act_table(AF.Sqrt))
            run("A", lambda a: a.drain())

            wt("V", "A", mA_d)
            if s > 0:
                wt("V", "T", m_chT_last)

            def p2_v1(v):
                bufs = [SCR1, ca]
                cur = 0
                for k in (1, 2, 4, 8, 16, 32, 64):
                    a4 = bufs[cur][:].rearrange("p (t ch i) -> p t ch i", t=2, i=CHUNK)
                    b4 = bufs[1 - cur][:].rearrange("p (t ch i) -> p t ch i", t=2, i=CHUNK)
                    v.tensor_copy(b4[:, :, :, 0:k], a4[:, :, :, 0:k])
                    v.tensor_tensor(b4[:, :, :, k:CHUNK], a4[:, :, :, k:CHUNK],
                                    a4[:, :, :, 0:CHUNK - k], ALU.mult)
                    v.drain()
                    cur = 1 - cur
                a4 = ca[:].rearrange("p (t ch i) -> p t ch i", t=2, i=CHUNK)
                c4 = cprev[:].rearrange("p (t ch i) -> p t ch i", t=2, i=CHUNK)
                v.tensor_copy(c4[:, :, :, 1:CHUNK], a4[:, :, :, 0:CHUNK - 1])
                v.memset(c4[:, :, :, 0:1], 1.0)
                v.reciprocal(cinv[:], ca[:])
                v.drain()
                for ti in range(2):
                    v.tensor_scalar(SCR2[:, ti * SL:(ti + 1) * SL], kS[:, ti * SL:(ti + 1) * SL],
                                    smt[:, 4 + ti:5 + ti], None, ALU.mult)
                v.drain()
                v.tensor_tensor(SCB1[:], SCR2[:], SCR2[:], ALU.mult)
                return v.drain()
            mV_v1 = sig("V", p2_v1)

            for ti in range(2):
                for nt in range(2):
                    def ssmm(t, bank, ti=ti, nt=nt):
                        return t.matmul(bank[:, 0:SLH], blkones[:],
                                        SCB1[:, ti * SL + SLH * nt:ti * SL + SLH * nt + SLH],
                                        start=True, stop=True)

                    def ssev(a, bank, ti=ti, nt=nt):
                        return a.activation(SCR1[:, ti * SL + SLH * nt:ti * SL + SLH * nt + SLH],
                                            bank[:, 0:SLH], AF.Sqrt)
                    w = [("V", mV_v1)] if (ti == 0 and nt == 0) else []
                    _, mA_ss = grp("S", ssmm, "A", ssev, waits=w)

            wt("V", "A", mA_ss)

            def p2_v2(v):
                v.tensor_scalar(SCR1[:], SCR1[:], 1e-12, None, ALU.max)
                v.drain()
                v.reciprocal(SCR1[:], SCR1[:])
                v.drain()
                v.tensor_tensor(SCR2[:], SCR2[:], SCR1[:], ALU.mult)
                v.drain()
                v.scalar_tensor_tensor(atil[:], SCR2[:], -1.0, cprev[:], ALU.mult, ALU.mult)
                v.tensor_tensor(SCR1[:], SCR2[:], asig[:], ALU.mult)
                v.drain()
                v.tensor_tensor(btil[:], SCR1[:], cinv[:], ALU.mult)
                for ti in range(2):
                    v.tensor_scalar(SCR1[:, ti * SL:(ti + 1) * SL], asig[:, ti * SL:(ti + 1) * SL],
                                    smt[:, 6 + ti:7 + ti], onemka[:, ti:ti + 1], ALU.mult, ALU.add)
                v.drain()
                v.tensor_tensor(SCR1[:], kS[:], SCR1[:], ALU.mult)
                v.drain()
                v.tensor_tensor(ktil[:], SCR1[:], cinv[:], ALU.mult)
                v.tensor_tensor(SCR2[:], rS[:], SCR1[:], ALU.mult)
                v.drain()
                for ti in range(2):
                    v.tensor_scalar(rkpb[:, ti * SL:(ti + 1) * SL], SCR2[:, ti * SL:(ti + 1) * SL],
                                    smt[:, 8 + ti:9 + ti], None, ALU.mult)
                v.tensor_tensor(rtil[:], rS[:], ca[:], ALU.mult)
                return v.drain()
            mV_prep = sig("V", p2_v2)

            t1_done = {}
            for ch in range(CPS if PB_STAGE >= 3 else 0):
                def fsl(buf, p, ch=ch):
                    r0 = 64 * (p % 2)
                    cc = (p // 2) * SL + ch * CHUNK
                    return buf[r0:r0 + 64, cc:cc + CHUNK]

                def t1_bundles(ch, first_waits):
                    specs = [(btil, atil, mlt, mtS), (atil, btil, mgt, mS), (ktil, atil, mlt, gtS)]
                    out = []
                    for ki, (sb_, mv_, msk, dst) in enumerate(specs):
                        for p2_ in range(2):
                            mms, evs = [], []
                            for p in (2 * p2_, 2 * p2_ + 1):
                                def t1mm(t, bank, sb_=sb_, mv_=mv_, p=p, ch=ch, fsl=fsl):
                                    return t.matmul(bank[:, 0:128], fsl(sb_, p, ch), fsl(mv_, p, ch),
                                                    start=True, stop=True)

                                def t1ev(v, bank, msk=msk, dst=dst, p=p):
                                    return v.tensor_tensor(dst[:, 128 * p:128 * (p + 1)], bank[:], msk[:], ALU.mult)
                                mms.append(t1mm)
                                evs.append(t1ev)
                            w = first_waits if (ki == 0 and p2_ == 0) else []
                            out.append((mms, evs, w))
                    return out

                if ch in t1_done:
                    mE_gtS = t1_done.pop(ch)
                else:
                    w0 = [("V", mV_prep)]
                    if s == 0 and ch == 0:
                        w0.append(("C", mC_mask))
                    mE_gtS = 0
                    for mms, evs, w in t1_bundles(ch, w0):
                        _, mE_gtS = grp2("P", mms, "V", evs, waits=w)

                if PB_STAGE < 4:
                    continue
                mA_t2 = 0
                for p2_ in range(2):
                    mms, evs = [], []
                    for p in (2 * p2_, 2 * p2_ + 1):
                        def t2mm(t, bank, p=p, ch=ch, fsl=fsl):
                            r0 = 64 * (p % 2)
                            c0 = 64 * (p // 2)
                            t.matmul(bank[:, 0:64], fsl(atil, p, ch), s0b[r0:r0 + 64, c0:c0 + 64],
                                     start=True, stop=False)
                            return t.matmul(bank[:, 0:64], gtS[:, 128 * p:128 * (p + 1)],
                                            vS[:, ch * 256 + 64 * p:ch * 256 + 64 * (p + 1)],
                                            start=False, stop=True)

                        def t2ev(a, bank, p=p):
                            return a.activation(uB[:, 64 * p:64 * (p + 1)], bank[:, 0:64], AF.Identity)
                        mms.append(t2mm)
                        evs.append(t2ev)
                    w = [("V", mSb), ("V", mE_gtS)] if p2_ == 0 else []
                    _, mA_t2 = grp2("Q", mms, "A", evs, waits=w)

                fill_state = {"d2": 0, "tr": 0}
                fillers = []
                for ki, (sb_, dst) in enumerate([(btil, d1S), (ktil, d2S)]):
                    for p2_ in range(2):
                        mms, evs = [], []
                        for p in (2 * p2_, 2 * p2_ + 1):
                            def t3mm(t, bank, sb_=sb_, p=p, ch=ch, fsl=fsl):
                                return t.matmul(bank[:, 0:128], fsl(sb_, p, ch), fsl(rtil, p, ch),
                                                start=True, stop=True)

                            def t3ev(v, bank, dst=dst, p=p):
                                return v.tensor_tensor(dst[:, 128 * p:128 * (p + 1)], bank[:], mle[:], ALU.mult)
                            mms.append(t3mm)
                            evs.append(t3ev)

                        def emit_t3(mms=mms, evs=evs):
                            _, m = grp2("P", mms, "V", evs)
                            fill_state["d2"] = m
                        fillers.append(emit_t3)
                for ti in range(2):
                    for bi, src in enumerate((btil, ktil)):
                        cc = ti * SL + ch * CHUNK

                        def trmm(t, bank, src=src, cc=cc):
                            return t.transpose(bank[:, 0:128], src[:, cc:cc + CHUNK], ident[:])

                        def trev(a, bank, ti=ti, bi=bi):
                            return a.activation(bkT[:, 256 * ti + 128 * bi:256 * ti + 128 * bi + 128],
                                                bank[:], AF.Identity)

                        def emit_tr(trmm=trmm, trev=trev):
                            _, m = grp("X", trmm, "A", trev)
                            fill_state["tr"] = m
                        fillers.append(emit_tr)
                if ch + 1 < CPS and PB_STAGE >= 4:
                    for mms, evs, w in t1_bundles(ch + 1, []):
                        def emit_n1(mms=mms, evs=evs, w=w, nch=ch + 1):
                            _, m = grp2("P", mms, "V", evs, waits=w)
                            t1_done[nch] = m
                        fillers.append(emit_n1)

                cur_t, cur_u = mtS, mS
                mE_pwT_last = 0
                mE_pwU_last = 0
                mE_apply_last = 0
                for lvl in range(7):
                    for p2_ in range(2):
                        mms, evs = [], []
                        for p in (2 * p2_, 2 * p2_ + 1):
                            def apmm(t, bank, cur_t=cur_t, p=p):
                                return t.matmul(bank[:, 0:64], cur_t[:, 128 * p:128 * (p + 1)],
                                                uB[:, 64 * p:64 * (p + 1)], start=True, stop=True)

                            def apev(v, bank, p=p):
                                return v.tensor_tensor(uB[:, 64 * p:64 * (p + 1)], uB[:, 64 * p:64 * (p + 1)],
                                                       bank[:, 0:64], ALU.add)
                            mms.append(apmm)
                            evs.append(apev)
                        w = []
                        if p2_ == 0:
                            if lvl == 0:
                                w = [("A", mA_t2)]
                            else:
                                w = [("A", mE_pwT_last), ("V", mE_pwU_last)]
                        _, mE_apply_last = grp2("Q", mms, "V", evs, waits=w)
                    if lvl < 6:
                        nx_u, nx_t = pwU[lvl % 2], pwT[lvl % 2]
                        for p2_ in range(2):
                            umms, uevs, tmms, tevs = [], [], [], []
                            for p in (2 * p2_, 2 * p2_ + 1):
                                def pumm(t, bank, cur_t=cur_t, cur_u=cur_u, p=p):
                                    return t.matmul(bank[:, 0:128], cur_t[:, 128 * p:128 * (p + 1)],
                                                    cur_u[:, 128 * p:128 * (p + 1)], start=True, stop=True)

                                def puev(v, bank, nx_u=nx_u, p=p):
                                    return v.tensor_copy(nx_u[:, 128 * p:128 * (p + 1)], bank[:])

                                def ptmm(t, bank, cur_t=cur_t, cur_u=cur_u, p=p):
                                    return t.matmul(bank[:, 0:128], cur_u[:, 128 * p:128 * (p + 1)],
                                                    cur_t[:, 128 * p:128 * (p + 1)], start=True, stop=True)

                                def ptev(a, bank, nx_t=nx_t, p=p):
                                    return a.activation(nx_t[:, 128 * p:128 * (p + 1)], bank[:], AF.Identity)
                                umms.append(pumm)
                                uevs.append(puev)
                                tmms.append(ptmm)
                                tevs.append(ptev)
                            w = []
                            if p2_ == 0 and lvl > 0:
                                w = [("A", mE_pwT_last), ("V", mE_pwU_last)]
                            _, mE_pwU_last = grp2("P", umms, "V", uevs, waits=w)
                            _, mE_pwT_last = grp2("P", tmms, "A", tevs)
                        cur_t, cur_u = nx_t, nx_u
                    for _f in range(2 if lvl < 6 else 0):
                        if fillers:
                            fillers.pop(0)()

                if PB_STAGE < 5:
                    continue

                for f in fillers:
                    f()
                fillers = []
                mE_d2 = fill_state["d2"]
                mA_tr = fill_state["tr"]

                mA_t4 = 0
                for p2_ in range(2):
                    mms, evs = [], []
                    for p in (2 * p2_, 2 * p2_ + 1):
                        def t4mm(t, bank, p=p, ch=ch, fsl=fsl):
                            r0 = 64 * (p % 2)
                            c0 = 64 * (p // 2)
                            t.matmul(bank[:, 0:64], fsl(rtil, p, ch), s0b[r0:r0 + 64, c0:c0 + 64],
                                     start=True, stop=False)
                            t.matmul(bank[:, 0:64], d1S[:, 128 * p:128 * (p + 1)],
                                     uB[:, 64 * p:64 * (p + 1)], start=False, stop=False)
                            return t.matmul(bank[:, 0:64], d2S[:, 128 * p:128 * (p + 1)],
                                            vS[:, ch * 256 + 64 * p:ch * 256 + 64 * (p + 1)],
                                            start=False, stop=True)

                        def t4ev(a, bank, p=p):
                            return a.activation(ogchunk[:, 64 * p:64 * (p + 1)], bank[:, 0:64], AF.Identity)
                        mms.append(t4mm)
                        evs.append(t4ev)
                    w = []
                    if p2_ == 0:
                        w = [("V", mE_d2), ("V", mE_apply_last)]
                        if m_v7:
                            w.append(("V", m_v7))
                    _, mA_t4 = grp2("Q", mms, "A", evs, waits=w)


                if PB_STAGE < 6:
                    continue
                for g2 in range(2):
                    def t5mm(t, bank, g2=g2, ch=ch):
                        t.matmul(bank[:, 0:128], bkT[:, 256 * g2:256 * g2 + 128],
                                 uB[:, 128 * g2:128 * (g2 + 1)], start=True, stop=False)
                        return t.matmul(bank[:, 0:128], bkT[:, 256 * g2 + 128:256 * g2 + 256],
                                        vS[:, ch * 256 + 128 * g2:ch * 256 + 128 * (g2 + 1)],
                                        start=False, stop=True)

                    def t5ev(v, bank, g2=g2, ch=ch):
                        cc = g2 * SL + ch * CHUNK + CHUNK - 1
                        v.tensor_tensor(sTmp[0:64, 64 * g2:64 * g2 + 64],
                                        s0f[0:64, 64 * g2:64 * g2 + 64], bank[0:64, 0:64], ALU.add)
                        v.tensor_tensor(sTmp[64:128, 64 * g2:64 * g2 + 64],
                                        s0f[64:128, 64 * g2:64 * g2 + 64], bank[64:128, 64:128], ALU.add)
                        v.drain()
                        v.tensor_scalar(s0f[0:64, 64 * g2:64 * g2 + 64],
                                        sTmp[0:64, 64 * g2:64 * g2 + 64],
                                        ca[0:64, cc:cc + 1], None, ALU.mult)
                        v.tensor_scalar(s0f[64:128, 64 * g2:64 * g2 + 64],
                                        sTmp[64:128, 64 * g2:64 * g2 + 64],
                                        ca[64:128, cc:cc + 1], None, ALU.mult)
                        v.drain()
                        v.tensor_copy(s0b[:, 64 * g2:64 * g2 + 64], s0f[:, 64 * g2:64 * g2 + 64])
                        return v.drain()
                    w = [("A", mA_tr)] if g2 == 0 else []
                    _, mSb = grp("P", t5mm, "V", t5ev, waits=w)

                if PB_STAGE < 7:
                    continue
                mms, evs = [], []
                for ti in range(2):
                    def t6mm(t, bank, ti=ti, ch=ch):
                        return t.matmul(bank[:, 0:64],
                                        rkpb[:, ti * SL + ch * CHUNK:ti * SL + (ch + 1) * CHUNK],
                                        ind64[:], start=True, stop=True)

                    def t6ev(v, bank, ti=ti):
                        return v.tensor_copy(rksum[:, 2 * ti:2 * ti + 2], bank[:, 0:2])
                    mms.append(t6mm)
                    evs.append(t6ev)
                _, mE_rk = grp2("Q", mms, "V", evs)

                wt("V", "A", mA_t4)

                def v6(v):
                    o4m = ogchunk[:].rearrange("p (h i) -> p h i", h=4)
                    v.tensor_reduce(gnm[:], o4m, AX, ALU.add)
                    v.tensor_tensor(oq[:], ogchunk[:], ogchunk[:], ALU.mult)
                    v.drain()
                    o4 = oq[:].rearrange("p (h i) -> p h i", h=4)
                    v.tensor_reduce(gns[:], o4, AX, ALU.add)
                    v.tensor_scalar(gnm[:], gnm[:], 1.0 / 64.0, None, ALU.mult)
                    v.drain()
                    v.tensor_scalar(gnr[:], gns[:], 1.0 / 64.0, None, ALU.mult)
                    v.tensor_tensor(gns[:], gnm[:], gnm[:], ALU.mult)
                    v.drain()
                    v.tensor_tensor(gnr[:], gnr[:], gns[:], ALU.subtract)
                    v.drain()
                    v.tensor_scalar(gnr[:], gnr[:], float(GNEPS), None, ALU.add)
                    v.drain()
                    v.reciprocal(gnr[:], gnr[:])
                    return v.drain()
                mV6 = sig("V", v6)
                wt("A", "V", mV6)

                def a6(a):
                    return a.activation(gnr[:], gnr[:], AF.Sqrt)
                mA6 = sig("A", a6)
                wt("V", "A", mA6)
                if ch == 0 and s > 0:
                    wt("V", "O", s + 1)

                def v7(v, ch=ch):
                    for h in range(4):
                        v.tensor_scalar(ogchunk[:, 64 * h:64 * (h + 1)], ogchunk[:, 64 * h:64 * (h + 1)],
                                        gnm[:, h:h + 1], gnr[:, h:h + 1], ALU.subtract, ALU.mult)
                    v.drain()
                    v.tensor_tensor(ogchunk[:], ogchunk[:], gnt[:, 0:256], ALU.mult)
                    v.drain()
                    v.tensor_tensor(ogchunk[:], ogchunk[:], gnt[:, 256:512], ALU.add)
                    v.drain()
                    for h in range(4):
                        v.scalar_tensor_tensor(ogchunk[:, 64 * h:64 * (h + 1)],
                                               vS[:, ch * 256 + 64 * h:ch * 256 + 64 * (h + 1)],
                                               rksum[:, h:h + 1], ogchunk[:, 64 * h:64 * (h + 1)],
                                               ALU.mult, ALU.add)
                    v.drain()
                    v.tensor_tensor(ogS[:, ch * 256:(ch + 1) * 256], ogchunk[:],
                                    gS[:, ch * 256:(ch + 1) * 256], ALU.mult)
                    return v.drain()
                m_v7 = sig("V", v7)
                m_chT_last = tick["T"]
            ogall[s] = m_v7 if m_v7 else tick["V"]

            if PB_STAGE < 7:
                continue
            wt("G", "V", m_v7)
            sig("O", lambda g, t0=t0: g.dma_start(
                out=og[t0:t0 + SL, :].rearrange("(tt p) j -> p tt j", p=128),
                in_=ogS[:, 0:NTT * 256].rearrange("p (tt j) -> p tt j", tt=NTT)))

        if PB_STAGE >= 7:
            wt("G", "O", tick["O"])
        wt("G", "V", tick["V"])
        wt("G", "T", tick["T"])
        wt("G", "A", tick["A"])
        sig("C", lambda g: g.drain())
        _finalize_pb(block, prog)
    return nc


def host_pB_inputs(inputs, z2):
    pre = lambda k: np.asarray(inputs["L0_attn_" + k], np.float32)
    z2 = np.asarray(z2).reshape(B, T, C)
    maps = []
    for c in range(NC):
        b, g = c // 4, c % 4
        cs = slice(256 * g, 256 * (g + 1))
        z2T = np.zeros((C, T + 1), bf)
        z2T[:, 1:] = np.ascontiguousarray(z2[b].T)
        wcat = np.concatenate([pre("W_r")[:, cs], pre("W_k")[:, cs], pre("w1"),
                               pre("a1"), pre("g1"), pre("W_v")[:, cs]], axis=1)
        mixl = [pre("x_r"), pre("x_k"), pre("x_w"), pre("x_a"), pre("x_g"), pre("x_v")]
        mixcat = np.concatenate([m.reshape(8, 128).T for m in mixl], axis=1)
        lnwb = np.concatenate([pre("ln_attn_w").reshape(8, 128).T,
                               pre("ln_attn_b").reshape(8, 128).T], axis=1)
        smallc = np.concatenate([pre("w0")[cs].reshape(2, 128).T,
                                 pre("a0")[cs].reshape(2, 128).T,
                                 pre("k_k")[cs].reshape(2, 128).T,
                                 pre("k_a")[cs].reshape(2, 128).T,
                                 pre("r_k").reshape(C)[cs].reshape(2, 128).T], axis=1)
        gnwb = np.concatenate([np.broadcast_to(pre("gn_w")[cs][None], (128, 256)),
                               np.broadcast_to(pre("gn_b")[cs][None], (128, 256))], axis=1)
        maps.append({
            "z2T": z2T,
            "wcat": np.ascontiguousarray(wcat),
            "mixcat": np.ascontiguousarray(mixcat),
            "lnwb": np.ascontiguousarray(lnwb),
            "w2sI": np.ascontiguousarray(pre("w2")[:, cs]),
            "a2sI": np.ascontiguousarray(pre("a2")[:, cs]),
            "g2sI": np.ascontiguousarray(pre("g2")[:, cs]),
            "smallc": np.ascontiguousarray(smallc),
            "gnwb": np.ascontiguousarray(gnwb),
        })
    return maps


def run_pB(inputs, z2):
    nc = build_pB()
    maps = host_pB_inputs(inputs, z2)
    res = _run(nc, maps)
    og = np.zeros((B, T, C), bf)
    for c in range(NC):
        og[c // 4, :, 256 * (c % 4):256 * (c % 4 + 1)] = np.asarray(res.results[c]["og"])
    return og.reshape(B * T, C)




class Sched:
    def __init__(self, stack, nc, streams, pools):
        self.nc = nc
        self.seqs = {k: SemSeq(stack, nc, "s" + k, inc, total)
                     for k, (_, inc, total) in streams.items()}
        self.eng = {k: v[0] for k, v in streams.items()}
        self.tick = {k: 0 for k in streams}
        self.prog = {"S": [], "T": [], "V": [], "A": [], "G": []}
        self.pools = pools
        self.rot = {k: 0 for k in pools}
        self.blast = {}

    def emit(self, engn, fn):
        self.prog[engn].append(fn)

    def run(self, engn, fn):
        self.prog[engn].append(fn)

    def sig(self, stream, fn):
        i = self.tick[stream]
        self.tick[stream] += 1
        sq = self.seqs[stream]

        def g(eng, fn=fn, i=i, sq=sq):
            sq.attach(fn(eng), i)
        self.emit(self.eng[stream], g)
        return i + 1

    def wt(self, engn, stream, count):
        if count <= 0:
            return
        i = count - 1
        sq = self.seqs[stream]
        if sq.inc == 16:
            def g(eng, sq=sq, i=i):
                sq.wait(eng, i)
        else:
            def g(eng, sq=sq, i=i):
                eng.wait_ge(sq.sems[i % sq.k], sq.inc * (i // sq.k + 1))
        self.emit(engn, g)

    def grp(self, pool, mmfn, estream, efn, waits=()):
        bank = self.pools[pool][self.rot[pool] % len(self.pools[pool])]
        self.rot[pool] += 1
        bl = self.blast.get(bank.name)
        if bl:
            self.wt("T", bl[0], bl[1])
        for (stm, m) in waits:
            self.wt("T", stm, m)
        mT = self.sig("T", lambda t, mmfn=mmfn, bank=bank: mmfn(t, bank))
        self.wt(self.eng[estream], "T", mT)
        if self.eng[estream] == "V":
            mE = self.sig(estream, lambda v, efn=efn, bank=bank: (efn(v, bank), v.drain())[1])
        else:
            mE = self.sig(estream, lambda v, efn=efn, bank=bank: efn(v, bank))
        self.blast[bank.name] = (estream, mE)
        return mT, mE

    def finalize(self, block):
        prog = self.prog

        @block.sync
        def _(sy):
            for f in prog["S"]:
                f(sy)

        @block.tensor
        def _(t):
            for f in prog["T"]:
                f(t)

        @block.vector
        def _(v):
            for f in prog["V"]:
                f(v)

        @block.scalar
        def _(a):
            for f in prog["A"]:
                f(a)

        @block.gpsimd
        def _(g):
            for f in prog["G"]:
                f(g)



CT = TPC + 128
CNT = CT // 128


def build_pC():
    nc = _bass()
    ogT = nc.dram_tensor("ogT", [C, CT], BF16, kind="ExternalInput").ap()
    xpt = nc.dram_tensor("xpt", [CT, C], F32, kind="ExternalInput").ap()
    x0t = nc.dram_tensor("x0t", [TPC, C], F32, kind="ExternalInput").ap()
    WoI = nc.dram_tensor("WoI", [C, C], F32, kind="ExternalInput").ap()
    Wf1I = nc.dram_tensor("Wf1I", [C, 4096], F32, kind="ExternalInput").ap()
    Wf2I = nc.dram_tensor("Wf2I", [4096, C], F32, kind="ExternalInput").ap()
    lnfc = nc.dram_tensor("lnfc", [128, 24], F32, kind="ExternalInput").ap()
    bmaskI = nc.dram_tensor("bmaskI", [128, 1], F32, kind="ExternalInput").ap()
    identI = nc.dram_tensor("identI", [128, 128], BF16, kind="ExternalInput").ap()
    xres = nc.dram_tensor("xres", [TPC, C], F32, kind="ExternalOutput").ap()
    z3o = nc.dram_tensor("z3o", [TPC, C], BF16, kind="ExternalOutput").ap()
    zsumo = nc.dram_tensor("zsumo", [1, C], F32, kind="ExternalOutput").ap()

    from contextlib import ExitStack
    with ExitStack() as stack:
        e = stack.enter_context
        wbig = e(nc.sbuf_tensor("wbig", [128, 8192], BF16))
        wf1b = [e(nc.sbuf_tensor(f"wf1b{j}", [128, 4096], BF16)) for j in range(2)]
        w1bg = e(nc.sbuf_tensor("w1bg", [128, 4096], BF16))
        zfT = e(nc.sbuf_tensor("zfT", [128, 8 * CT], BF16))
        x1acc = e(nc.sbuf_tensor("x1acc", [128, 16 * 1024], F32))
        hR = e(nc.sbuf_tensor("hR", [128, 4 * 2048], BF16))
        ogst = e(nc.sbuf_tensor("ogst", [128, 2 * 1024], BF16))
        xt2 = e(nc.sbuf_tensor("xt2", [128, 2 * 1024], F32))
        xpt2 = e(nc.sbuf_tensor("xpt2", [128, 2 * 1024], F32))
        x0t2 = e(nc.sbuf_tensor("x0t2", [128, 2 * 1024], F32))
        zft = e(nc.sbuf_tensor("zft", [128, 2 * 1024], BF16))
        z3t = e(nc.sbuf_tensor("z3t", [128, 2 * 1024], BF16))
        lnc = e(nc.sbuf_tensor("lnc", [128, 24], F32))
        bmk = e(nc.sbuf_tensor("bmk", [128, 1], F32))
        sabF = e(nc.sbuf_tensor("sabF", [128, 16], F32))
        lnfbB = e(nc.sbuf_tensor("lnfbB", [128, 8 * 64], BF16))
        onesFc = e(nc.sbuf_tensor("onesFc", [128, 64], F32))
        ones128b = e(nc.sbuf_tensor("ones128b", [128, 128], BF16))
        identT = e(nc.sbuf_tensor("identT", [128, 128], BF16))
        biasF = e(nc.sbuf_tensor("biasF", [128, 4], F32))
        zsumacc = e(nc.sbuf_tensor("zsumacc", [1, 1024], F32))
        scr = e(nc.sbuf_tensor("scr", [128, 1024], F32))
        sm = e(nc.sbuf_tensor("sm", [128, 1], F32))
        mean = e(nc.sbuf_tensor("mean", [128, 1], F32))
        var = e(nc.sbuf_tensor("var", [128, 1], F32))
        ur = e(nc.sbuf_tensor("ur", [128, 1], F32))
        pM = [e(nc.psum_tensor(f"pM{j}", [128, 512], F32)) for j in range(6)]
        pQ = [e(nc.psum_tensor("pQ0", [128, 64], F32))]
        pX = [e(nc.psum_tensor("pX0", [128, 128], BF16))]
        block_streams = {
            "T": ("T", 1, 700), "V": ("V", 1, 700), "A": ("A", 1, 450),
            "D": ("S", 16, 60), "G": ("G", 16, 24), "O": ("G", 16, 40),
        }
        sc = Sched(stack, nc, block_streams, {"S": pM, "Q": pQ, "X": pX})
        sig, wt, run, grp = sc.sig, sc.wt, sc.run, sc.grp
        block = e(nc.Block())

        run("A", lambda a: a.preload_act_table(AF.Sqrt))
        run("A", lambda a: a.drain())
        mG_wo = sig("G", lambda g: g.dma_start(
            out=wbig[:].rearrange("p (c j) -> p c j", c=8),
            in_=WoI.rearrange("(c p) j -> p c j", p=128)))
        mD = sig("D", lambda s: s.dma_start(out=lnc[:], in_=lnfc))
        mD = sig("D", lambda s: s.dma_start(out=bmk[:], in_=bmaskI))
        mD = sig("D", lambda s: s.dma_start(out=identT[:], in_=identI))
        mD_small = mD

        def const_build(v):
            v.memset(onesFc[:], 1.0)
            v.memset(ones128b[:], 1.0)
            v.memset(zsumacc[:], 0.0)
            v.drain()
            return v.drain()
        mV_const = sig("V", const_build)
        wt("V", "D", mD_small)

        def prep(v):
            for c in range(8):
                v.tensor_tensor(sabF[:, 2 * c + 1:2 * c + 2], lnc[:, c:c + 1],
                                lnc[:, 16 + c:17 + c], ALU.mult)
            v.drain()
            for c in range(8):
                v.tensor_tensor(sabF[:, 2 * c:2 * c + 1], lnc[:, c:c + 1],
                                sabF[:, 2 * c + 1:2 * c + 2], ALU.subtract)
                v.tensor_scalar(lnfbB[:, 64 * c:64 * (c + 1)], onesFc[:],
                                lnc[:, 8 + c:9 + c], None, ALU.mult)
            return v.drain()
        mV_prep = sig("V", prep)

        mT_tr_last = [0] * 2
        mV_x_last = [0] * 2
        for ti in range(CNT):
            sl = ti % 2
            if mT_tr_last[sl]:
                wt("S", "T", mT_tr_last[sl])
            if mV_x_last[sl]:
                wt("S", "V", mV_x_last[sl])
            mD_og = sig("D", lambda s, ti=ti, sl=sl: s.dma_start(
                out=ogst[:, 1024 * sl:1024 * (sl + 1)].rearrange("p (c j) -> p c j", c=8),
                in_=ogT.rearrange("(c p) t -> p c t", p=128)[:, :, 128 * ti:128 * (ti + 1)]))
            mD_xp = sig("D", lambda s, ti=ti, sl=sl: s.dma_start(
                out=xpt2[:, 1024 * sl:1024 * (sl + 1)],
                in_=xpt[128 * ti:128 * (ti + 1), :]))
            if ti >= 1:
                mD_x0 = sig("D", lambda s, ti=ti, sl=sl: s.dma_start(
                    out=x0t2[:, 1024 * sl:1024 * (sl + 1)],
                    in_=x0t[128 * (ti - 1):128 * ti, :]))
            mE_half = 0
            for h in range(2):
                def womm(t, bank, sl=sl, h=h):
                    inst = None
                    for c in range(8):
                        inst = t.matmul(bank[:, 0:512], ogst[:, 1024 * sl + 128 * c:1024 * sl + 128 * (c + 1)],
                                        wbig[:, 1024 * c + 512 * h:1024 * c + 512 * (h + 1)],
                                        start=(c == 0), stop=(c == 7))
                    return inst

                def woev(v, bank, sl=sl, h=h):
                    return v.tensor_tensor(xt2[:, 1024 * sl + 512 * h:1024 * sl + 512 * (h + 1)],
                                           bank[:, 0:512],
                                           xpt2[:, 1024 * sl + 512 * h:1024 * sl + 512 * (h + 1)], ALU.add)
                w = [("D", mD_og), ("D", mD_xp)] if h == 0 else []
                if ti == 0 and h == 0:
                    w.append(("G", mG_wo))
                _, mE_half = grp("S", womm, "V", woev, waits=w)

            def lnstats(v, sl=sl):
                xv = xt2[:, 1024 * sl:1024 * (sl + 1)]
                emit_stats1024(v, xv, scr[:], sm[:], var[:], mean[:])
                emit_inv(v, var[:], ur[:])
                return v.drain()
            mVs = sig("V", lnstats)
            wt("A", "V", mVs)
            mAs = sig("A", lambda a: a.activation(ur[:], ur[:], AF.Sqrt))
            wt("V", "A", mAs)
            if ti >= 1:
                wt("V", "D", mD_x0)
            if mT_tr_last[sl]:
                wt("V", "T", mT_tr_last[sl])

            def zfscale(v, sl=sl, ti=ti):
                xv = xt2[:, 1024 * sl:1024 * (sl + 1)]
                v.tensor_scalar(zft[:, 1024 * sl:1024 * (sl + 1)], xv, mean[:], ur[:],
                                ALU.subtract, ALU.mult)
                if ti >= 1:
                    v.drain()
                    v.tensor_tensor(x1acc[:, 1024 * (ti - 1):1024 * ti], xv,
                                    x0t2[:, 1024 * sl:1024 * (sl + 1)], ALU.add)
                return v.drain()
            mVz = sig("V", zfscale)
            mV_x_last[sl] = mVz

            for c in range(8):
                def trmm(t, bank, sl=sl, c=c):
                    return t.transpose(bank[:, 0:128], zft[:, 1024 * sl + 128 * c:1024 * sl + 128 * (c + 1)],
                                       identT[:])

                def trev(a, bank, c=c, ti=ti):
                    return a.activation(zfT[:, CT * c + 128 * ti:CT * c + 128 * (ti + 1)],
                                        bank[:], AF.Identity)
                w = [("V", mVz)] if c == 0 else []
                if ti == 0 and c == 0:
                    w.append(("D", mD_small))
                mTt, _ = grp("X", trmm, "A", trev, waits=w)
            mT_tr_last[sl] = mTt

        wt("V", "A", sc.tick["A"])

        def bnd(v):
            for c in range(8):
                v.tensor_scalar(zfT[:, CT * c + 127:CT * c + 128],
                                zfT[:, CT * c + 127:CT * c + 128], bmk[:, 0:1], None, ALU.mult)
            return v.drain()
        mV_bnd = sig("V", bnd)

        NG = 8
        mT_w_last = 0
        m_f1T = [0, 0]
        for g in range(NG):
            gsl = g % 2
            wf1g = wf1b[gsl]
            if m_f1T[gsl]:
                wt("G", "T", m_f1T[gsl])
            elif g == 0:
                wt("G", "T", sc.tick["T"])
            mG_w1 = sig("G", lambda gp, g=g, wf1g=wf1g: gp.dma_start(
                out=wf1g[:].rearrange("p (c j) -> p c j", c=8),
                in_=Wf1I[:, 512 * g:512 * (g + 1)].rearrange("(c p) j -> p c j", p=128)))
            if g > 0:
                wt("G", "T", mT_w_last)
            mG_w2 = sig("G", lambda gp, g=g: gp.dma_start(
                out=wbig[:, 0:4096].rearrange("p (c j) -> p c j", c=4),
                in_=Wf2I[512 * g:512 * (g + 1), :].rearrange("(c p) j -> p c j", p=128)))
            mT_bias = 0
            for oc in range(4):
                def bmm(t, bank, oc=oc, wf1g=wf1g):
                    inst = None
                    for c in range(8):
                        inst = t.matmul(bank[:, 0:64], wf1g[:, 512 * c + 128 * oc:512 * c + 128 * (oc + 1)],
                                        lnfbB[:, 64 * c:64 * c + 64], start=(c == 0), stop=(c == 7))
                    return inst

                def bev(a, bank, oc=oc):
                    return a.activation(biasF[:, oc:oc + 1], bank[:, 0:1], AF.Identity)
                w = [("G", mG_w1), ("V", mV_prep)] if oc == 0 else []
                mT_bias, _ = grp("Q", bmm, "A", bev, waits=w)

            wt("V", "T", mT_bias)

            def wsc(v, wf1g=wf1g):
                for c in range(8):
                    v.tensor_scalar(w1bg[:, 512 * c:512 * (c + 1)], wf1g[:, 512 * c:512 * (c + 1)],
                                    sabF[:, 2 * c + 1:2 * c + 2], None, ALU.mult)
                v.drain()
                for c in range(8):
                    v.tensor_scalar(wf1g[:, 512 * c:512 * (c + 1)], wf1g[:, 512 * c:512 * (c + 1)],
                                    sabF[:, 2 * c:2 * c + 1], None, ALU.mult)
                return v.drain()
            mV_wsc = sig("V", wsc)

            mV_sq = 0
            for oc in range(4):
                for tt in range(4):
                    def f1mm(t, bank, oc=oc, tt=tt, wf1g=wf1g):
                        for c in range(8):
                            t.matmul(bank[:, 0:512], wf1g[:, 512 * c + 128 * oc:512 * c + 128 * (oc + 1)],
                                     zfT[:, CT * c + 128 + 512 * tt:CT * c + 128 + 512 * (tt + 1)],
                                     start=(c == 0), stop=False)
                        inst = None
                        for c in range(8):
                            inst = t.matmul(bank[:, 0:512], w1bg[:, 512 * c + 128 * oc:512 * c + 128 * (oc + 1)],
                                            zfT[:, CT * c + 127 + 512 * tt:CT * c + 127 + 512 * (tt + 1)],
                                            start=False, stop=(c == 7))
                        return inst

                    def f1ev(a, bank, oc=oc, tt=tt):
                        return a.activation(hR[:, 2048 * oc + 512 * tt:2048 * oc + 512 * (tt + 1)],
                                            bank[:, 0:512], AF.Relu, bias=biasF[:, oc:oc + 1])
                    w = []
                    if oc == 0 and tt == 0:
                        w = [("V", mV_wsc), ("V", mV_bnd)]
                    mT_f1c, mAr = grp("S", f1mm, "A", f1ev, waits=w)
                    wt("V", "A", mAr)

                    def sq(v, oc=oc, tt=tt):
                        s = hR[:, 2048 * oc + 512 * tt:2048 * oc + 512 * (tt + 1)]
                        return v.tensor_tensor(s, s, s, ALU.mult)
                    mV_sq = sig("V", sq)
            m_f1T[gsl] = mT_f1c

            for st in range(16):
                for h in range(2):
                    def f2mm(t, bank, st=st, h=h):
                        inst = None
                        for c4 in range(4):
                            inst = t.matmul(bank[:, 0:512], hR[:, 2048 * c4 + 128 * st:2048 * c4 + 128 * (st + 1)],
                                            wbig[:, 1024 * c4 + 512 * h:1024 * c4 + 512 * (h + 1)],
                                            start=(c4 == 0), stop=(c4 == 3))
                        return inst

                    def f2ev(v, bank, st=st, h=h):
                        s = x1acc[:, 1024 * st + 512 * h:1024 * st + 512 * (h + 1)]
                        return v.tensor_tensor(s, s, bank[:, 0:512], ALU.add)
                    w = []
                    if st == 0 and h == 0:
                        w = [("V", mV_sq), ("G", mG_w2)]
                    mT_w_last, _ = grp("S", f2mm, "V", f2ev, waits=w)

        mO = 0
        mT_zs = [0, 0]
        for ti in range(16):
            sl = ti % 2
            wt("V", "T", mT_w_last)

            def lnstats2(v, ti=ti):
                xv = x1acc[:, 1024 * ti:1024 * (ti + 1)]
                emit_stats1024(v, xv, scr[:], sm[:], var[:], mean[:])
                emit_inv(v, var[:], ur[:])
                return v.drain()
            mVs = sig("V", lnstats2)
            wt("A", "V", mVs)
            mAs = sig("A", lambda a: a.activation(ur[:], ur[:], AF.Sqrt))
            wt("V", "A", mAs)
            if ti >= 2:
                wt("V", "O", mO - 1)
                wt("V", "T", mT_zs[sl])

            def z3scale(v, ti=ti, sl=sl):
                xv = x1acc[:, 1024 * ti:1024 * (ti + 1)]
                return v.tensor_scalar(z3t[:, 1024 * sl:1024 * (sl + 1)], xv, mean[:], ur[:],
                                       ALU.subtract, ALU.mult)
            mVz = sig("V", z3scale)

            for h in range(2):
                def zsmm(t, bank, sl=sl, h=h):
                    return t.matmul(bank[:, 0:512], ones128b[:],
                                    z3t[:, 1024 * sl + 512 * h:1024 * sl + 512 * (h + 1)],
                                    start=True, stop=True)

                def zsev(v, bank, h=h):
                    return v.tensor_tensor(zsumacc[0:1, 512 * h:512 * (h + 1)],
                                           zsumacc[0:1, 512 * h:512 * (h + 1)], bank[0:1, 0:512], ALU.add)
                w = [("V", mVz)] if h == 0 else []
                mT_zs[sl], _ = grp("S", zsmm, "V", zsev, waits=w)

            wt("G", "V", mVz)
            mO = sig("O", lambda gp, ti=ti: gp.dma_start(
                out=xres[128 * ti:128 * (ti + 1), :], in_=x1acc[:, 1024 * ti:1024 * (ti + 1)]))
            mO = sig("O", lambda gp, ti=ti, sl=sl: gp.dma_start(
                out=z3o[128 * ti:128 * (ti + 1), :], in_=z3t[:, 1024 * sl:1024 * (sl + 1)]))

        wt("G", "V", sc.tick["V"])
        mO = sig("O", lambda gp: gp.dma_start(out=zsumo, in_=zsumacc[:]))
        wt("G", "O", sc.tick["O"])
        wt("G", "T", sc.tick["T"])
        wt("G", "A", sc.tick["A"])
        sig("G", lambda gp: gp.drain())
        sc.finalize(block)
    return nc


def host_pC_inputs(inputs, xp, og):
    xp = np.asarray(xp, np.float32).reshape(B, T, C)
    og = np.asarray(og, bf).reshape(B, T, C)
    x0 = np.asarray(inputs["L0_attn_x"], np.float32).reshape(B, T, C)
    lnf = [np.asarray(inputs["L0_attn_ln_ffn_w"], np.float32),
           np.asarray(inputs["L0_attn_ln_ffn_b"], np.float32),
           np.asarray(inputs["L0_attn_ffn_x_k"], np.float32)]
    lnfc = np.concatenate([v.reshape(8, 128).T for v in lnf], axis=1)
    maps = []
    for c in range(NC):
        b, q = c // 4, c % 4
        s0 = 2048 * q
        ogp = np.zeros((CT, C), bf)
        xpp = np.zeros((CT, C), np.float32)
        if q == 0:
            ogp[128:] = og[b, 0:2048 + 128 - 128]
            xpp[128:] = xp[b, 0:2048]
            bmv = 0.0
        else:
            ogp[:] = og[b, s0 - 128:s0 + 2048]
            xpp[:] = xp[b, s0 - 128:s0 + 2048]
            bmv = 1.0
        maps.append({
            "ogT": np.ascontiguousarray(ogp.T),
            "xpt": xpp,
            "x0t": np.ascontiguousarray(x0[b, s0:s0 + 2048]),
            "WoI": np.asarray(inputs["L0_attn_W_o"], np.float32),
            "Wf1I": np.asarray(inputs["L0_attn_W_ffn1"], np.float32),
            "Wf2I": np.asarray(inputs["L0_attn_W_ffn2"], np.float32),
            "lnfc": np.ascontiguousarray(lnfc),
            "bmaskI": np.full((128, 1), bmv, np.float32),
            "identI": np.ascontiguousarray(np.eye(128, dtype=bf)),
        })
    return maps


def run_pC(inputs, xp, og):
    nc = build_pC()
    maps = host_pC_inputs(inputs, xp, og)
    res = _run(nc, maps)
    xres = np.concatenate([np.asarray(r["xres"]) for r in res.results], 0)
    z3 = np.concatenate([np.asarray(r["z3o"]) for r in res.results], 0)
    zsum = np.stack([np.asarray(r["zsumo"])[0] for r in res.results], 0)
    return xres, z3, zsum




def build_pD2():
    nc = _bass()
    zparts = nc.dram_tensor("zparts", [128, 32], F32, kind="ExternalInput").ap()
    WrC = nc.dram_tensor("WrC", [128, 64], F32, kind="ExternalInput").ap()
    brB = nc.dram_tensor("brB", [128, 8], F32, kind="ExternalInput").ap()
    oho = nc.dram_tensor("oho", [1, 8], F32, kind="ExternalOutput").ap()
    from contextlib import ExitStack
    with ExitStack() as stack:
        e = stack.enter_context
        zp = e(nc.sbuf_tensor("zp", [128, 32], F32))
        wrc = e(nc.sbuf_tensor("wrc", [128, 64], F32))
        wrb = e(nc.sbuf_tensor("wrb", [128, 64], BF16))
        brb = e(nc.sbuf_tensor("brb", [128, 8], F32))
        zm = e(nc.sbuf_tensor("zm", [128, 8], F32))
        zh = e(nc.sbuf_tensor("zh", [128, 8], BF16))
        zhf = e(nc.sbuf_tensor("zhf", [128, 8], F32))
        zr = e(nc.sbuf_tensor("zr", [128, 8], F32))
        onesR = e(nc.sbuf_tensor("onesR", [128, 128], F32))
        zmB = e(nc.sbuf_tensor("zmB", [128, 16 * 128], BF16))
        sc8 = e(nc.sbuf_tensor("sc8", [128, 8], F32))
        mx = e(nc.sbuf_tensor("mx", [128, 1], F32))
        oh = e(nc.sbuf_tensor("oh", [128, 8], F32))
        pR = e(nc.psum_tensor("pR", [128, 8], F32))
        sg = e(nc.semaphore("sg"))
        sv = e(nc.semaphore("sv"))
        st = e(nc.semaphore("st"))
        so = e(nc.semaphore("so"))
        block = e(nc.Block())

        @block.sync
        def _(sync):
            sync.dma_start(out=zp[:], in_=zparts).then_inc(sg, 16)
            sync.dma_start(out=wrc[:], in_=WrC).then_inc(sg, 16)
            sync.dma_start(out=brb[:], in_=brB).then_inc(sg, 16)
            sync.wait_ge(sv, 3)
            sync.dma_start(out=oho, in_=oh[0:1, 0:8]).then_inc(so, 16)

        @block.vector
        def _(v):
            v.memset(onesR[:], 1.0)
            v.wait_ge(sg, 48)
            v.tensor_tensor(zm[:], zp[:, 0:8], zp[:, 8:16], ALU.add)
            v.tensor_copy(wrb[:], wrc[:])
            v.drain()
            v.tensor_tensor(zm[:], zm[:], zp[:, 16:24], ALU.add)
            v.drain()
            v.tensor_tensor(zm[:], zm[:], zp[:, 24:32], ALU.add)
            v.drain()
            v.tensor_scalar(zm[:], zm[:], 1.0 / float(T), None, ALU.mult)
            v.drain()
            v.tensor_copy(zh[:], zm[:])
            v.drain()
            v.tensor_copy(zhf[:], zh[:])
            v.drain()
            v.tensor_tensor(zr[:], zm[:], zhf[:], ALU.subtract)
            v.drain()
            for c in range(8):
                v.tensor_scalar(zmB[:, 128 * c:128 * (c + 1)], onesR[:],
                                zm[:, c:c + 1], None, ALU.mult)
                v.tensor_scalar(zmB[:, 1024 + 128 * c:1024 + 128 * (c + 1)], onesR[:],
                                zr[:, c:c + 1], None, ALU.mult)
            v.drain().then_inc(sv, 1)
            v.wait_ge(st, 1)
            v.tensor_tensor(sc8[:], pR[:], brb[:], ALU.add)
            v.drain()
            v.tensor_reduce(mx[:], sc8[:], AX, ALU.max)
            v.drain()
            v.tensor_scalar(oh[:], sc8[:], mx[:, 0:1], None, ALU.is_ge)
            v.drain().then_inc(sv, 2)

        @block.tensor
        def _(t):
            t.wait_ge(sv, 1)
            for c in range(16):
                t.matmul(pR[:], zmB[:, 128 * c:128 * (c + 1)], wrb[:, 8 * (c % 8):8 * (c % 8 + 1)],
                         start=(c == 0), stop=(c == 15))
            t.drain().then_inc(st, 1)

        @block.gpsimd
        def _(g):
            g.wait_ge(so, 16)
    return nc


def run_pD2(inputs, zsum):
    nc = build_pD2()
    Wr = np.asarray(inputs["L0_ffn_Wr"], np.float32)
    br = np.asarray(inputs["L0_ffn_br"], np.float32)
    WrC = np.concatenate([Wr.reshape(8, 128, 8)[c] for c in range(8)], axis=1)
    brB = np.ascontiguousarray(np.broadcast_to(br[None, :], (128, 8)))
    maps = []
    for c in range(NC):
        b = c // 4
        zparts = np.concatenate([zsum[4 * b + j].reshape(8, 128).T for j in range(4)], axis=1)
        maps.append({"zparts": np.ascontiguousarray(zparts), "WrC": np.ascontiguousarray(WrC),
                     "brB": brB})
    res = _run(nc, maps)
    ohs = [np.asarray(res.results[c]["oho"])[0] for c in (0, 4)]
    return ohs




def build_pD():
    nc = _bass()
    xresI = nc.dram_tensor("xresI", [TPC, C], F32, kind="ExternalInput").ap()
    z3TI = nc.dram_tensor("z3TI", [C, TPC], BF16, kind="ExternalInput").ap()
    W1e = nc.dram_tensor("W1e", [C, 4096], F32, kind="ExternalInput").ap()
    b1CI = nc.dram_tensor("b1CI", [128, 32], F32, kind="ExternalInput").ap()
    W2e = nc.dram_tensor("W2e", [4096, C], F32, kind="ExternalInput").ap()
    b2BI = nc.dram_tensor("b2BI", [128, C], F32, kind="ExternalInput").ap()
    WqD = nc.dram_tensor("WqD", [C, C], F32, kind="ExternalInput").ap()
    WkD = nc.dram_tensor("WkD", [C, C], F32, kind="ExternalInput").ap()
    WvI = nc.dram_tensor("WvI", [C, C], F32, kind="ExternalInput").ap()
    csQI = nc.dram_tensor("csQI", [128, 8192], F32, kind="ExternalInput").ap()
    snQI = nc.dram_tensor("snQI", [128, 8192], F32, kind="ExternalInput").ap()
    csKI = nc.dram_tensor("csKI", [128, 8192], F32, kind="ExternalInput").ap()
    snKI = nc.dram_tensor("snKI", [128, 8192], F32, kind="ExternalInput").ap()
    identI = nc.dram_tensor("identI", [128, 128], BF16, kind="ExternalInput").ap()
    x2o = nc.dram_tensor("x2o", [TPC, C], F32, kind="ExternalOutput").ap()
    QTo = nc.dram_tensor("QTo", [C, TPC], BF16, kind="ExternalOutput").ap()
    KTo = nc.dram_tensor("KTo", [C, TPC], BF16, kind="ExternalOutput").ap()
    Vo = nc.dram_tensor("Vo", [TPC, C], BF16, kind="ExternalOutput").ap()

    from contextlib import ExitStack
    with ExitStack() as stack:
        e = stack.enter_context
        z3s = e(nc.sbuf_tensor("z3s", [128, 8 * 2048], BF16))
        SCR = e(nc.sbuf_tensor("SCR", [128, 16 * 1024], F32))
        hR = e(nc.sbuf_tensor("hR", [128, 4 * 2048], BF16))
        wf1g = e(nc.sbuf_tensor("wf1g", [128, 4096], BF16))
        wbig = e(nc.sbuf_tensor("wbig", [128, 8192], BF16))
        qkvB = e(nc.sbuf_tensor("qkvB", [128, 16 * 1024], BF16))
        xrt = e(nc.sbuf_tensor("xrt", [128, 2 * 1024], F32))
        z4t = e(nc.sbuf_tensor("z4t", [128, 2 * 1024], BF16))
        rtmp = e(nc.sbuf_tensor("rtmp", [128, 2048], BF16))
        rtm2 = e(nc.sbuf_tensor("rtm2", [128, 2048], BF16))
        csb = e(nc.sbuf_tensor("csb", [128, 2048], BF16))
        snb = e(nc.sbuf_tensor("snb", [128, 2048], BF16))
        b1C = e(nc.sbuf_tensor("b1C", [128, 32], F32))
        b2B = e(nc.sbuf_tensor("b2B", [128, 1024], F32))
        identT = e(nc.sbuf_tensor("identT", [128, 128], BF16))
        scr = e(nc.sbuf_tensor("scr", [128, 1024], F32))
        sm = e(nc.sbuf_tensor("sm", [128, 1], F32))
        mean = e(nc.sbuf_tensor("mean", [128, 1], F32))
        var = e(nc.sbuf_tensor("var", [128, 1], F32))
        ur = e(nc.sbuf_tensor("ur", [128, 1], F32))
        pM = [e(nc.psum_tensor(f"pM{j}", [128, 512], F32)) for j in range(6)]
        pX = [e(nc.psum_tensor("pX0", [128, 128], BF16))]
        sc = Sched(stack, nc, {
            "T": ("T", 1, 800), "V": ("V", 1, 700), "A": ("A", 1, 600),
            "D": ("S", 16, 60), "G": ("G", 16, 40), "O": ("G", 16, 60),
        }, {"S": pM, "X": pX})
        sig, wt, run, grp = sc.sig, sc.wt, sc.run, sc.grp
        block = e(nc.Block())

        run("A", lambda a: a.preload_act_table(AF.Sqrt))
        run("A", lambda a: a.drain())
        mD = sig("D", lambda s: s.dma_start(out=identT[:], in_=identI))
        mD = sig("D", lambda s: s.dma_start(out=b1C[:], in_=b1CI))
        mD = sig("D", lambda s: s.dma_start(out=b2B[:], in_=b2BI))
        mD_small = mD
        mD_z3 = sig("D", lambda s: s.dma_start(
            out=z3s[:].rearrange("p (c t) -> p c t", c=8),
            in_=z3TI.rearrange("(c p) t -> p c t", p=128)))

        mV_init = 0
        for ti in range(16):
            sl = ti % 2
            if ti >= 2:
                wt("S", "V", mV_init - 1)
            mD_x = sig("D", lambda s, ti=ti, sl=sl: s.dma_start(
                out=xrt[:, 1024 * sl:1024 * (sl + 1)], in_=xresI[128 * ti:128 * (ti + 1), :]))
            wt("V", "D", mD_x)
            if ti == 0:
                wt("V", "D", mD_small)

            def init(v, ti=ti, sl=sl):
                v.tensor_tensor(SCR[:, 1024 * ti:1024 * (ti + 1)],
                                xrt[:, 1024 * sl:1024 * (sl + 1)], b2B[:], ALU.add)
                return v.drain()
            mV_init = sig("V", init)

        NG = 8
        mT_w_last = 0
        for g in range(NG):
            if g > 0:
                wt("G", "T", mT_w_last)
            mG_w1 = sig("G", lambda gp, g=g: gp.dma_start(
                out=wf1g[:].rearrange("p (c j) -> p c j", c=8),
                in_=W1e[:, 512 * g:512 * (g + 1)].rearrange("(c p) j -> p c j", p=128)))
            mG_w2 = sig("G", lambda gp, g=g: gp.dma_start(
                out=wbig[:, 0:4096].rearrange("p (c j) -> p c j", c=4),
                in_=W2e[512 * g:512 * (g + 1), :].rearrange("(c p) j -> p c j", p=128)))
            for oc in range(4):
                for tt in range(4):
                    def f1mm(t, bank, oc=oc, tt=tt):
                        inst = None
                        for c in range(8):
                            inst = t.matmul(bank[:, 0:512],
                                            wf1g[:, 512 * c + 128 * oc:512 * c + 128 * (oc + 1)],
                                            z3s[:, 2048 * c + 512 * tt:2048 * c + 512 * (tt + 1)],
                                            start=(c == 0), stop=(c == 7))
                        return inst

                    def f1ev(a, bank, oc=oc, tt=tt, g=g):
                        return a.activation(hR[:, 2048 * oc + 512 * tt:2048 * oc + 512 * (tt + 1)],
                                            bank[:, 0:512], AF.Relu, bias=b1C[:, 4 * g + oc:4 * g + oc + 1])
                    w = []
                    if oc == 0 and tt == 0:
                        w = [("G", mG_w1), ("D", mD_z3), ("D", mD_small)]
                    _, mA_h = grp("S", f1mm, "A", f1ev, waits=w)
            for st in range(16):
                for h in range(2):
                    def f2mm(t, bank, st=st, h=h):
                        inst = None
                        for c4 in range(4):
                            inst = t.matmul(bank[:, 0:512],
                                            hR[:, 2048 * c4 + 128 * st:2048 * c4 + 128 * (st + 1)],
                                            wbig[:, 1024 * c4 + 512 * h:1024 * c4 + 512 * (h + 1)],
                                            start=(c4 == 0), stop=(c4 == 3))
                        return inst

                    def f2ev(v, bank, st=st, h=h):
                        s = SCR[:, 1024 * st + 512 * h:1024 * st + 512 * (h + 1)]
                        return v.tensor_tensor(s, s, bank[:, 0:512], ALU.add)
                    w = []
                    if st == 0 and h == 0:
                        w = [("A", mA_h), ("G", mG_w2), ("V", mV_init)]
                    mT_w_last, mV_x2 = grp("S", f2mm, "V", f2ev, waits=w)

        mO_x2 = 0
        mT_tr = 0
        mT_trs = [0, 0]
        for ti in range(16):
            sl = ti % 2

            def lnstats(v, ti=ti):
                xv = SCR[:, 1024 * ti:1024 * (ti + 1)]
                emit_stats1024(v, xv, scr[:], sm[:], var[:], mean[:])
                emit_inv(v, var[:], ur[:])
                return v.drain()
            if ti == 0:
                wt("V", "V", mV_x2)
            mVs = sig("V", lnstats)
            wt("A", "V", mVs)
            mAs = sig("A", lambda a: a.activation(ur[:], ur[:], AF.Sqrt))
            wt("V", "A", mAs)
            if mT_trs[sl]:
                wt("V", "T", mT_trs[sl])

            def z4scale(v, ti=ti, sl=sl):
                xv = SCR[:, 1024 * ti:1024 * (ti + 1)]
                return v.tensor_scalar(z4t[:, 1024 * sl:1024 * (sl + 1)], xv, mean[:], ur[:],
                                       ALU.subtract, ALU.mult)
            mVz = sig("V", z4scale)
            wt("G", "V", mVz)
            mO_x2 = sig("O", lambda gp, ti=ti: gp.dma_start(
                out=x2o[128 * ti:128 * (ti + 1), :], in_=SCR[:, 1024 * ti:1024 * (ti + 1)]))
            for c in range(8):
                def trmm(t, bank, sl=sl, c=c):
                    return t.transpose(bank[:, 0:128],
                                       z4t[:, 1024 * sl + 128 * c:1024 * sl + 128 * (c + 1)], identT[:])

                def trev(a, bank, c=c, ti=ti):
                    return a.activation(z3s[:, 2048 * c + 128 * ti:2048 * c + 128 * (ti + 1)],
                                        bank[:], AF.Identity)
                w = []
                if c == 0:
                    w = [("V", mVz)]
                    if ti == 0:
                        w.append(("T", mT_w_last))
                mT_tr, _ = grp("X", trmm, "A", trev, waits=w)
            mT_trs[sl] = mT_tr

        wt("G", "O", sc.tick["O"])
        mG_cs = sig("G", lambda gp: gp.dma_start(out=SCR[:, 0:8192], in_=csQI))
        mG_cs = sig("G", lambda gp: gp.dma_start(out=SCR[:, 8192:16384], in_=snQI))
        mT_proj_last = 0
        mV_rot = 0
        for which in range(3):
            WI = (WqD, WkD, WvI)[which]
            if which > 0:
                wt("G", "T", mT_proj_last)
            mG_w = sig("G", lambda gp, WI=WI: gp.dma_start(
                out=wbig[:].rearrange("p (c j) -> p c j", c=8),
                in_=WI.rearrange("(c p) j -> p c j", p=128)))
            if which == 1:
                wt("G", "V", mV_rot)
                mG_cs = sig("G", lambda gp: gp.dma_start(out=SCR[:, 0:8192], in_=csKI))
                mG_cs = sig("G", lambda gp: gp.dma_start(out=SCR[:, 8192:16384], in_=snKI))
            if which < 2:
                for oc in range(8):
                    for tt in range(4):
                        def qmm(t, bank, oc=oc, tt=tt):
                            inst = None
                            for c in range(8):
                                inst = t.matmul(bank[:, 0:512],
                                                wbig[:, 1024 * c + 128 * oc:1024 * c + 128 * (oc + 1)],
                                                z3s[:, 2048 * c + 512 * tt:2048 * c + 512 * (tt + 1)],
                                                start=(c == 0), stop=(c == 7))
                            return inst

                        def qev(a, bank, oc=oc, tt=tt):
                            return a.activation(qkvB[:, 2048 * oc + 512 * tt:2048 * oc + 512 * (tt + 1)],
                                                bank[:, 0:512], AF.Identity)
                        w = []
                        if oc == 0 and tt == 0:
                            w = [("G", mG_w), ("T", mT_tr), ("A", sc.tick["A"])]
                            if which > 0:
                                w.append(("V", sc.tick["V"]))
                                w.append(("O", sc.tick["O"]))
                        mT_proj_last, mA_p = grp("S", qmm, "A", qev, waits=w)
                wt("V", "A", mA_p)
                wt("V", "G", mG_cs)

                def rot(v):
                    for c in range(4):
                        ev = qkvB[:, 2048 * c:2048 * (c + 1)]
                        od = qkvB[:, 2048 * (c + 4):2048 * (c + 5)]
                        v.tensor_copy(csb[:], SCR[:, 2048 * c:2048 * (c + 1)])
                        v.tensor_copy(snb[:], SCR[:, 8192 + 2048 * c:8192 + 2048 * (c + 1)])
                        v.tensor_copy(rtmp[:], ev)
                        v.drain()
                        v.tensor_tensor(ev, ev, csb[:], ALU.mult)
                        v.tensor_tensor(rtm2[:], od, snb[:], ALU.mult)
                        v.drain()
                        v.tensor_tensor(ev, ev, rtm2[:], ALU.subtract)
                        v.tensor_tensor(od, od, csb[:], ALU.mult)
                        v.drain()
                        v.tensor_tensor(rtm2[:], rtmp[:], snb[:], ALU.mult)
                        v.drain()
                        v.tensor_tensor(od, od, rtm2[:], ALU.add)
                        v.drain()
                    return v.drain()
                mV_rot = sig("V", rot)
                wt("G", "V", mV_rot)
                out_t = (QTo, KTo)[which]
                sig("O", lambda gp, out_t=out_t: gp.dma_start(
                    out=out_t.rearrange("(c p) t -> p c t", p=128),
                    in_=qkvB[:].rearrange("p (c t) -> p c t", c=8)))
            else:
                for st in range(16):
                    for h in range(2):
                        def vmm(t, bank, st=st, h=h):
                            inst = None
                            for c in range(8):
                                inst = t.matmul(bank[:, 0:512],
                                                z3s[:, 2048 * c + 128 * st:2048 * c + 128 * (st + 1)],
                                                wbig[:, 1024 * c + 512 * h:1024 * c + 512 * (h + 1)],
                                                start=(c == 0), stop=(c == 7))
                            return inst

                        def vev(a, bank, st=st, h=h):
                            return a.activation(qkvB[:, 1024 * st + 512 * h:1024 * st + 512 * (h + 1)],
                                                bank[:, 0:512], AF.Identity)
                        w = []
                        if st == 0 and h == 0:
                            w = [("G", mG_w), ("V", sc.tick["V"]), ("O", sc.tick["O"])]
                        mT_proj_last, mA_p = grp("S", vmm, "A", vev, waits=w)
                wt("G", "A", mA_p)
                sig("O", lambda gp: gp.dma_start(
                    out=Vo.rearrange("(s p) j -> p s j", p=128),
                    in_=qkvB[:].rearrange("p (s j) -> p s j", s=16)))

        wt("G", "O", sc.tick["O"])
        wt("G", "T", sc.tick["T"])
        wt("G", "V", sc.tick["V"])
        wt("G", "A", sc.tick["A"])
        sig("G", lambda gp: gp.drain())
        sc.finalize(block)
    return nc


def _xpos_tabs(start, downscale):
    half = C // 2
    j = np.arange(half, dtype=np.float64)
    scale_vec = (2.0 * j + 0.4 * C) / (1.4 * C)
    pos = np.arange(start, start + TPC, dtype=np.float64)
    scale = scale_vec[None, :] ** (pos[:, None] / 512.0)
    if downscale:
        scale = 1.0 / scale
    inv_freq = 1.0 / 10000.0 ** (j / half)
    ang = pos[:, None] * inv_freq[None, :]
    gam = GAMMA ** ((pos % 128.0) if not downscale else -(pos % 128.0))
    cs = np.cos(ang) * scale * gam[:, None]
    sn = np.sin(ang) * scale * gam[:, None]
    cs = cs.T.reshape(4, 128, TPC).transpose(1, 0, 2).reshape(128, 4 * TPC)
    sn = sn.T.reshape(4, 128, TPC).transpose(1, 0, 2).reshape(128, 4 * TPC)
    z = np.zeros((128, 8192), np.float32)
    z[:, 0:4 * TPC] = cs
    z2_ = np.zeros((128, 8192), np.float32)
    z2_[:, 0:4 * TPC] = sn
    return np.ascontiguousarray(z), np.ascontiguousarray(z2_)


def host_pD_inputs(inputs, xres, z3, experts):
    maps = []
    deint = lambda W: np.ascontiguousarray(
        np.concatenate([np.asarray(W, np.float32)[:, 0::2], np.asarray(W, np.float32)[:, 1::2]], axis=1))
    WqD = deint(inputs["L1_attn_W_Q"])
    WkD = deint(inputs["L1_attn_W_K"])
    WvI = np.asarray(inputs["L1_attn_W_V"], np.float32)
    identI = np.ascontiguousarray(np.eye(128, dtype=bf))
    for c in range(NC):
        b, q = c // 4, c % 4
        s0 = 2048 * q
        ei = experts[b]
        W1 = np.asarray(inputs["L0_ffn_W1"][ei], np.float32)
        W2 = np.asarray(inputs["L0_ffn_W2"][ei], np.float32)
        b1 = np.asarray(inputs["L0_ffn_b1"][ei], np.float32)
        b2 = np.asarray(inputs["L0_ffn_b2"][ei], np.float32)
        csQ, snQ = _xpos_tabs(s0, False)
        csK, snK = _xpos_tabs(s0, True)
        rows = slice(8192 * b + s0, 8192 * b + s0 + 2048)
        maps.append({
            "xresI": np.ascontiguousarray(np.asarray(xres, np.float32)[rows]),
            "z3TI": np.ascontiguousarray(np.asarray(z3, bf)[rows].T),
            "W1e": W1, "W2e": W2,
            "b1CI": np.ascontiguousarray(b1.reshape(32, 128).T),
            "b2BI": np.ascontiguousarray(np.broadcast_to(b2[None, :], (128, C))),
            "WqD": WqD, "WkD": WkD, "WvI": WvI,
            "csQI": csQ, "snQI": snQ, "csKI": csK, "snKI": snK,
            "identI": identI,
        })
    return maps


def run_pD(inputs, xres, z3, experts):
    nc = build_pD()
    res = _run(nc, host_pD_inputs(inputs, xres, z3, experts))
    x2 = np.concatenate([np.asarray(r["x2o"]) for r in res.results], 0)
    QT = [np.asarray(r["QTo"]) for r in res.results]
    KT = [np.asarray(r["KTo"]) for r in res.results]
    V = [np.asarray(r["Vo"]) for r in res.results]
    return x2, QT, KT, V



EWIN = 27
EKT = 4 * 15 + EWIN


def build_pE():
    nc = _bass()
    QTi = nc.dram_tensor("QTi", [C, TPC], BF16, kind="ExternalInput").ap()
    KTw = nc.dram_tensor("KTw", [C, EKT * 128], BF16, kind="ExternalInput").ap()
    Vw = nc.dram_tensor("Vw", [EKT * 128, C], BF16, kind="ExternalInput").ap()
    mleI = nc.dram_tensor("mleI", [128, 128], F32, kind="ExternalInput").ap()
    Eo = nc.dram_tensor("Eo", [TPC, C], F32, kind="ExternalOutput").ap()

    from contextlib import ExitStack
    with ExitStack() as stack:
        e = stack.enter_context
        qS = e(nc.sbuf_tensor("qS", [128, 8 * 2048], BF16))
        outacc = e(nc.sbuf_tensor("outacc", [128, 16 * 1024], F32))
        kG = [e(nc.sbuf_tensor(f"kG{j}", [128, 8 * 1024], BF16)) for j in range(2)]
        vG = [e(nc.sbuf_tensor(f"vG{j}", [128, 8 * 1024], BF16)) for j in range(2)]
        scB = e(nc.sbuf_tensor("scB", [128, 2 * 8 * 128], BF16))
        mleT = e(nc.sbuf_tensor("mleT", [128, 128], F32))
        pS = [e(nc.psum_tensor(f"pS{j}", [128, 128], F32)) for j in range(2)]
        pA = [e(nc.psum_tensor(f"pA{j}", [128, 512], F32)) for j in range(4)]
        sc = Sched(stack, nc, {
            "T": ("T", 1, 4000), "V": ("V", 1, 4000), "A": ("A", 1, 100),
            "D": ("S", 16, 200), "O": ("G", 16, 40), "G": ("G", 1, 10),
        }, {"P": pS, "AV": pA})
        sig, wt, run, grp = sc.sig, sc.wt, sc.run, sc.grp
        block = e(nc.Block())

        mD = sig("D", lambda s: s.dma_start(out=mleT[:], in_=mleI))
        mD = sig("D", lambda s: s.dma_start(
            out=qS[:].rearrange("p (c t) -> p c t", c=8),
            in_=QTi.rearrange("(c p) t -> p c t", p=128)))
        mD_q = mD

        def vinit(v):
            v.memset(outacc[:], 0.0)
            return v.drain()
        mV_init = sig("V", vinit)

        NKG = (EKT + 7) // 8
        mT_kv = [0, 0]
        mD_kv = [0, 0]
        for kg in range(NKG):
            gsl = kg % 2
            j0 = 8 * kg
            nj = min(8, EKT - j0)
            if mT_kv[gsl]:
                wt("S", "T", mT_kv[gsl])
            for jr in range(nj):
                mD_kv[gsl] = sig("D", lambda s, j=j0 + jr, jr=jr, gsl=gsl: s.dma_start(
                    out=kG[gsl][:, 1024 * jr:1024 * (jr + 1)].rearrange("p (c t) -> p c t", c=8),
                    in_=KTw.rearrange("(c p) t -> p c t", p=128)[:, :, 128 * j:128 * (j + 1)]))
                mD_kv[gsl] = sig("D", lambda s, j=j0 + jr, jr=jr, gsl=gsl: s.dma_start(
                    out=vG[gsl][:, 1024 * jr:1024 * (jr + 1)],
                    in_=Vw[128 * j:128 * (j + 1), :]))
            qlo = max(0, -(-(j0 - 26) // 4))
            qhi = min(15, (j0 + 7) // 4)
            for qi in range(qlo, qhi + 1):
                js = [j for j in range(j0, j0 + nj) if 4 * qi <= j <= 4 * qi + 26]
                if not js:
                    continue
                mV_sc = 0
                for ji, j in enumerate(js):
                    dlt = 4 * qi + 26 - j
                    g64 = float(GAMMA ** (64.0 * dlt))

                    def smm(t, bank, j=j, qi=qi, gsl=gsl, j0=j0):
                        inst = None
                        for c in range(8):
                            inst = t.matmul(bank[:, 0:128],
                                            kG[gsl][:, 1024 * (j - j0) + 128 * c:1024 * (j - j0) + 128 * (c + 1)],
                                            qS[:, 2048 * c + 128 * qi:2048 * c + 128 * (qi + 1)],
                                            start=(c == 0), stop=(c == 7))
                        return inst

                    def sev(v, bank, ji=ji, dlt=dlt, g64=g64, gsl=gsl):
                        dst = scB[:, 1024 * gsl + 128 * ji:1024 * gsl + 128 * (ji + 1)]
                        if dlt == 0:
                            v.tensor_tensor(dst, bank[:], mleT[:], ALU.mult)
                        else:
                            v.tensor_scalar(dst, bank[:], g64, g64, ALU.mult, ALU.mult)
                        return v.drain()
                    w = []
                    if kg == 0 and qi == qlo and ji == 0:
                        w = [("D", mD_q), ("D", mD)]
                    if ji == 0:
                        w = w + [("D", mD_kv[gsl])]
                    _, mV_sc = grp("P", smm, "V", sev, waits=w)
                for h in range(2):
                    def avmm(t, bank, js=js, j0=j0, qi=qi, h=h, gsl=gsl):
                        inst = None
                        for ji in range(len(js)):
                            inst = t.matmul(bank[:, 0:512],
                                            scB[:, 1024 * gsl + 128 * ji:1024 * gsl + 128 * (ji + 1)],
                                            vG[gsl][:, 1024 * (js[ji] - j0) + 512 * h:1024 * (js[ji] - j0) + 512 * (h + 1)],
                                            start=(ji == 0), stop=(ji == len(js) - 1))
                        return inst

                    def avev(v, bank, qi=qi, h=h):
                        s = outacc[:, 1024 * qi + 512 * h:1024 * qi + 512 * (h + 1)]
                        return v.tensor_tensor(s, s, bank[:, 0:512], ALU.add)
                    w = [("V", mV_sc)] if h == 0 else []
                    if kg == 0 and qi == qlo and h == 0:
                        w.append(("V", mV_init))
                    mT_avg, _ = grp("AV", avmm, "V", avev, waits=w)
                mT_kv[gsl] = mT_avg

        wt("G", "V", sc.tick["V"])
        for qi in range(16):
            sig("O", lambda gp, qi=qi: gp.dma_start(
                out=Eo[128 * qi:128 * (qi + 1), :], in_=outacc[:, 1024 * qi:1024 * (qi + 1)]))
        wt("G", "O", sc.tick["O"])
        wt("G", "T", sc.tick["T"])
        sig("G", lambda gp: gp.drain())
        sc.finalize(block)
    return nc


def run_pE(QT, KT, V):
    nc = build_pE()
    mle = (np.arange(128)[:, None] <= np.arange(128)[None, :]).astype(np.float32)
    maps = []
    for c in range(NC):
        b, q = c // 4, c % 4
        KTb = np.concatenate([KT[4 * b + j] for j in range(4)], axis=1)
        Vb = np.concatenate([V[4 * b + j] for j in range(4)], axis=0)
        QTb = np.concatenate([QT[4 * b + j] for j in range(4)], axis=1)
        qcols = np.concatenate([QTb[:, 128 * (q + 4 * i):128 * (q + 4 * i + 1)]
                                for i in range(16)], axis=1)
        KTw = np.zeros((C, EKT * 128), bf)
        Vw = np.zeros((EKT * 128, C), bf)
        for j in range(EKT):
            gt = q - 26 + j
            if 0 <= gt < 64:
                KTw[:, 128 * j:128 * (j + 1)] = KTb[:, 128 * gt:128 * (gt + 1)]
                Vw[128 * j:128 * (j + 1), :] = Vb[128 * gt:128 * (gt + 1), :]
        maps.append({"QTi": np.ascontiguousarray(qcols), "KTw": KTw, "Vw": Vw, "mleI": mle})
    res = _run(nc, maps)
    ret = np.zeros((B, T, C), np.float32)
    for c in range(NC):
        b, q = c // 4, c % 4
        eo = np.asarray(res.results[c]["Eo"])
        for i in range(16):
            gt = q + 4 * i
            ret[b, 128 * gt:128 * (gt + 1)] = eo[128 * i:128 * (i + 1)]
    return ret.reshape(B * T, C)



NEXP = 16


def build_pF():
    nc = _bass()
    x2I = nc.dram_tensor("x2I", [TPC, C], F32, kind="ExternalInput").ap()
    retI = nc.dram_tensor("retI", [TPC, C], F32, kind="ExternalInput").ap()
    eW1s = nc.dram_tensor("eW1s", [NEXP * C, C], F32, kind="ExternalInput").ap()
    eW2s = nc.dram_tensor("eW2s", [NEXP * C, C], F32, kind="ExternalInput").ap()
    eb1C = nc.dram_tensor("eb1C", [128, 8 * NEXP], F32, kind="ExternalInput").ap()
    eb2B = nc.dram_tensor("eb2B", [128, NEXP * C], F32, kind="ExternalInput").ap()
    rWp = nc.dram_tensor("rWp", [C, 16], F32, kind="ExternalInput").ap()
    rbB = nc.dram_tensor("rbB", [128, 16], F32, kind="ExternalInput").ap()
    identI = nc.dram_tensor("identI", [128, 128], BF16, kind="ExternalInput").ap()
    x4o = nc.dram_tensor("x4o", [TPC, C], F32, kind="ExternalOutput").ap()

    from contextlib import ExitStack
    with ExitStack() as stack:
        e = stack.enter_context
        x4acc = e(nc.sbuf_tensor("x4acc", [128, 16 * 1024], F32))
        z5T = e(nc.sbuf_tensor("z5T", [128, 8 * 2048], BF16))
        hE = e(nc.sbuf_tensor("hE", [128, 8 * 2048], BF16))
        ew1b = [e(nc.sbuf_tensor(f"ew1b{j}", [128, 8192], BF16)) for j in range(2)]
        ew2 = e(nc.sbuf_tensor("ew2", [128, 8192], BF16))
        x2t = e(nc.sbuf_tensor("x2t", [128, 2 * 1024], F32))
        rtt = e(nc.sbuf_tensor("rtt", [128, 2 * 1024], F32))
        z5t = e(nc.sbuf_tensor("z5t", [128, 2 * 1024], BF16))
        b1c = e(nc.sbuf_tensor("b1c", [128, 8 * NEXP], F32))
        b2s = e(nc.sbuf_tensor("b2s", [128, 1024], BF16))
        cwA = e(nc.sbuf_tensor("cwA", [128, 256], F32))
        onesPF = e(nc.sbuf_tensor("onesPF", [128, 128], BF16))
        rwb = e(nc.sbuf_tensor("rwb", [128, 8 * 16], BF16))
        rbb = e(nc.sbuf_tensor("rbb", [128, 16], F32))
        lg = e(nc.sbuf_tensor("lg", [128, 16], F32))
        pr = e(nc.sbuf_tensor("pr", [128, 16], F32))
        prm = e(nc.sbuf_tensor("prm", [128, 16], F32))
        ohx = e(nc.sbuf_tensor("ohx", [128, 16], F32))

        mx = e(nc.sbuf_tensor("mx", [128, 1], F32))
        nmx = e(nc.sbuf_tensor("nmx", [128, 1], F32))
        ssum = e(nc.sbuf_tensor("ssum", [128, 1], F32))
        identT = e(nc.sbuf_tensor("identT", [128, 128], BF16))
        scr = e(nc.sbuf_tensor("scr", [128, 1024], F32))
        sm = e(nc.sbuf_tensor("sm", [128, 1], F32))
        mean = e(nc.sbuf_tensor("mean", [128, 1], F32))
        var = e(nc.sbuf_tensor("var", [128, 1], F32))
        ur = e(nc.sbuf_tensor("ur", [128, 1], F32))
        pM = [e(nc.psum_tensor(f"pM{j}", [128, 512], F32)) for j in range(6)]
        pX = [e(nc.psum_tensor("pX0", [128, 128], BF16))]
        pR = [e(nc.psum_tensor("pR0", [128, 16], F32))]
        sc = Sched(stack, nc, {
            "T": ("T", 1, 5200), "V": ("V", 1, 3000), "A": ("A", 1, 4000),
            "D": ("S", 16, 60), "G": ("G", 16, 150), "O": ("G", 16, 60),
        }, {"S": pM, "X": pX, "R": pR})
        sig, wt, run, grp = sc.sig, sc.wt, sc.run, sc.grp
        block = e(nc.Block())

        run("A", lambda a: a.preload_act_table(AF.Sqrt))
        run("A", lambda a: a.drain())
        sig("V", lambda v: (v.memset(onesPF[:], 1.0), v.drain())[1])
        mD = sig("D", lambda s: s.dma_start(out=identT[:], in_=identI))
        mD = sig("D", lambda s: s.dma_start(out=b1c[:], in_=eb1C))
        mD = sig("D", lambda s: s.dma_start(out=rbb[:], in_=rbB))
        mD_small = mD
        mG_rw = sig("G", lambda gp: gp.dma_start(
            out=rwb[:].rearrange("p (c j) -> p c j", c=8),
            in_=rWp.rearrange("(c p) j -> p c j", p=128)))

        mT_tr = 0
        mT_trs = [0, 0]
        mV_ln = 0
        for ti in range(16):
            sl = ti % 2
            if ti >= 2:
                wt("S", "V", mV_ln)
            mD_x = sig("D", lambda s, ti=ti, sl=sl: s.dma_start(
                out=x2t[:, 1024 * sl:1024 * (sl + 1)], in_=x2I[128 * ti:128 * (ti + 1), :]))
            mD_r = sig("D", lambda s, ti=ti, sl=sl: s.dma_start(
                out=rtt[:, 1024 * sl:1024 * (sl + 1)], in_=retI[128 * ti:128 * (ti + 1), :]))
            wt("V", "D", mD_r)

            def x3i(v, ti=ti, sl=sl):
                v.tensor_tensor(x4acc[:, 1024 * ti:1024 * (ti + 1)],
                                x2t[:, 1024 * sl:1024 * (sl + 1)],
                                rtt[:, 1024 * sl:1024 * (sl + 1)], ALU.add)
                return v.drain()
            sig("V", x3i)

            def lnstats(v, ti=ti):
                xv = x4acc[:, 1024 * ti:1024 * (ti + 1)]
                emit_stats1024(v, xv, scr[:], sm[:], var[:], mean[:])
                emit_inv(v, var[:], ur[:])
                return v.drain()
            mVs = sig("V", lnstats)
            wt("A", "V", mVs)
            mAs = sig("A", lambda a: a.activation(ur[:], ur[:], AF.Sqrt))
            wt("V", "A", mAs)
            if mT_trs[sl]:
                wt("V", "T", mT_trs[sl])

            def z5s(v, ti=ti, sl=sl):
                xv = x4acc[:, 1024 * ti:1024 * (ti + 1)]
                return v.tensor_scalar(z5t[:, 1024 * sl:1024 * (sl + 1)], xv, mean[:], ur[:],
                                       ALU.subtract, ALU.mult)
            mV_ln = sig("V", z5s)
            for c in range(8):
                def trmm(t, bank, sl=sl, c=c):
                    return t.transpose(bank[:, 0:128],
                                       z5t[:, 1024 * sl + 128 * c:1024 * sl + 128 * (c + 1)], identT[:])

                def trev(a, bank, c=c, ti=ti):
                    return a.activation(z5T[:, 2048 * c + 128 * ti:2048 * c + 128 * (ti + 1)],
                                        bank[:], AF.Identity)
                w = []
                if c == 0:
                    w = [("V", mV_ln)]
                    if ti == 0:
                        w.append(("D", mD_small))
                mT_tr, mA_tr = grp("X", trmm, "A", trev, waits=w)
            mT_trs[sl] = mT_tr

        run("A", lambda a: a.preload_act_table(AF.Exp))
        run("A", lambda a: a.drain())
        mO_cw = 0
        for ti in range(16):
            def rmm(t, bank, ti=ti):
                inst = None
                for c in range(8):
                    inst = t.matmul(bank[:, 0:16], z5T[:, 2048 * c + 128 * ti:2048 * c + 128 * (ti + 1)],
                                    rwb[:, 16 * c:16 * (c + 1)], start=(c == 0), stop=(c == 7))
                return inst

            def rev(v, bank, ti=ti):
                return v.tensor_tensor(lg[:], bank[:, 0:16], rbb[:], ALU.add)
            w = [("A", mA_tr), ("G", mG_rw)] if ti == 0 else []
            _, mV_lg = grp("R", rmm, "V", rev, waits=w)

            def sfm1(v):
                v.tensor_reduce(mx[:], lg[:], AX, ALU.max)
                v.drain()
                v.tensor_scalar(nmx[:], mx[:], -1.0, None, ALU.mult)
                return v.drain()
            mV_s1 = sig("V", sfm1)
            wt("A", "V", mV_s1)
            mA_e = sig("A", lambda a: a.activation(lg[:], lg[:], AF.Exp, bias=nmx[:, 0:1]))
            wt("V", "A", mA_e)

            def sfm2(v, ti=ti):
                v.tensor_reduce(ssum[:], lg[:], AX, ALU.add)
                v.drain()
                v.reciprocal(ssum[:], ssum[:])
                v.drain()
                v.tensor_scalar(pr[:], lg[:], ssum[:, 0:1], None, ALU.mult)
                v.drain()
                v.tensor_reduce(mx[:], pr[:], AX, ALU.max)
                v.drain()
                v.tensor_scalar(ohx[:], pr[:], mx[:, 0:1], None, ALU.is_ge)
                v.drain()
                v.tensor_tensor(scr[:, 0:16], ohx[:], pr[:], ALU.mult)
                v.drain()
                v.tensor_tensor(prm[:], pr[:], scr[:, 0:16], ALU.subtract)
                v.drain()
                v.tensor_reduce(mx[:], prm[:], AX, ALU.max)
                v.drain()
                v.tensor_scalar(scr[:, 16:32], prm[:], mx[:, 0:1], None, ALU.is_ge)
                v.drain()
                v.tensor_tensor(ohx[:], ohx[:], scr[:, 16:32], ALU.add)
                v.tensor_tensor(scr[:, 0:16], scr[:, 16:32], prm[:], ALU.mult)
                v.drain()
                v.tensor_tensor(prm[:], prm[:], scr[:, 0:16], ALU.subtract)
                v.drain()
                v.tensor_reduce(mx[:], prm[:], AX, ALU.max)
                v.drain()
                v.tensor_scalar(scr[:, 16:32], prm[:], mx[:, 0:1], None, ALU.is_ge)
                v.drain()
                v.tensor_tensor(ohx[:], ohx[:], scr[:, 16:32], ALU.add)
                v.drain()
                v.tensor_tensor(cwA[:, 16 * ti:16 * (ti + 1)], ohx[:], pr[:], ALU.mult)
                v.drain()
                v.memset(cwA[:, 16 * ti + 15:16 * ti + 16], 1.0)
                return v.drain()
            mV_cw = sig("V", sfm2)

        mT_e_last = 0
        m_f1 = [0, 0]
        for ex in range(NEXP):
            esl = ex % 2
            ew1 = ew1b[esl]
            if m_f1[esl]:
                wt("G", "T", m_f1[esl])
            mG_w1 = sig("G", lambda gp, ex=ex, ew1=ew1: gp.dma_start(
                out=ew1[:].rearrange("p (c j) -> p c j", c=8),
                in_=eW1s[C * ex:C * (ex + 1), :].rearrange("(c p) j -> p c j", p=128)))
            if ex > 0:
                wt("G", "T", mT_e_last)
            mG_w2 = sig("G", lambda gp, ex=ex: gp.dma_start(
                out=ew2[:].rearrange("p (c j) -> p c j", c=8),
                in_=eW2s[C * ex:C * (ex + 1), :].rearrange("(c p) j -> p c j", p=128)))
            mG_b2 = sig("G", lambda gp, ex=ex: gp.dma_start(
                out=b2s[:], in_=eb2B[:, C * ex:C * (ex + 1)]))
            wt("V", "G", mG_b2)
            if ex > 0:
                wt("V", "T", mT_e_last)

            def b2scale(v):
                v.tensor_scalar(b2s[:], b2s[:], 1.0 / 128.0, None, ALU.mult)
                return v.drain()
            mV_b2 = sig("V", b2scale)

            mA_h = 0
            for oc in range(8):
                for tt in range(4):
                    def f1mm(t, bank, oc=oc, tt=tt, ew1=ew1):
                        inst = None
                        for c in range(8):
                            inst = t.matmul(bank[:, 0:512],
                                            ew1[:, 1024 * c + 128 * oc:1024 * c + 128 * (oc + 1)],
                                            z5T[:, 2048 * c + 512 * tt:2048 * c + 512 * (tt + 1)],
                                            start=(c == 0), stop=(c == 7))
                        return inst

                    def f1ev(a, bank, oc=oc, tt=tt, ex=ex):
                        return a.activation(hE[:, 2048 * oc + 512 * tt:2048 * oc + 512 * (tt + 1)],
                                            bank[:, 0:512], AF.Relu, bias=b1c[:, 8 * ex + oc:8 * ex + oc + 1])
                    w = []
                    if oc == 0 and tt == 0:
                        w = [("G", mG_w1)]
                    mT_f1x, mA_h = grp("S", f1mm, "A", f1ev, waits=w)
            m_f1[esl] = mT_f1x
            for st in range(16):
                for h in range(2):
                    def f2mm(t, bank, st=st, h=h):
                        for c in range(8):
                            t.matmul(bank[:, 0:512],
                                     hE[:, 2048 * c + 128 * st:2048 * c + 128 * (st + 1)],
                                     ew2[:, 1024 * c + 512 * h:1024 * c + 512 * (h + 1)],
                                     start=(c == 0), stop=False)
                        return t.matmul(bank[:, 0:512], onesPF[:],
                                        b2s[:, 512 * h:512 * (h + 1)], start=False, stop=True)

                    def f2ev(v, bank, st=st, h=h, ex=ex):
                        s = x4acc[:, 1024 * st + 512 * h:1024 * st + 512 * (h + 1)]
                        return v.scalar_tensor_tensor(s, bank[:, 0:512],
                                                      cwA[:, 16 * st + ex:16 * st + ex + 1],
                                                      s, ALU.mult, ALU.add)
                    w = []
                    if st == 0 and h == 0:
                        w = [("A", mA_h), ("V", mV_b2), ("G", mG_w2)]
                    mT_e_last, mV_x4 = grp("S", f2mm, "V", f2ev, waits=w)

        wt("G", "V", sc.tick["V"])
        for ti in range(16):
            sig("O", lambda gp, ti=ti: gp.dma_start(
                out=x4o[128 * ti:128 * (ti + 1), :], in_=x4acc[:, 1024 * ti:1024 * (ti + 1)]))

        wt("G", "O", sc.tick["O"])
        wt("G", "T", sc.tick["T"])
        wt("G", "A", sc.tick["A"])
        sig("G", lambda gp: gp.drain())
        sc.finalize(block)
    return nc


def run_pF(inputs, x2, ret):
    nc = build_pF()
    pre = lambda k: np.asarray(inputs["L1_ffn_" + k], np.float32)
    eW1 = np.concatenate([pre("eW1").reshape(15 * C, C), pre("sW1")], axis=0)
    eW2 = np.concatenate([pre("eW2").reshape(15 * C, C), pre("sW2")], axis=0)
    eb1 = np.concatenate([pre("eb1"), pre("sb1")[None]], axis=0)
    eb2 = np.concatenate([pre("eb2"), pre("sb2")[None]], axis=0)
    eb1C = np.concatenate([eb1[e].reshape(8, 128).T for e in range(NEXP)], axis=1)
    eb2B = np.concatenate([np.broadcast_to(eb2[e][None], (128, C)) for e in range(NEXP)], axis=1)
    rWp = np.zeros((C, 16), np.float32)
    rWp[:, 0:15] = pre("router_W")
    rbB = np.zeros((16,), np.float32)
    rbB[0:15] = pre("router_b")
    rbB[15] = -1e30
    rbB = np.ascontiguousarray(np.broadcast_to(rbB[None], (128, 16)))
    identI = np.ascontiguousarray(np.eye(128, dtype=bf))
    maps = []
    for c in range(NC):
        rows = slice(2048 * c, 2048 * (c + 1))
        maps.append({
            "x2I": np.ascontiguousarray(np.asarray(x2, np.float32)[rows]),
            "retI": np.ascontiguousarray(np.asarray(ret, np.float32)[rows]),
            "eW1s": eW1, "eW2s": eW2,
            "eb1C": np.ascontiguousarray(eb1C), "eb2B": np.ascontiguousarray(eb2B),
            "rWp": rWp, "rbB": rbB, "identI": identI,
        })
    res = _run(nc, maps)
    return np.concatenate([np.asarray(r["x4o"]) for r in res.results], 0)


def kernel(**inputs):
    xp, z2 = run_pA(inputs)
    og = run_pB(inputs, z2)
    xres, z3, zsum = run_pC(inputs, xp, og)
    ohs = run_pD2(inputs, zsum)
    experts = [int(np.argmax(np.asarray(oh))) for oh in ohs]
    x2, QT, KT, V = run_pD(inputs, xres, z3, experts)
    ret = run_pE(QT, KT, V)
    x4 = run_pF(inputs, x2, ret)
    return np.ascontiguousarray(x4.reshape(B, T, C).astype(np.float32))


# version 10
# speedup vs baseline: 1.1462x; 1.0032x over previous
import math
import numpy as np
import ml_dtypes
import concourse.bass as bass
import concourse.bacc as bacc
import concourse.mybir as mybir
from concourse.bass_utils import run_bass_kernel_spmd

F32 = mybir.dt.float32
BF16 = mybir.dt.bfloat16
AF = mybir.ActivationFunctionType
ALU = mybir.AluOpType
AX = mybir.AxisListType.X

B = 2
T = 8192
C = 1024
H = 16
N = 64
NC = 8
TPC = 2048
NT = TPC // 128
LNEPS = 1e-5
GNEPS = 0.00064
GAMMA = 0.96875
LOG_GAMMA = math.log(GAMMA)
CHUNK = 128
NCHUNK = T // CHUNK

bf = ml_dtypes.bfloat16


def _bass():
    return bass.Bass("TRN2", target_bir_lowering=False, detect_race_conditions=False)


def _run(nc, in_maps):
    mybir.codegen_inst_isa_subclasses(nc)
    return run_bass_kernel_spmd(nc, in_maps, core_ids=list(range(NC)))



class SemSeq:

    def __init__(self, stack, nc, name, inc, total):
        self.k = max(1, -(-(inc * total) // 240))
        self.sems = [stack.enter_context(nc.semaphore(f"{name}{j}")) for j in range(self.k)]
        self.inc = inc

    def attach(self, inst, i):
        inst.then_inc(self.sems[i % self.k], self.inc)
        return inst

    def wait(self, eng, i):
        if i < 0:
            return
        for j in range(self.k):
            cnt = (i - j) // self.k + 1 if i >= j else 0
            if cnt > 0:
                eng.wait_ge(self.sems[j], self.inc * cnt)


def emit_stats1024(v, xv, scr, sm, var, mean):
    v.tensor_tensor(scr, xv, xv, ALU.mult)
    v.reduce_sum(sm, xv, AX)
    v.drain()
    v.reduce_sum(var, scr, AX)
    v.tensor_scalar(mean, sm, 1.0 / 1024.0, None, ALU.mult)
    v.drain()
    v.tensor_scalar(var, var, 1.0 / 1024.0, None, ALU.mult)
    v.tensor_tensor(sm, mean, mean, ALU.mult)
    v.drain()
    v.tensor_tensor(var, var, sm, ALU.subtract)
    v.drain()


def emit_invsq(v, var, ur, vt):
    v.tensor_scalar(ur, var, float(LNEPS), None, ALU.add)
    v.drain()
    v.reciprocal(vt, ur)
    v.drain()
    v.tensor_tensor(vt, vt, var, ALU.mult)
    v.drain()
    v.tensor_scalar(vt, vt, float(LNEPS), None, ALU.add)
    v.drain()
    v.tensor_tensor(vt, vt, ur, ALU.mult)
    v.drain()
    v.reciprocal(ur, vt)
    v.drain()


def emit_inv(v, var, ur):
    v.tensor_scalar(ur, var, float(LNEPS), None, ALU.add)
    v.drain()
    v.reciprocal(ur, ur)
    v.drain()




def build_pA():
    nc = _bass()
    x = nc.dram_tensor("x", [TPC, C], F32, kind="ExternalInput").ap()
    lnpw = nc.dram_tensor("lnpw", [128, C], F32, kind="ExternalInput").ap()
    lnpb = nc.dram_tensor("lnpb", [128, C], F32, kind="ExternalInput").ap()
    xp = nc.dram_tensor("xp", [TPC, C], F32, kind="ExternalOutput").ap()
    z2 = nc.dram_tensor("z2", [TPC, C], BF16, kind="ExternalOutput").ap()

    from contextlib import ExitStack
    with ExitStack() as stack:
        e = stack.enter_context
        xt = e(nc.sbuf_tensor("xt", [128, 2 * C], F32))
        wt = e(nc.sbuf_tensor("wt", [128, C], F32))
        bt = e(nc.sbuf_tensor("bt", [128, C], F32))
        xpt = e(nc.sbuf_tensor("xpt", [128, 2 * C], F32))
        z2t = e(nc.sbuf_tensor("z2t", [128, 2 * C], BF16))
        scr = e(nc.sbuf_tensor("scr", [128, C], F32))
        sm = e(nc.sbuf_tensor("sm", [128, 1], F32))
        mean = e(nc.sbuf_tensor("mean", [128, 1], F32))
        var = e(nc.sbuf_tensor("var", [128, 1], F32))
        ur1 = e(nc.sbuf_tensor("ur1", [128, 1], F32))
        ur2 = e(nc.sbuf_tensor("ur2", [128, 1], F32))
        vt = e(nc.sbuf_tensor("vt", [128, 1], F32))
        sl = SemSeq(stack, nc, "sl", 16, NT)
        so = SemSeq(stack, nc, "so", 16, 2 * NT)
        sb = e(nc.semaphore("sb"))
        sv = e(nc.semaphore("sv"))
        sa = e(nc.semaphore("sa"))
        block = e(nc.Block())

        @block.sync
        def _(sync):
            for i in range(NT):
                if i >= 2:
                    sync.wait_ge(sv, 4 * (i - 2) + 3)
                sl.attach(sync.dma_start(out=xt[:, (i % 2) * C:(i % 2 + 1) * C],
                                         in_=x[128 * i:128 * (i + 1), :]), i)

        @block.vector
        def _(vector):
            for i in range(NT):
                sli = slice((i % 2) * C, (i % 2 + 1) * C)
                sl.wait(vector, i)
                if i == 0:
                    vector.wait_ge(sb, 32)
                if i >= 2:
                    so.wait(vector, 2 * (i - 2) + 1)
                xv = xt[:, sli]
                emit_stats1024(vector, xv, scr[:], sm[:], var[:], mean[:])
                emit_invsq(vector, var[:], ur1[:], vt[:])
                vector.engine_nop().then_inc(sv, 1)
                vector.wait_ge(sa, 2 * i + 1)
                vector.tensor_scalar(xpt[:, sli], xv, mean[:], ur1[:], ALU.subtract, ALU.mult)
                vector.drain()
                vector.tensor_tensor(xpt[:, sli], xpt[:, sli], wt[:], ALU.mult)
                vector.drain()
                vector.tensor_tensor(xpt[:, sli], xpt[:, sli], bt[:], ALU.add).then_inc(sv, 1)
                vector.drain()
                emit_stats1024(vector, xpt[:, sli], scr[:], sm[:], var[:], mean[:])
                emit_inv(vector, var[:], ur2[:])
                vector.engine_nop().then_inc(sv, 1)
                vector.wait_ge(sa, 2 * i + 2)
                vector.tensor_scalar(z2t[:, sli], xpt[:, sli], mean[:], ur2[:],
                                     ALU.subtract, ALU.mult).then_inc(sv, 1)

        @block.scalar
        def _(scalar):
            scalar.preload_act_table(AF.Sqrt)
            for i in range(NT):
                scalar.wait_ge(sv, 4 * i + 1)
                scalar.activation(ur1[:], ur1[:], AF.Sqrt).then_inc(sa, 1)
                scalar.wait_ge(sv, 4 * i + 3)
                scalar.activation(ur2[:], ur2[:], AF.Sqrt).then_inc(sa, 1)

        @block.gpsimd
        def _(gpsimd):
            gpsimd.dma_start(out=wt[:], in_=lnpw).then_inc(sb, 16)
            gpsimd.dma_start(out=bt[:], in_=lnpb).then_inc(sb, 16)
            for i in range(NT):
                gpsimd.wait_ge(sv, 4 * i + 2)
                so.attach(gpsimd.dma_start(out=xp[128 * i:128 * (i + 1), :],
                                           in_=xpt[:, (i % 2) * C:(i % 2 + 1) * C]), 2 * i)
                gpsimd.wait_ge(sv, 4 * i + 4)
                so.attach(gpsimd.dma_start(out=z2[128 * i:128 * (i + 1), :],
                                           in_=z2t[:, (i % 2) * C:(i % 2 + 1) * C]), 2 * i + 1)
            so.wait(gpsimd, 2 * NT - 1)
    return nc


def run_pA(inputs):
    nc = build_pA()
    x = np.asarray(inputs["L0_attn_x"], np.float32).reshape(B * T, C)
    lnpw = np.asarray(inputs["L0_attn_ln_pre_w"], np.float32)
    lnpb = np.asarray(inputs["L0_attn_ln_pre_b"], np.float32)
    lnpw = np.ascontiguousarray(np.broadcast_to(lnpw[None, :], (128, C)))
    lnpb = np.ascontiguousarray(np.broadcast_to(lnpb[None, :], (128, C)))
    in_maps = [{"x": np.ascontiguousarray(x[TPC * c:TPC * (c + 1)]),
                "lnpw": lnpw, "lnpb": lnpb} for c in range(NC)]
    res = _run(nc, in_maps)
    xp = np.concatenate([np.asarray(r["xp"]) for r in res.results], 0)
    z2 = np.concatenate([np.asarray(r["z2"]) for r in res.results], 0)
    return xp, z2




SL = 1024
NSTRIP = T // SL
CPS = SL // CHUNK
PB_STAGE = 7




def _finalize_pb(block, prog):
    @block.sync
    def _(sy):
        for f in prog["S"]:
            f(sy)

    @block.tensor
    def _(t):
        for f in prog["T"]:
            f(t)

    @block.vector
    def _(v):
        for f in prog["V"]:
            f(v)

    @block.scalar
    def _(a):
        for f in prog["A"]:
            f(a)

    @block.gpsimd
    def _(g):
        for f in prog["G"]:
            f(g)


def build_pB():
    nc = _bass()
    z2T = nc.dram_tensor("z2T", [C, T + 1], BF16, kind="ExternalInput").ap()
    wcat = nc.dram_tensor("wcat", [C, 1024], F32, kind="ExternalInput").ap()
    mixcat = nc.dram_tensor("mixcat", [128, 48], F32, kind="ExternalInput").ap()
    lnwb = nc.dram_tensor("lnwb", [128, 16], F32, kind="ExternalInput").ap()
    w2sI = nc.dram_tensor("w2sI", [64, 256], F32, kind="ExternalInput").ap()
    a2sI = nc.dram_tensor("a2sI", [64, 256], F32, kind="ExternalInput").ap()
    g2sI = nc.dram_tensor("g2sI", [128, 256], F32, kind="ExternalInput").ap()
    smallc = nc.dram_tensor("smallc", [128, 10], F32, kind="ExternalInput").ap()
    gnwb = nc.dram_tensor("gnwb", [128, 512], F32, kind="ExternalInput").ap()
    og = nc.dram_tensor("og", [T, 256], BF16, kind="ExternalOutput").ap()
    vbx = nc.dram_tensor("vbx", [1, 256], F32, kind="Internal").ap()

    SLH = SL // 2
    NTT = SL // 128
    from contextlib import ExitStack
    with ExitStack() as stack:
        e = stack.enter_context
        mixc = e(nc.sbuf_tensor("mixc", [128, 48], F32))
        lnt = e(nc.sbuf_tensor("lnt", [128, 16], F32))
        smt = e(nc.sbuf_tensor("smt", [128, 10], F32))
        onemka = e(nc.sbuf_tensor("onemka", [128, 2], F32))
        gnt = e(nc.sbuf_tensor("gnt", [128, 512], F32))
        sab = e(nc.sbuf_tensor("sab", [128, 96], F32))
        w2t = e(nc.sbuf_tensor("w2t", [64, 256], BF16))
        a2t = e(nc.sbuf_tensor("a2t", [128, 256], BF16))
        g2t = e(nc.sbuf_tensor("g2t", [128, 256], BF16))
        biasc = e(nc.sbuf_tensor("biasc", [128, 6], F32))
        vbrow = e(nc.sbuf_tensor("vbrow", [1, 256], F32))
        vbrowB = e(nc.sbuf_tensor("vbrowB", [128, 256], F32))
        vbrowS = e(nc.sbuf_tensor("vbrowS", [128, 256], BF16))
        lnbbB = e(nc.sbuf_tensor("lnbbB", [128, 8 * 128], BF16))
        onesF = e(nc.sbuf_tensor("onesF", [128, 128], F32))
        onesB = e(nc.sbuf_tensor("onesB", [128, 128], BF16))
        blkones = e(nc.sbuf_tensor("blkones", [128, 128], BF16))
        ident = e(nc.sbuf_tensor("ident", [128, 128], BF16))
        mlt = e(nc.sbuf_tensor("mlt", [128, 128], F32))
        mgt = e(nc.sbuf_tensor("mgt", [128, 128], F32))
        mle = e(nc.sbuf_tensor("mle", [128, 128], F32))
        ind64 = e(nc.sbuf_tensor("ind64", [128, 64], BF16))
        wtmpb = e(nc.sbuf_tensor("wtmpb", [128, 8 * 1024], BF16))
        wzs = e(nc.sbuf_tensor("wzs", [128, 8 * 1024], BF16))
        wz = wtmpb
        s0f = e(nc.sbuf_tensor("s0f", [128, 128], F32))
        s0b = e(nc.sbuf_tensor("s0b", [128, 128], BF16))
        sTmp = e(nc.sbuf_tensor("sTmp", [128, 128], F32))
        zst = e(nc.sbuf_tensor("zst", [128, 8 * (SL + 1)], BF16))
        rS = e(nc.sbuf_tensor("rS", [128, 2 * SL], F32))
        kS = e(nc.sbuf_tensor("kS", [128, 2 * SL], F32))
        dsig = e(nc.sbuf_tensor("dsig", [128, 2 * SL], F32))
        asig = e(nc.sbuf_tensor("asig", [128, 2 * SL], F32))
        zwza = e(nc.sbuf_tensor("zwza", [128, SL], BF16))
        zg = e(nc.sbuf_tensor("zg", [128, SL], BF16))
        vS = e(nc.sbuf_tensor("vS", [128, 8 * 256], BF16))
        gS = e(nc.sbuf_tensor("gS", [128, 8 * 256], BF16))
        ca = e(nc.sbuf_tensor("ca", [128, 2 * SL], F32))
        cprev = e(nc.sbuf_tensor("cprev", [128, 2 * SL], F32))
        cinv = e(nc.sbuf_tensor("cinv", [128, 2 * SL], F32))
        SCR1 = e(nc.sbuf_tensor("SCR1", [128, 2 * SL], F32))
        SCR2 = e(nc.sbuf_tensor("SCR2", [128, 2 * SL], F32))
        SCB1 = e(nc.sbuf_tensor("SCB1", [128, 2 * SL], BF16))
        atil = e(nc.sbuf_tensor("atil", [128, 2 * SL], BF16))
        btil = e(nc.sbuf_tensor("btil", [128, 2 * SL], BF16))
        ktil = e(nc.sbuf_tensor("ktil", [128, 2 * SL], BF16))
        rtil = e(nc.sbuf_tensor("rtil", [128, 2 * SL], BF16))
        rkpb = e(nc.sbuf_tensor("rkpb", [128, 2 * SL], BF16))
        ogS = e(nc.sbuf_tensor("ogS", [128, 8 * 256], BF16))
        mtS = e(nc.sbuf_tensor("mtS", [128, 512], BF16))
        mS = e(nc.sbuf_tensor("mS", [128, 512], BF16))
        gtS = e(nc.sbuf_tensor("gtS", [128, 512], BF16))
        d1S = e(nc.sbuf_tensor("d1S", [128, 512], BF16))
        d2S = e(nc.sbuf_tensor("d2S", [128, 512], BF16))
        pwT = [e(nc.sbuf_tensor(f"pwT{j}", [128, 512], BF16)) for j in range(2)]
        pwU = [e(nc.sbuf_tensor(f"pwU{j}", [128, 512], BF16)) for j in range(2)]
        uB = e(nc.sbuf_tensor("uB", [128, 256], BF16))
        bkT = e(nc.sbuf_tensor("bkT", [128, 512], BF16))
        gnm = e(nc.sbuf_tensor("gnm", [128, 4], F32))
        gns = e(nc.sbuf_tensor("gns", [128, 4], F32))
        gnr = e(nc.sbuf_tensor("gnr", [128, 4], F32))
        rksum = e(nc.sbuf_tensor("rksum", [128, 4], F32))
        ogchunk = e(nc.sbuf_tensor("ogchunk", [128, 256], F32))
        oq = e(nc.sbuf_tensor("oq", [128, 256], F32))
        pbS = [e(nc.psum_tensor(f"pbS{j}", [128, 512], F32)) for j in range(2)]
        pbV = [e(nc.psum_tensor("pbV0", [128, 256], F32))]
        pbP = [e(nc.psum_tensor(f"pbP{j}", [128, 128], F32)) for j in range(2)]
        pbQ = [e(nc.psum_tensor(f"pbQ{j}", [128, 64], F32)) for j in range(2)]
        pbX = [e(nc.psum_tensor("pbX0", [128, 128], BF16))]

        sqD = SemSeq(stack, nc, "sqD", 16, NSTRIP + 8)
        sqO = SemSeq(stack, nc, "sqO", 16, NSTRIP + 2)
        sqT = SemSeq(stack, nc, "sqT", 1, 8500)
        sqV = SemSeq(stack, nc, "sqV", 1, 6500)
        sqA = SemSeq(stack, nc, "sqA", 1, 3500)
        sqG = SemSeq(stack, nc, "sqG", 16, 12)
        sqC = SemSeq(stack, nc, "sqC", 1, 8)
        block = e(nc.Block())

        tick = {"T": 0, "V": 0, "A": 0, "G": 0, "D": 0, "O": 0, "C": 0}
        seqs = {"T": sqT, "V": sqV, "A": sqA, "G": sqG, "D": sqD, "O": sqO, "C": sqC}
        prog = {"S": [], "T": [], "V": [], "A": [], "G": []}

        def emit(engn, fn):
            prog[engn].append(fn)

        def sig(stream, fn):
            i = tick[stream]
            tick[stream] += 1
            engn = {"D": "S", "O": "G", "C": "G"}.get(stream, stream)
            sq = seqs[stream]

            def g(eng, fn=fn, i=i, sq=sq):
                sq.attach(fn(eng), i)
            emit(engn, g)
            return i + 1

        def wt(engn, stream, count):
            if count <= 0:
                return
            i = count - 1
            sq = seqs[stream]
            if stream in ("D", "O", "G"):
                def g(eng, sq=sq, i=i):
                    sq.wait(eng, i)
            else:
                def g(eng, sq=sq, i=i):
                    eng.wait_ge(sq.sems[i % sq.k], sq.inc * (i // sq.k + 1))
            emit(engn, g)

        def run(engn, fn):
            emit(engn, fn)

        pools = {"S": pbS, "V": pbV, "P": pbP, "Q": pbQ, "X": pbX}
        rot = {k: 0 for k in pools}
        blast = {}

        def grp(pool, mmfn, estream, efn, waits=()):
            bank = pools[pool][rot[pool] % len(pools[pool])]
            rot[pool] += 1
            bl = blast.get(bank.name)
            if bl:
                wt("T", bl[0], bl[1])
            for (stm, m) in waits:
                wt("T", stm, m)
            mT = sig("T", lambda t, mmfn=mmfn, bank=bank: mmfn(t, bank))
            wt(estream, "T", mT)
            mE = sig(estream, lambda v, efn=efn, bank=bank: efn(v, bank))
            blast[bank.name] = (estream, mE)
            return mT, mE

        def grp2(pool, mmfns, estream, efns, waits=()):
            mT = mE = 0
            for i, (f, ef) in enumerate(zip(mmfns, efns)):
                mT, mE = grp(pool, f, estream, ef, waits if i == 0 else ())
            return mT, mE

        mD = 0
        for dst, src in ((mixc, mixcat), (lnt, lnwb), (smt, smallc), (gnt, gnwb)):
            mD = sig("D", lambda s, dst=dst, src=src: s.dma_start(out=dst[:], in_=src))
        mD_small = mD
        mG = sig("G", lambda g: g.dma_start(
            out=wtmpb[:].rearrange("p (c j) -> p c j", c=8),
            in_=wcat.rearrange("(c p) j -> p c j", p=128)))
        mG = sig("G", lambda g: g.dma_start(out=w2t[:], in_=w2sI))
        mG = sig("G", lambda g: g.dma_start(out=a2t[64:128, :], in_=a2sI))
        mG = sig("G", lambda g: g.dma_start(out=g2t[:], in_=g2sI))
        mG_w = mG

        run("A", lambda a: a.preload_act_table(AF.Sigmoid))
        run("A", lambda a: a.drain())

        if PB_STAGE < 0.4:
            wt("G", "D", mD_small)
            sig("C", lambda g: g.drain())
            _finalize_pb(block, prog)
            return nc

        def const_build(v):
            v.memset(onesF[:], 1.0)
            v.memset(onesB[:], 1.0)
            v.memset(blkones[:], 0.0)
            v.memset(ind64[:], 0.0)
            v.memset(s0f[:], 0.0)
            v.memset(s0b[:], 0.0)
            v.drain()
            v.memset(blkones[0:64, 0:64], 1.0)
            v.memset(blkones[64:128, 64:128], 1.0)
            v.memset(ind64[0:64, 0:1], 1.0)
            v.memset(ind64[64:128, 1:2], 1.0)
            return v.drain()
        mV_const = sig("V", const_build)

        if PB_STAGE < 0.6:
            wt("G", "V", mV_const)
            sig("C", lambda g: g.drain())
            _finalize_pb(block, prog)
            return nc

        wt("G", "V", mV_const)

        def mask_build(g):
            g.affine_select(mlt[:], onesF[:], [[1, 128]], ALU.is_ge, 0.0,
                            base=-1, channel_multiplier=-1)
            g.affine_select(mgt[:], onesF[:], [[-1, 128]], ALU.is_ge, 0.0,
                            base=-1, channel_multiplier=1)
            g.affine_select(mle[:], onesF[:], [[1, 128]], ALU.is_ge, 0.0,
                            base=0, channel_multiplier=-1)
            g.affine_select(ident[:], onesB[:], [[-1, 128]], ALU.is_ge, 0.0,
                            base=0, channel_multiplier=1)
            g.drain()
            g.affine_select(ident[:], ident[:], [[1, 128]], ALU.is_ge, 0.0,
                            base=0, channel_multiplier=-1)
            return g.drain()
        mC_mask = sig("C", mask_build)

        if PB_STAGE < 0.8:
            wt("G", "C", mC_mask)
            sig("C", lambda g: g.drain())
            _finalize_pb(block, prog)
            return nc

        wt("V", "D", mD_small)
        wt("V", "G", mG_w)

        def wprep(v):
            for m in range(6):
                for c in range(8):
                    col = (m * 8 + c) * 2
                    v.tensor_tensor(sab[:, col + 1:col + 2], lnt[:, c:c + 1],
                                    mixc[:, m * 8 + c:m * 8 + c + 1], ALU.mult)
            v.drain()
            for m in range(6):
                for c in range(8):
                    col = (m * 8 + c) * 2
                    v.tensor_tensor(sab[:, col:col + 1], lnt[:, c:c + 1],
                                    sab[:, col + 1:col + 2], ALU.subtract)
            v.tensor_scalar(onemka[:], smt[:, 6:8], -1.0, 1.0, ALU.mult, ALU.add)
            for c in range(8):
                v.tensor_scalar(lnbbB[:, 128 * c:128 * (c + 1)], onesF[:],
                                lnt[:, 8 + c:9 + c], None, ALU.mult)
            return v.drain()
        mV_wprep = sig("V", wprep)

        if PB_STAGE < 0.9:
            wt("G", "V", mV_wprep)
            sig("C", lambda g: g.drain())
            _finalize_pb(block, prog)
            return nc

        def bias_mm(oc):
            def f(t, bank, oc=oc):
                inst = None
                for c in range(8):
                    inst = t.matmul(bank[:, 0:64],
                                    wtmpb[:, c * 1024 + 128 * oc:c * 1024 + 128 * (oc + 1)],
                                    lnbbB[:, 128 * c:128 * c + 64],
                                    start=(c == 0), stop=(c == 7))
                return inst
            return f

        def bias_ev(oc):
            def f(a, bank, oc=oc):
                return a.activation(biasc[:, oc:oc + 1], bank[:, 0:1], AF.Identity)
            return f

        mT_bias = 0
        for oc in range(6):
            w = [("V", mV_wprep)] if oc == 0 else []
            mT_bias, mA_bias = grp("Q", bias_mm(oc), "A", bias_ev(oc), waits=w)

        def vb_mm(t, bank):
            inst = None
            for c in range(8):
                inst = t.matmul(bank[:, 0:256], lnbbB[:, 128 * c:128 * (c + 1)],
                                wtmpb[:, c * 1024 + 768:c * 1024 + 1024],
                                start=(c == 0), stop=(c == 7))
            return inst

        def vb_ev(v, bank):
            v.tensor_copy(vbrow[:], bank[0:1, 0:256])
            return v.drain()
        mT_vb, mV_vb = grp("V", vb_mm, "V", vb_ev)

        wt("G", "V", mV_vb)
        mO_vb = sig("O", lambda g: g.dma_start(out=vbx, in_=vbrow[:]))
        wt("G", "O", mO_vb)
        mG_vbB = sig("G", lambda g: g.dma_start(out=vbrowB[:], in_=vbx.partition_broadcast(128)))
        wt("V", "G", mG_vbB)
        wt("V", "T", mT_vb)

        def wscale(v):
            regions = [(0, 256, 0), (256, 512, 1), (512, 576, 2), (576, 640, 3),
                       (640, 768, 4), (768, 1024, 5)]
            v.tensor_scalar(vbrowS[:], vbrowB[:], 1.0 / 128.0, None, ALU.mult)
            for c in range(8):
                base = c * 1024
                for lo, hi, m in regions:
                    col = (m * 8 + c) * 2
                    v.tensor_scalar(wzs[:, base + lo:base + hi], wtmpb[:, base + lo:base + hi],
                                    sab[:, col + 1:col + 2], None, ALU.mult)
            v.drain()
            for c in range(8):
                base = c * 1024
                for lo, hi, m in regions:
                    col = (m * 8 + c) * 2
                    v.tensor_scalar(wtmpb[:, base + lo:base + hi], wtmpb[:, base + lo:base + hi],
                                    sab[:, col:col + 1], None, ALU.mult)
            return v.drain()
        mV_wz = sig("V", wscale)

        if PB_STAGE < 0.99:
            wt("G", "V", mV_wz)
            sig("C", lambda g: g.drain())
            _finalize_pb(block, prog)
            return nc

        OCD = [0, 128, 256, 384, 512, 640]
        mD_z = [0] * NSTRIP
        zdone = [0] * NSTRIP
        ogall = [0] * NSTRIP
        m_chT_last = 0
        mSb = mV_const
        m_v7 = 0
        for s in range(NSTRIP if PB_STAGE >= 1 else 0):
            t0 = SL * s
            if s > 0:
                wt("S", "T", zdone[s - 1])
            mD_z[s] = sig("D", lambda sy, t0=t0: sy.dma_start(
                out=zst[:].rearrange("p (c j) -> p c j", c=8),
                in_=z2T.rearrange("(c p) t -> p c t", p=128)[:, :, t0:t0 + SL + 1]))

            if s > 0:
                run("A", lambda a: a.preload_act_table(AF.Sigmoid))
                run("A", lambda a: a.drain())

            ev_zwza = ev_zg = 0
            for oc in range(6):
                for nt in range(2):
                    off = OCD[oc]

                    def s1mm(t, bank, off=off, nt=nt):
                        for c in range(8):
                            t.matmul(bank[:, 0:SLH], wz[:, c * 1024 + off:c * 1024 + off + 128],
                                     zst[:, c * (SL + 1) + 1 + SLH * nt:c * (SL + 1) + 1 + SLH * nt + SLH],
                                     start=(c == 0), stop=False)
                        inst = None
                        for c in range(8):
                            inst = t.matmul(bank[:, 0:SLH], wzs[:, c * 1024 + off:c * 1024 + off + 128],
                                            zst[:, c * (SL + 1) + SLH * nt:c * (SL + 1) + SLH * nt + SLH],
                                            start=False, stop=(c == 7))
                        return inst

                    if oc < 2:
                        def s1ev(a, bank, oc=oc, nt=nt):
                            return a.activation(rS[:, oc * SL + SLH * nt:oc * SL + SLH * nt + SLH],
                                                bank[:, 0:SLH], AF.Identity, bias=biasc[:, oc:oc + 1])
                    elif oc < 4:
                        def s1ev(a, bank, oc=oc, nt=nt):
                            return a.activation(kS[:, (oc - 2) * SL + SLH * nt:(oc - 2) * SL + SLH * nt + SLH],
                                                bank[:, 0:SLH], AF.Identity, bias=biasc[:, oc:oc + 1])
                    elif oc == 4:
                        def s1ev(a, bank, nt=nt):
                            a.activation(zwza[0:64, SLH * nt:SLH * nt + SLH], bank[0:64, 0:SLH],
                                         AF.Tanh, bias=biasc[0:64, 4:5])
                            return a.activation(zwza[64:128, SLH * nt:SLH * nt + SLH], bank[64:128, 0:SLH],
                                                AF.Identity, bias=biasc[64:128, 4:5])
                    else:
                        def s1ev(a, bank, nt=nt):
                            return a.activation(zg[:, SLH * nt:SLH * nt + SLH], bank[:, 0:SLH],
                                                AF.Sigmoid, bias=biasc[:, 5:6])
                    w = []
                    if oc == 0 and nt == 0:
                        w = [("D", mD_z[s]), ("V", mV_wz)]
                        if s > 0:
                            w.append(("V", ogall[s - 1]))
                    mT1g, mE1g = grp("S", s1mm, "A", s1ev, waits=w)
                    if oc == 4 and nt == 1:
                        ev_zwza = mE1g
                    if oc == 5 and nt == 1:
                        ev_zg = mE1g
            for tt in range(NTT):
                def vmm(t, bank, tt=tt):
                    for c in range(8):
                        t.matmul(bank[:, 0:256], zst[:, c * (SL + 1) + 1 + 128 * tt:c * (SL + 1) + 1 + 128 * tt + 128],
                                 wz[:, c * 1024 + 768:c * 1024 + 1024], start=(c == 0), stop=False)
                    for c in range(8):
                        t.matmul(bank[:, 0:256], zst[:, c * (SL + 1) + 128 * tt:c * (SL + 1) + 128 * tt + 128],
                                 wzs[:, c * 1024 + 768:c * 1024 + 1024], start=False, stop=False)
                    return t.matmul(bank[:, 0:256], onesB[:], vbrowS[:], start=False, stop=True)

                def vev(a, bank, tt=tt):
                    return a.activation(vS[:, 256 * tt:256 * (tt + 1)], bank[:, 0:256], AF.Identity)
                mTv, _ = grp("V", vmm, "A", vev)
                if tt == NTT - 1:
                    zdone[s] = mTv
            for nt in range(2):
                for mh in range(2):
                    def w2mm(t, bank, nt=nt, mh=mh):
                        return t.matmul(bank[:, 0:SLH], w2t[:, 128 * mh:128 * (mh + 1)],
                                        zwza[0:64, SLH * nt:SLH * (nt + 1)], start=True, stop=True)

                    def w2ev(a, bank, nt=nt, mh=mh):
                        return a.activation(dsig[:, mh * SL + SLH * nt:mh * SL + SLH * nt + SLH],
                                            bank[:, 0:SLH], AF.Sigmoid, bias=smt[:, mh:mh + 1])
                    w = [("A", ev_zwza)] if (nt == 0 and mh == 0) else []
                    grp("S", w2mm, "A", w2ev, waits=w)
            for nt in range(2):
                for mh in range(2):
                    def a2mm(t, bank, nt=nt, mh=mh):
                        return t.matmul(bank[:, 0:SLH], a2t[64:128, 128 * mh:128 * (mh + 1)],
                                        zwza[64:128, SLH * nt:SLH * (nt + 1)], start=True, stop=True)

                    def a2ev(a, bank, nt=nt, mh=mh):
                        return a.activation(asig[:, mh * SL + SLH * nt:mh * SL + SLH * nt + SLH],
                                            bank[:, 0:SLH], AF.Sigmoid, bias=smt[:, 2 + mh:3 + mh])
                    grp("S", a2mm, "A", a2ev)
            for tt in range(NTT):
                def gmm(t, bank, tt=tt):
                    return t.matmul(bank[:, 0:256], zg[:, 128 * tt:128 * (tt + 1)],
                                    g2t[:], start=True, stop=True)

                def gev(a, bank, tt=tt):
                    return a.activation(gS[:, 256 * tt:256 * (tt + 1)], bank[:, 0:256], AF.Identity)
                w = [("A", ev_zg)] if tt == 0 else []
                grp("V", gmm, "A", gev, waits=w)

            if PB_STAGE < 2:
                continue
            run("A", lambda a: a.preload_act_table(AF.Exp))
            run("A", lambda a: a.drain())

            def p2_act(a):
                a.drain()
                return a.activation(SCR1[:], dsig[:], AF.Exp, scale=-0.60653066)
            mA_d = sig("A", p2_act)
            run("A", lambda a: a.preload_act_table(AF.Sqrt))
            run("A", lambda a: a.drain())

            wt("V", "A", mA_d)
            if s > 0:
                wt("V", "T", m_chT_last)

            def p2_v1(v):
                bufs = [SCR1, ca]
                cur = 0
                for k in (1, 2, 4, 8, 16, 32, 64):
                    a4 = bufs[cur][:].rearrange("p (t ch i) -> p t ch i", t=2, i=CHUNK)
                    b4 = bufs[1 - cur][:].rearrange("p (t ch i) -> p t ch i", t=2, i=CHUNK)
                    v.tensor_copy(b4[:, :, :, 0:k], a4[:, :, :, 0:k])
                    v.tensor_tensor(b4[:, :, :, k:CHUNK], a4[:, :, :, k:CHUNK],
                                    a4[:, :, :, 0:CHUNK - k], ALU.mult)
                    v.drain()
                    cur = 1 - cur
                a4 = ca[:].rearrange("p (t ch i) -> p t ch i", t=2, i=CHUNK)
                c4 = cprev[:].rearrange("p (t ch i) -> p t ch i", t=2, i=CHUNK)
                v.tensor_copy(c4[:, :, :, 1:CHUNK], a4[:, :, :, 0:CHUNK - 1])
                v.memset(c4[:, :, :, 0:1], 1.0)
                v.reciprocal(cinv[:], ca[:])
                v.drain()
                for ti in range(2):
                    v.tensor_scalar(SCR2[:, ti * SL:(ti + 1) * SL], kS[:, ti * SL:(ti + 1) * SL],
                                    smt[:, 4 + ti:5 + ti], None, ALU.mult)
                v.drain()
                v.tensor_tensor(SCB1[:], SCR2[:], SCR2[:], ALU.mult)
                return v.drain()
            mV_v1 = sig("V", p2_v1)

            for ti in range(2):
                for nt in range(2):
                    def ssmm(t, bank, ti=ti, nt=nt):
                        return t.matmul(bank[:, 0:SLH], blkones[:],
                                        SCB1[:, ti * SL + SLH * nt:ti * SL + SLH * nt + SLH],
                                        start=True, stop=True)

                    def ssev(a, bank, ti=ti, nt=nt):
                        return a.activation(SCR1[:, ti * SL + SLH * nt:ti * SL + SLH * nt + SLH],
                                            bank[:, 0:SLH], AF.Sqrt)
                    w = [("V", mV_v1)] if (ti == 0 and nt == 0) else []
                    _, mA_ss = grp("S", ssmm, "A", ssev, waits=w)

            wt("V", "A", mA_ss)

            def p2_v2(v):
                v.tensor_scalar(SCR1[:], SCR1[:], 1e-12, None, ALU.max)
                v.drain()
                v.reciprocal(SCR1[:], SCR1[:])
                v.drain()
                v.tensor_tensor(SCR2[:], SCR2[:], SCR1[:], ALU.mult)
                v.drain()
                v.scalar_tensor_tensor(atil[:], SCR2[:], -1.0, cprev[:], ALU.mult, ALU.mult)
                v.tensor_tensor(SCR1[:], SCR2[:], asig[:], ALU.mult)
                v.drain()
                v.tensor_tensor(btil[:], SCR1[:], cinv[:], ALU.mult)
                for ti in range(2):
                    v.tensor_scalar(SCR1[:, ti * SL:(ti + 1) * SL], asig[:, ti * SL:(ti + 1) * SL],
                                    smt[:, 6 + ti:7 + ti], onemka[:, ti:ti + 1], ALU.mult, ALU.add)
                v.drain()
                v.tensor_tensor(SCR1[:], kS[:], SCR1[:], ALU.mult)
                v.drain()
                v.tensor_tensor(ktil[:], SCR1[:], cinv[:], ALU.mult)
                v.tensor_tensor(SCR2[:], rS[:], SCR1[:], ALU.mult)
                v.drain()
                for ti in range(2):
                    v.tensor_scalar(rkpb[:, ti * SL:(ti + 1) * SL], SCR2[:, ti * SL:(ti + 1) * SL],
                                    smt[:, 8 + ti:9 + ti], None, ALU.mult)
                v.tensor_tensor(rtil[:], rS[:], ca[:], ALU.mult)
                return v.drain()
            mV_prep = sig("V", p2_v2)

            t1_done = {}
            for ch in range(CPS if PB_STAGE >= 3 else 0):
                def fsl(buf, p, ch=ch):
                    r0 = 64 * (p % 2)
                    cc = (p // 2) * SL + ch * CHUNK
                    return buf[r0:r0 + 64, cc:cc + CHUNK]

                def t1_bundles(ch, first_waits):
                    specs = [(btil, atil, mlt, mtS), (atil, btil, mgt, mS), (ktil, atil, mlt, gtS)]
                    out = []
                    for ki, (sb_, mv_, msk, dst) in enumerate(specs):
                        for p2_ in range(2):
                            mms, evs = [], []
                            for p in (2 * p2_, 2 * p2_ + 1):
                                def t1mm(t, bank, sb_=sb_, mv_=mv_, p=p, ch=ch, fsl=fsl):
                                    return t.matmul(bank[:, 0:128], fsl(sb_, p, ch), fsl(mv_, p, ch),
                                                    start=True, stop=True)

                                def t1ev(v, bank, msk=msk, dst=dst, p=p):
                                    return v.tensor_tensor(dst[:, 128 * p:128 * (p + 1)], bank[:], msk[:], ALU.mult)
                                mms.append(t1mm)
                                evs.append(t1ev)
                            w = first_waits if (ki == 0 and p2_ == 0) else []
                            out.append((mms, evs, w))
                    return out

                if ch in t1_done:
                    mE_gtS = t1_done.pop(ch)
                else:
                    w0 = [("V", mV_prep)]
                    if s == 0 and ch == 0:
                        w0.append(("C", mC_mask))
                    mE_gtS = 0
                    for mms, evs, w in t1_bundles(ch, w0):
                        _, mE_gtS = grp2("P", mms, "V", evs, waits=w)

                if PB_STAGE < 4:
                    continue
                mA_t2 = 0
                for p2_ in range(2):
                    mms, evs = [], []
                    for p in (2 * p2_, 2 * p2_ + 1):
                        def t2mm(t, bank, p=p, ch=ch, fsl=fsl):
                            r0 = 64 * (p % 2)
                            c0 = 64 * (p // 2)
                            t.matmul(bank[:, 0:64], fsl(atil, p, ch), s0b[r0:r0 + 64, c0:c0 + 64],
                                     start=True, stop=False)
                            return t.matmul(bank[:, 0:64], gtS[:, 128 * p:128 * (p + 1)],
                                            vS[:, ch * 256 + 64 * p:ch * 256 + 64 * (p + 1)],
                                            start=False, stop=True)

                        def t2ev(a, bank, p=p):
                            return a.activation(uB[:, 64 * p:64 * (p + 1)], bank[:, 0:64], AF.Identity)
                        mms.append(t2mm)
                        evs.append(t2ev)
                    w = [("V", mSb), ("V", mE_gtS)] if p2_ == 0 else []
                    _, mA_t2 = grp2("Q", mms, "A", evs, waits=w)

                fill_state = {"d2": 0, "tr": 0}
                fillers = []
                for ki, (sb_, dst) in enumerate([(btil, d1S), (ktil, d2S)]):
                    for p2_ in range(2):
                        mms, evs = [], []
                        for p in (2 * p2_, 2 * p2_ + 1):
                            def t3mm(t, bank, sb_=sb_, p=p, ch=ch, fsl=fsl):
                                return t.matmul(bank[:, 0:128], fsl(sb_, p, ch), fsl(rtil, p, ch),
                                                start=True, stop=True)

                            def t3ev(v, bank, dst=dst, p=p):
                                return v.tensor_tensor(dst[:, 128 * p:128 * (p + 1)], bank[:], mle[:], ALU.mult)
                            mms.append(t3mm)
                            evs.append(t3ev)

                        def emit_t3(mms=mms, evs=evs):
                            _, m = grp2("P", mms, "V", evs)
                            fill_state["d2"] = m
                        fillers.append(emit_t3)
                for ti in range(2):
                    for bi, src in enumerate((btil, ktil)):
                        cc = ti * SL + ch * CHUNK

                        def trmm(t, bank, src=src, cc=cc):
                            return t.transpose(bank[:, 0:128], src[:, cc:cc + CHUNK], ident[:])

                        def trev(a, bank, ti=ti, bi=bi):
                            return a.activation(bkT[:, 256 * ti + 128 * bi:256 * ti + 128 * bi + 128],
                                                bank[:], AF.Identity)

                        def emit_tr(trmm=trmm, trev=trev):
                            _, m = grp("X", trmm, "A", trev)
                            fill_state["tr"] = m
                        fillers.append(emit_tr)
                if ch + 1 < CPS and PB_STAGE >= 4:
                    for mms, evs, w in t1_bundles(ch + 1, []):
                        def emit_n1(mms=mms, evs=evs, w=w, nch=ch + 1):
                            _, m = grp2("P", mms, "V", evs, waits=w)
                            t1_done[nch] = m
                        fillers.append(emit_n1)

                cur_t, cur_u = mtS, mS
                mE_pwT_last = 0
                mE_pwU_last = 0
                mE_apply_last = 0
                for lvl in range(7):
                    for p2_ in range(2):
                        mms, evs = [], []
                        for p in (2 * p2_, 2 * p2_ + 1):
                            def apmm(t, bank, cur_t=cur_t, p=p):
                                return t.matmul(bank[:, 0:64], cur_t[:, 128 * p:128 * (p + 1)],
                                                uB[:, 64 * p:64 * (p + 1)], start=True, stop=True)

                            def apev(v, bank, p=p):
                                return v.tensor_tensor(uB[:, 64 * p:64 * (p + 1)], uB[:, 64 * p:64 * (p + 1)],
                                                       bank[:, 0:64], ALU.add)
                            mms.append(apmm)
                            evs.append(apev)
                        w = []
                        if p2_ == 0:
                            if lvl == 0:
                                w = [("A", mA_t2)]
                            else:
                                w = [("A", mE_pwT_last), ("V", mE_pwU_last)]
                        _, mE_apply_last = grp2("Q", mms, "V", evs, waits=w)
                    if lvl < 6:
                        nx_u, nx_t = pwU[lvl % 2], pwT[lvl % 2]
                        for p2_ in range(2):
                            umms, uevs, tmms, tevs = [], [], [], []
                            for p in (2 * p2_, 2 * p2_ + 1):
                                def pumm(t, bank, cur_t=cur_t, cur_u=cur_u, p=p):
                                    return t.matmul(bank[:, 0:128], cur_t[:, 128 * p:128 * (p + 1)],
                                                    cur_u[:, 128 * p:128 * (p + 1)], start=True, stop=True)

                                def puev(v, bank, nx_u=nx_u, p=p):
                                    return v.tensor_copy(nx_u[:, 128 * p:128 * (p + 1)], bank[:])

                                def ptmm(t, bank, cur_t=cur_t, cur_u=cur_u, p=p):
                                    return t.matmul(bank[:, 0:128], cur_u[:, 128 * p:128 * (p + 1)],
                                                    cur_t[:, 128 * p:128 * (p + 1)], start=True, stop=True)

                                def ptev(a, bank, nx_t=nx_t, p=p):
                                    return a.activation(nx_t[:, 128 * p:128 * (p + 1)], bank[:], AF.Identity)
                                umms.append(pumm)
                                uevs.append(puev)
                                tmms.append(ptmm)
                                tevs.append(ptev)
                            w = []
                            if p2_ == 0 and lvl > 0:
                                w = [("A", mE_pwT_last), ("V", mE_pwU_last)]
                            _, mE_pwU_last = grp2("P", umms, "V", uevs, waits=w)
                            _, mE_pwT_last = grp2("P", tmms, "A", tevs)
                        cur_t, cur_u = nx_t, nx_u
                    for _f in range(2 if lvl < 6 else 0):
                        if fillers:
                            fillers.pop(0)()

                if PB_STAGE < 5:
                    continue

                for f in fillers:
                    f()
                fillers = []
                mE_d2 = fill_state["d2"]
                mA_tr = fill_state["tr"]

                mA_t4 = 0
                for p2_ in range(2):
                    mms, evs = [], []
                    for p in (2 * p2_, 2 * p2_ + 1):
                        def t4mm(t, bank, p=p, ch=ch, fsl=fsl):
                            r0 = 64 * (p % 2)
                            c0 = 64 * (p // 2)
                            t.matmul(bank[:, 0:64], fsl(rtil, p, ch), s0b[r0:r0 + 64, c0:c0 + 64],
                                     start=True, stop=False)
                            t.matmul(bank[:, 0:64], d1S[:, 128 * p:128 * (p + 1)],
                                     uB[:, 64 * p:64 * (p + 1)], start=False, stop=False)
                            return t.matmul(bank[:, 0:64], d2S[:, 128 * p:128 * (p + 1)],
                                            vS[:, ch * 256 + 64 * p:ch * 256 + 64 * (p + 1)],
                                            start=False, stop=True)

                        def t4ev(a, bank, p=p):
                            return a.activation(ogchunk[:, 64 * p:64 * (p + 1)], bank[:, 0:64], AF.Identity)
                        mms.append(t4mm)
                        evs.append(t4ev)
                    w = []
                    if p2_ == 0:
                        w = [("V", mE_d2), ("V", mE_apply_last)]
                        if m_v7:
                            w.append(("V", m_v7))
                    _, mA_t4 = grp2("Q", mms, "A", evs, waits=w)


                if PB_STAGE < 6:
                    continue
                for g2 in range(2):
                    def t5mm(t, bank, g2=g2, ch=ch):
                        t.matmul(bank[:, 0:128], bkT[:, 256 * g2:256 * g2 + 128],
                                 uB[:, 128 * g2:128 * (g2 + 1)], start=True, stop=False)
                        return t.matmul(bank[:, 0:128], bkT[:, 256 * g2 + 128:256 * g2 + 256],
                                        vS[:, ch * 256 + 128 * g2:ch * 256 + 128 * (g2 + 1)],
                                        start=False, stop=True)

                    def t5ev(v, bank, g2=g2, ch=ch):
                        cc = g2 * SL + ch * CHUNK + CHUNK - 1
                        v.tensor_tensor(sTmp[0:64, 64 * g2:64 * g2 + 64],
                                        s0f[0:64, 64 * g2:64 * g2 + 64], bank[0:64, 0:64], ALU.add)
                        v.tensor_tensor(sTmp[64:128, 64 * g2:64 * g2 + 64],
                                        s0f[64:128, 64 * g2:64 * g2 + 64], bank[64:128, 64:128], ALU.add)
                        v.drain()
                        v.tensor_scalar(s0f[0:64, 64 * g2:64 * g2 + 64],
                                        sTmp[0:64, 64 * g2:64 * g2 + 64],
                                        ca[0:64, cc:cc + 1], None, ALU.mult)
                        v.tensor_scalar(s0f[64:128, 64 * g2:64 * g2 + 64],
                                        sTmp[64:128, 64 * g2:64 * g2 + 64],
                                        ca[64:128, cc:cc + 1], None, ALU.mult)
                        v.drain()
                        v.tensor_copy(s0b[:, 64 * g2:64 * g2 + 64], s0f[:, 64 * g2:64 * g2 + 64])
                        return v.drain()
                    w = [("A", mA_tr)] if g2 == 0 else []
                    _, mSb = grp("P", t5mm, "V", t5ev, waits=w)

                if PB_STAGE < 7:
                    continue
                mms, evs = [], []
                for ti in range(2):
                    def t6mm(t, bank, ti=ti, ch=ch):
                        return t.matmul(bank[:, 0:64],
                                        rkpb[:, ti * SL + ch * CHUNK:ti * SL + (ch + 1) * CHUNK],
                                        ind64[:], start=True, stop=True)

                    def t6ev(v, bank, ti=ti):
                        return v.tensor_copy(rksum[:, 2 * ti:2 * ti + 2], bank[:, 0:2])
                    mms.append(t6mm)
                    evs.append(t6ev)
                _, mE_rk = grp2("Q", mms, "V", evs)

                wt("V", "A", mA_t4)

                def v6(v):
                    o4m = ogchunk[:].rearrange("p (h i) -> p h i", h=4)
                    v.tensor_reduce(gnm[:], o4m, AX, ALU.add)
                    v.tensor_tensor(oq[:], ogchunk[:], ogchunk[:], ALU.mult)
                    v.drain()
                    o4 = oq[:].rearrange("p (h i) -> p h i", h=4)
                    v.tensor_reduce(gns[:], o4, AX, ALU.add)
                    v.tensor_scalar(gnm[:], gnm[:], 1.0 / 64.0, None, ALU.mult)
                    v.drain()
                    v.tensor_scalar(gnr[:], gns[:], 1.0 / 64.0, None, ALU.mult)
                    v.tensor_tensor(gns[:], gnm[:], gnm[:], ALU.mult)
                    v.drain()
                    v.tensor_tensor(gnr[:], gnr[:], gns[:], ALU.subtract)
                    v.drain()
                    v.tensor_scalar(gnr[:], gnr[:], float(GNEPS), None, ALU.add)
                    v.drain()
                    v.reciprocal(gnr[:], gnr[:])
                    return v.drain()
                mV6 = sig("V", v6)
                wt("A", "V", mV6)

                def a6(a):
                    return a.activation(gnr[:], gnr[:], AF.Sqrt)
                mA6 = sig("A", a6)
                wt("V", "A", mA6)
                if ch == 0 and s > 0:
                    wt("V", "O", s + 1)

                def v7(v, ch=ch):
                    for h in range(4):
                        v.tensor_scalar(ogchunk[:, 64 * h:64 * (h + 1)], ogchunk[:, 64 * h:64 * (h + 1)],
                                        gnm[:, h:h + 1], gnr[:, h:h + 1], ALU.subtract, ALU.mult)
                    v.drain()
                    v.tensor_tensor(ogchunk[:], ogchunk[:], gnt[:, 0:256], ALU.mult)
                    v.drain()
                    v.tensor_tensor(ogchunk[:], ogchunk[:], gnt[:, 256:512], ALU.add)
                    v.drain()
                    for h in range(4):
                        v.scalar_tensor_tensor(ogchunk[:, 64 * h:64 * (h + 1)],
                                               vS[:, ch * 256 + 64 * h:ch * 256 + 64 * (h + 1)],
                                               rksum[:, h:h + 1], ogchunk[:, 64 * h:64 * (h + 1)],
                                               ALU.mult, ALU.add)
                    v.drain()
                    v.tensor_tensor(ogS[:, ch * 256:(ch + 1) * 256], ogchunk[:],
                                    gS[:, ch * 256:(ch + 1) * 256], ALU.mult)
                    return v.drain()
                m_v7 = sig("V", v7)
                m_chT_last = tick["T"]
            ogall[s] = m_v7 if m_v7 else tick["V"]

            if PB_STAGE < 7:
                continue
            wt("G", "V", m_v7)
            sig("O", lambda g, t0=t0: g.dma_start(
                out=og[t0:t0 + SL, :].rearrange("(tt p) j -> p tt j", p=128),
                in_=ogS[:, 0:NTT * 256].rearrange("p (tt j) -> p tt j", tt=NTT)))

        if PB_STAGE >= 7:
            wt("G", "O", tick["O"])
        wt("G", "V", tick["V"])
        wt("G", "T", tick["T"])
        wt("G", "A", tick["A"])
        sig("C", lambda g: g.drain())
        _finalize_pb(block, prog)
    return nc


def host_pB_inputs(inputs, z2):
    pre = lambda k: np.asarray(inputs["L0_attn_" + k], np.float32)
    z2 = np.asarray(z2).reshape(B, T, C)
    maps = []
    for c in range(NC):
        b, g = c // 4, c % 4
        cs = slice(256 * g, 256 * (g + 1))
        z2T = np.zeros((C, T + 1), bf)
        z2T[:, 1:] = np.ascontiguousarray(z2[b].T)
        wcat = np.concatenate([pre("W_r")[:, cs], pre("W_k")[:, cs], pre("w1"),
                               pre("a1"), pre("g1"), pre("W_v")[:, cs]], axis=1)
        mixl = [pre("x_r"), pre("x_k"), pre("x_w"), pre("x_a"), pre("x_g"), pre("x_v")]
        mixcat = np.concatenate([m.reshape(8, 128).T for m in mixl], axis=1)
        lnwb = np.concatenate([pre("ln_attn_w").reshape(8, 128).T,
                               pre("ln_attn_b").reshape(8, 128).T], axis=1)
        smallc = np.concatenate([pre("w0")[cs].reshape(2, 128).T,
                                 pre("a0")[cs].reshape(2, 128).T,
                                 pre("k_k")[cs].reshape(2, 128).T,
                                 pre("k_a")[cs].reshape(2, 128).T,
                                 pre("r_k").reshape(C)[cs].reshape(2, 128).T], axis=1)
        gnwb = np.concatenate([np.broadcast_to(pre("gn_w")[cs][None], (128, 256)),
                               np.broadcast_to(pre("gn_b")[cs][None], (128, 256))], axis=1)
        maps.append({
            "z2T": z2T,
            "wcat": np.ascontiguousarray(wcat),
            "mixcat": np.ascontiguousarray(mixcat),
            "lnwb": np.ascontiguousarray(lnwb),
            "w2sI": np.ascontiguousarray(pre("w2")[:, cs]),
            "a2sI": np.ascontiguousarray(pre("a2")[:, cs]),
            "g2sI": np.ascontiguousarray(pre("g2")[:, cs]),
            "smallc": np.ascontiguousarray(smallc),
            "gnwb": np.ascontiguousarray(gnwb),
        })
    return maps


def run_pB(inputs, z2):
    nc = build_pB()
    maps = host_pB_inputs(inputs, z2)
    res = _run(nc, maps)
    og = np.zeros((B, T, C), bf)
    for c in range(NC):
        og[c // 4, :, 256 * (c % 4):256 * (c % 4 + 1)] = np.asarray(res.results[c]["og"])
    return og.reshape(B * T, C)




class Sched:
    def __init__(self, stack, nc, streams, pools):
        self.nc = nc
        self.seqs = {k: SemSeq(stack, nc, "s" + k, inc, total)
                     for k, (_, inc, total) in streams.items()}
        self.eng = {k: v[0] for k, v in streams.items()}
        self.tick = {k: 0 for k in streams}
        self.prog = {"S": [], "T": [], "V": [], "A": [], "G": []}
        self.pools = pools
        self.rot = {k: 0 for k in pools}
        self.blast = {}

    def emit(self, engn, fn):
        self.prog[engn].append(fn)

    def run(self, engn, fn):
        self.prog[engn].append(fn)

    def sig(self, stream, fn):
        i = self.tick[stream]
        self.tick[stream] += 1
        sq = self.seqs[stream]

        def g(eng, fn=fn, i=i, sq=sq):
            sq.attach(fn(eng), i)
        self.emit(self.eng[stream], g)
        return i + 1

    def wt(self, engn, stream, count):
        if count <= 0:
            return
        i = count - 1
        sq = self.seqs[stream]
        if sq.inc == 16:
            def g(eng, sq=sq, i=i):
                sq.wait(eng, i)
        else:
            def g(eng, sq=sq, i=i):
                eng.wait_ge(sq.sems[i % sq.k], sq.inc * (i // sq.k + 1))
        self.emit(engn, g)

    def grp(self, pool, mmfn, estream, efn, waits=()):
        bank = self.pools[pool][self.rot[pool] % len(self.pools[pool])]
        self.rot[pool] += 1
        bl = self.blast.get(bank.name)
        if bl:
            self.wt("T", bl[0], bl[1])
        for (stm, m) in waits:
            self.wt("T", stm, m)
        mT = self.sig("T", lambda t, mmfn=mmfn, bank=bank: mmfn(t, bank))
        self.wt(self.eng[estream], "T", mT)
        if self.eng[estream] == "V":
            mE = self.sig(estream, lambda v, efn=efn, bank=bank: (efn(v, bank), v.drain())[1])
        else:
            mE = self.sig(estream, lambda v, efn=efn, bank=bank: efn(v, bank))
        self.blast[bank.name] = (estream, mE)
        return mT, mE

    def finalize(self, block):
        prog = self.prog

        @block.sync
        def _(sy):
            for f in prog["S"]:
                f(sy)

        @block.tensor
        def _(t):
            for f in prog["T"]:
                f(t)

        @block.vector
        def _(v):
            for f in prog["V"]:
                f(v)

        @block.scalar
        def _(a):
            for f in prog["A"]:
                f(a)

        @block.gpsimd
        def _(g):
            for f in prog["G"]:
                f(g)



CT = TPC + 128
CNT = CT // 128


def build_pC():
    nc = _bass()
    ogT = nc.dram_tensor("ogT", [C, CT], BF16, kind="ExternalInput").ap()
    xpt = nc.dram_tensor("xpt", [CT, C], F32, kind="ExternalInput").ap()
    x0t = nc.dram_tensor("x0t", [TPC, C], F32, kind="ExternalInput").ap()
    WoI = nc.dram_tensor("WoI", [C, C], F32, kind="ExternalInput").ap()
    Wf1I = nc.dram_tensor("Wf1I", [C, 4096], F32, kind="ExternalInput").ap()
    Wf2I = nc.dram_tensor("Wf2I", [4096, C], F32, kind="ExternalInput").ap()
    lnfc = nc.dram_tensor("lnfc", [128, 24], F32, kind="ExternalInput").ap()
    bmaskI = nc.dram_tensor("bmaskI", [128, 1], F32, kind="ExternalInput").ap()
    identI = nc.dram_tensor("identI", [128, 128], BF16, kind="ExternalInput").ap()
    xres = nc.dram_tensor("xres", [TPC, C], F32, kind="ExternalOutput").ap()
    z3o = nc.dram_tensor("z3o", [TPC, C], BF16, kind="ExternalOutput").ap()
    zsumo = nc.dram_tensor("zsumo", [1, C], F32, kind="ExternalOutput").ap()

    from contextlib import ExitStack
    with ExitStack() as stack:
        e = stack.enter_context
        wbig = e(nc.sbuf_tensor("wbig", [128, 8192], BF16))
        wf1b = [e(nc.sbuf_tensor(f"wf1b{j}", [128, 4096], BF16)) for j in range(2)]
        w1bg = e(nc.sbuf_tensor("w1bg", [128, 4096], BF16))
        zfT = e(nc.sbuf_tensor("zfT", [128, 8 * CT], BF16))
        x1acc = e(nc.sbuf_tensor("x1acc", [128, 16 * 1024], F32))
        hR = e(nc.sbuf_tensor("hR", [128, 4 * 2048], BF16))
        ogst = e(nc.sbuf_tensor("ogst", [128, 2 * 1024], BF16))
        xt2 = e(nc.sbuf_tensor("xt2", [128, 2 * 1024], F32))
        xpt2 = e(nc.sbuf_tensor("xpt2", [128, 2 * 1024], F32))
        x0t2 = e(nc.sbuf_tensor("x0t2", [128, 2 * 1024], F32))
        zft = e(nc.sbuf_tensor("zft", [128, 2 * 1024], BF16))
        z3t = e(nc.sbuf_tensor("z3t", [128, 2 * 1024], BF16))
        lnc = e(nc.sbuf_tensor("lnc", [128, 24], F32))
        bmk = e(nc.sbuf_tensor("bmk", [128, 1], F32))
        sabF = e(nc.sbuf_tensor("sabF", [128, 16], F32))
        lnfbB = e(nc.sbuf_tensor("lnfbB", [128, 8 * 64], BF16))
        onesFc = e(nc.sbuf_tensor("onesFc", [128, 64], F32))
        ones128b = e(nc.sbuf_tensor("ones128b", [128, 128], BF16))
        identT = e(nc.sbuf_tensor("identT", [128, 128], BF16))
        biasF = e(nc.sbuf_tensor("biasF", [128, 4], F32))
        zsumacc = e(nc.sbuf_tensor("zsumacc", [1, 1024], F32))
        scr = e(nc.sbuf_tensor("scr", [128, 1024], F32))
        sm = e(nc.sbuf_tensor("sm", [128, 1], F32))
        mean = e(nc.sbuf_tensor("mean", [128, 1], F32))
        var = e(nc.sbuf_tensor("var", [128, 1], F32))
        ur = e(nc.sbuf_tensor("ur", [128, 1], F32))
        pM = [e(nc.psum_tensor(f"pM{j}", [128, 512], F32)) for j in range(6)]
        pQ = [e(nc.psum_tensor("pQ0", [128, 64], F32))]
        pX = [e(nc.psum_tensor("pX0", [128, 128], BF16))]
        block_streams = {
            "T": ("T", 1, 700), "V": ("V", 1, 700), "A": ("A", 1, 450),
            "D": ("S", 16, 60), "G": ("G", 16, 24), "O": ("G", 16, 40),
        }
        sc = Sched(stack, nc, block_streams, {"S": pM, "Q": pQ, "X": pX})
        sig, wt, run, grp = sc.sig, sc.wt, sc.run, sc.grp
        block = e(nc.Block())

        run("A", lambda a: a.preload_act_table(AF.Sqrt))
        run("A", lambda a: a.drain())
        mG_wo = sig("G", lambda g: g.dma_start(
            out=wbig[:].rearrange("p (c j) -> p c j", c=8),
            in_=WoI.rearrange("(c p) j -> p c j", p=128)))
        mD = sig("D", lambda s: s.dma_start(out=lnc[:], in_=lnfc))
        mD = sig("D", lambda s: s.dma_start(out=bmk[:], in_=bmaskI))
        mD = sig("D", lambda s: s.dma_start(out=identT[:], in_=identI))
        mD_small = mD

        def const_build(v):
            v.memset(onesFc[:], 1.0)
            v.memset(ones128b[:], 1.0)
            v.memset(zsumacc[:], 0.0)
            v.drain()
            return v.drain()
        mV_const = sig("V", const_build)
        wt("V", "D", mD_small)

        def prep(v):
            for c in range(8):
                v.tensor_tensor(sabF[:, 2 * c + 1:2 * c + 2], lnc[:, c:c + 1],
                                lnc[:, 16 + c:17 + c], ALU.mult)
            v.drain()
            for c in range(8):
                v.tensor_tensor(sabF[:, 2 * c:2 * c + 1], lnc[:, c:c + 1],
                                sabF[:, 2 * c + 1:2 * c + 2], ALU.subtract)
                v.tensor_scalar(lnfbB[:, 64 * c:64 * (c + 1)], onesFc[:],
                                lnc[:, 8 + c:9 + c], None, ALU.mult)
            return v.drain()
        mV_prep = sig("V", prep)

        mT_tr_last = [0] * 2
        mV_x_last = [0] * 2
        for ti in range(CNT):
            sl = ti % 2
            if mT_tr_last[sl]:
                wt("S", "T", mT_tr_last[sl])
            if mV_x_last[sl]:
                wt("S", "V", mV_x_last[sl])
            mD_og = sig("D", lambda s, ti=ti, sl=sl: s.dma_start(
                out=ogst[:, 1024 * sl:1024 * (sl + 1)].rearrange("p (c j) -> p c j", c=8),
                in_=ogT.rearrange("(c p) t -> p c t", p=128)[:, :, 128 * ti:128 * (ti + 1)]))
            mD_xp = sig("D", lambda s, ti=ti, sl=sl: s.dma_start(
                out=xpt2[:, 1024 * sl:1024 * (sl + 1)],
                in_=xpt[128 * ti:128 * (ti + 1), :]))
            if ti >= 1:
                mD_x0 = sig("D", lambda s, ti=ti, sl=sl: s.dma_start(
                    out=x0t2[:, 1024 * sl:1024 * (sl + 1)],
                    in_=x0t[128 * (ti - 1):128 * ti, :]))
            mE_half = 0
            for h in range(2):
                def womm(t, bank, sl=sl, h=h):
                    inst = None
                    for c in range(8):
                        inst = t.matmul(bank[:, 0:512], ogst[:, 1024 * sl + 128 * c:1024 * sl + 128 * (c + 1)],
                                        wbig[:, 1024 * c + 512 * h:1024 * c + 512 * (h + 1)],
                                        start=(c == 0), stop=(c == 7))
                    return inst

                def woev(v, bank, sl=sl, h=h):
                    return v.tensor_tensor(xt2[:, 1024 * sl + 512 * h:1024 * sl + 512 * (h + 1)],
                                           bank[:, 0:512],
                                           xpt2[:, 1024 * sl + 512 * h:1024 * sl + 512 * (h + 1)], ALU.add)
                w = [("D", mD_og), ("D", mD_xp)] if h == 0 else []
                if ti == 0 and h == 0:
                    w.append(("G", mG_wo))
                _, mE_half = grp("S", womm, "V", woev, waits=w)

            def lnstats(v, sl=sl):
                xv = xt2[:, 1024 * sl:1024 * (sl + 1)]
                emit_stats1024(v, xv, scr[:], sm[:], var[:], mean[:])
                emit_inv(v, var[:], ur[:])
                return v.drain()
            mVs = sig("V", lnstats)
            wt("A", "V", mVs)
            mAs = sig("A", lambda a: a.activation(ur[:], ur[:], AF.Sqrt))
            wt("V", "A", mAs)
            if ti >= 1:
                wt("V", "D", mD_x0)
            if mT_tr_last[sl]:
                wt("V", "T", mT_tr_last[sl])

            def zfscale(v, sl=sl, ti=ti):
                xv = xt2[:, 1024 * sl:1024 * (sl + 1)]
                v.tensor_scalar(zft[:, 1024 * sl:1024 * (sl + 1)], xv, mean[:], ur[:],
                                ALU.subtract, ALU.mult)
                if ti >= 1:
                    v.drain()
                    v.tensor_tensor(x1acc[:, 1024 * (ti - 1):1024 * ti], xv,
                                    x0t2[:, 1024 * sl:1024 * (sl + 1)], ALU.add)
                return v.drain()
            mVz = sig("V", zfscale)
            mV_x_last[sl] = mVz

            for c in range(8):
                def trmm(t, bank, sl=sl, c=c):
                    return t.transpose(bank[:, 0:128], zft[:, 1024 * sl + 128 * c:1024 * sl + 128 * (c + 1)],
                                       identT[:])

                def trev(a, bank, c=c, ti=ti):
                    return a.activation(zfT[:, CT * c + 128 * ti:CT * c + 128 * (ti + 1)],
                                        bank[:], AF.Identity)
                w = [("V", mVz)] if c == 0 else []
                if ti == 0 and c == 0:
                    w.append(("D", mD_small))
                mTt, _ = grp("X", trmm, "A", trev, waits=w)
            mT_tr_last[sl] = mTt

        wt("V", "A", sc.tick["A"])

        def bnd(v):
            for c in range(8):
                v.tensor_scalar(zfT[:, CT * c + 127:CT * c + 128],
                                zfT[:, CT * c + 127:CT * c + 128], bmk[:, 0:1], None, ALU.mult)
            return v.drain()
        mV_bnd = sig("V", bnd)

        NG = 8
        mT_w_last = 0
        m_f1T = [0, 0]
        for g in range(NG):
            gsl = g % 2
            wf1g = wf1b[gsl]
            if m_f1T[gsl]:
                wt("G", "T", m_f1T[gsl])
            elif g == 0:
                wt("G", "T", sc.tick["T"])
            mG_w1 = sig("G", lambda gp, g=g, wf1g=wf1g: gp.dma_start(
                out=wf1g[:].rearrange("p (c j) -> p c j", c=8),
                in_=Wf1I[:, 512 * g:512 * (g + 1)].rearrange("(c p) j -> p c j", p=128)))
            if g > 0:
                wt("G", "T", mT_w_last)
            mG_w2 = sig("G", lambda gp, g=g: gp.dma_start(
                out=wbig[:, 0:4096].rearrange("p (c j) -> p c j", c=4),
                in_=Wf2I[512 * g:512 * (g + 1), :].rearrange("(c p) j -> p c j", p=128)))
            mT_bias = 0
            for oc in range(4):
                def bmm(t, bank, oc=oc, wf1g=wf1g):
                    inst = None
                    for c in range(8):
                        inst = t.matmul(bank[:, 0:64], wf1g[:, 512 * c + 128 * oc:512 * c + 128 * (oc + 1)],
                                        lnfbB[:, 64 * c:64 * c + 64], start=(c == 0), stop=(c == 7))
                    return inst

                def bev(a, bank, oc=oc):
                    return a.activation(biasF[:, oc:oc + 1], bank[:, 0:1], AF.Identity)
                w = [("G", mG_w1), ("V", mV_prep)] if oc == 0 else []
                mT_bias, _ = grp("Q", bmm, "A", bev, waits=w)

            wt("V", "T", mT_bias)

            def wsc(v, wf1g=wf1g):
                for c in range(8):
                    v.tensor_scalar(w1bg[:, 512 * c:512 * (c + 1)], wf1g[:, 512 * c:512 * (c + 1)],
                                    sabF[:, 2 * c + 1:2 * c + 2], None, ALU.mult)
                v.drain()
                for c in range(8):
                    v.tensor_scalar(wf1g[:, 512 * c:512 * (c + 1)], wf1g[:, 512 * c:512 * (c + 1)],
                                    sabF[:, 2 * c:2 * c + 1], None, ALU.mult)
                return v.drain()
            mV_wsc = sig("V", wsc)

            mV_sq = 0
            for oc in range(4):
                for tt in range(4):
                    def f1mm(t, bank, oc=oc, tt=tt, wf1g=wf1g):
                        for c in range(8):
                            t.matmul(bank[:, 0:512], wf1g[:, 512 * c + 128 * oc:512 * c + 128 * (oc + 1)],
                                     zfT[:, CT * c + 128 + 512 * tt:CT * c + 128 + 512 * (tt + 1)],
                                     start=(c == 0), stop=False)
                        inst = None
                        for c in range(8):
                            inst = t.matmul(bank[:, 0:512], w1bg[:, 512 * c + 128 * oc:512 * c + 128 * (oc + 1)],
                                            zfT[:, CT * c + 127 + 512 * tt:CT * c + 127 + 512 * (tt + 1)],
                                            start=False, stop=(c == 7))
                        return inst

                    def f1ev(a, bank, oc=oc, tt=tt):
                        return a.activation(hR[:, 2048 * oc + 512 * tt:2048 * oc + 512 * (tt + 1)],
                                            bank[:, 0:512], AF.Relu, bias=biasF[:, oc:oc + 1])
                    w = []
                    if oc == 0 and tt == 0:
                        w = [("V", mV_wsc), ("V", mV_bnd)]
                    mT_f1c, mAr = grp("S", f1mm, "A", f1ev, waits=w)
                    wt("V", "A", mAr)

                    def sq(v, oc=oc, tt=tt):
                        s = hR[:, 2048 * oc + 512 * tt:2048 * oc + 512 * (tt + 1)]
                        return v.tensor_tensor(s, s, s, ALU.mult)
                    mV_sq = sig("V", sq)
            m_f1T[gsl] = mT_f1c

            for st in range(16):
                for h in range(2):
                    def f2mm(t, bank, st=st, h=h):
                        inst = None
                        for c4 in range(4):
                            inst = t.matmul(bank[:, 0:512], hR[:, 2048 * c4 + 128 * st:2048 * c4 + 128 * (st + 1)],
                                            wbig[:, 1024 * c4 + 512 * h:1024 * c4 + 512 * (h + 1)],
                                            start=(c4 == 0), stop=(c4 == 3))
                        return inst

                    def f2ev(v, bank, st=st, h=h):
                        s = x1acc[:, 1024 * st + 512 * h:1024 * st + 512 * (h + 1)]
                        return v.tensor_tensor(s, s, bank[:, 0:512], ALU.add)
                    w = []
                    if st == 0 and h == 0:
                        w = [("V", mV_sq), ("G", mG_w2)]
                    mT_w_last, _ = grp("S", f2mm, "V", f2ev, waits=w)

        mO = 0
        mT_zs = [0, 0]
        for ti in range(16):
            sl = ti % 2
            wt("V", "T", mT_w_last)

            def lnstats2(v, ti=ti):
                xv = x1acc[:, 1024 * ti:1024 * (ti + 1)]
                emit_stats1024(v, xv, scr[:], sm[:], var[:], mean[:])
                emit_inv(v, var[:], ur[:])
                return v.drain()
            mVs = sig("V", lnstats2)
            wt("A", "V", mVs)
            mAs = sig("A", lambda a: a.activation(ur[:], ur[:], AF.Sqrt))
            wt("V", "A", mAs)
            if ti >= 2:
                wt("V", "O", mO - 1)
                wt("V", "T", mT_zs[sl])

            def z3scale(v, ti=ti, sl=sl):
                xv = x1acc[:, 1024 * ti:1024 * (ti + 1)]
                return v.tensor_scalar(z3t[:, 1024 * sl:1024 * (sl + 1)], xv, mean[:], ur[:],
                                       ALU.subtract, ALU.mult)
            mVz = sig("V", z3scale)

            for h in range(2):
                def zsmm(t, bank, sl=sl, h=h):
                    return t.matmul(bank[:, 0:512], ones128b[:],
                                    z3t[:, 1024 * sl + 512 * h:1024 * sl + 512 * (h + 1)],
                                    start=True, stop=True)

                def zsev(v, bank, h=h):
                    return v.tensor_tensor(zsumacc[0:1, 512 * h:512 * (h + 1)],
                                           zsumacc[0:1, 512 * h:512 * (h + 1)], bank[0:1, 0:512], ALU.add)
                w = [("V", mVz)] if h == 0 else []
                mT_zs[sl], _ = grp("S", zsmm, "V", zsev, waits=w)

            wt("G", "V", mVz)
            mO = sig("O", lambda gp, ti=ti: gp.dma_start(
                out=xres[128 * ti:128 * (ti + 1), :], in_=x1acc[:, 1024 * ti:1024 * (ti + 1)]))
            mO = sig("O", lambda gp, ti=ti, sl=sl: gp.dma_start(
                out=z3o[128 * ti:128 * (ti + 1), :], in_=z3t[:, 1024 * sl:1024 * (sl + 1)]))

        wt("G", "V", sc.tick["V"])
        mO = sig("O", lambda gp: gp.dma_start(out=zsumo, in_=zsumacc[:]))
        wt("G", "O", sc.tick["O"])
        wt("G", "T", sc.tick["T"])
        wt("G", "A", sc.tick["A"])
        sig("G", lambda gp: gp.drain())
        sc.finalize(block)
    return nc


def host_pC_inputs(inputs, xp, og):
    xp = np.asarray(xp, np.float32).reshape(B, T, C)
    og = np.asarray(og, bf).reshape(B, T, C)
    x0 = np.asarray(inputs["L0_attn_x"], np.float32).reshape(B, T, C)
    lnf = [np.asarray(inputs["L0_attn_ln_ffn_w"], np.float32),
           np.asarray(inputs["L0_attn_ln_ffn_b"], np.float32),
           np.asarray(inputs["L0_attn_ffn_x_k"], np.float32)]
    lnfc = np.concatenate([v.reshape(8, 128).T for v in lnf], axis=1)
    maps = []
    for c in range(NC):
        b, q = c // 4, c % 4
        s0 = 2048 * q
        ogp = np.zeros((CT, C), bf)
        xpp = np.zeros((CT, C), np.float32)
        if q == 0:
            ogp[128:] = og[b, 0:2048 + 128 - 128]
            xpp[128:] = xp[b, 0:2048]
            bmv = 0.0
        else:
            ogp[:] = og[b, s0 - 128:s0 + 2048]
            xpp[:] = xp[b, s0 - 128:s0 + 2048]
            bmv = 1.0
        maps.append({
            "ogT": np.ascontiguousarray(ogp.T),
            "xpt": xpp,
            "x0t": np.ascontiguousarray(x0[b, s0:s0 + 2048]),
            "WoI": np.asarray(inputs["L0_attn_W_o"], np.float32),
            "Wf1I": np.asarray(inputs["L0_attn_W_ffn1"], np.float32),
            "Wf2I": np.asarray(inputs["L0_attn_W_ffn2"], np.float32),
            "lnfc": np.ascontiguousarray(lnfc),
            "bmaskI": np.full((128, 1), bmv, np.float32),
            "identI": np.ascontiguousarray(np.eye(128, dtype=bf)),
        })
    return maps


def run_pC(inputs, xp, og):
    nc = build_pC()
    maps = host_pC_inputs(inputs, xp, og)
    res = _run(nc, maps)
    xres = np.concatenate([np.asarray(r["xres"]) for r in res.results], 0)
    z3 = np.concatenate([np.asarray(r["z3o"]) for r in res.results], 0)
    zsum = np.stack([np.asarray(r["zsumo"])[0] for r in res.results], 0)
    return xres, z3, zsum




def build_pD2():
    nc = _bass()
    zparts = nc.dram_tensor("zparts", [128, 32], F32, kind="ExternalInput").ap()
    WrC = nc.dram_tensor("WrC", [128, 64], F32, kind="ExternalInput").ap()
    brB = nc.dram_tensor("brB", [128, 8], F32, kind="ExternalInput").ap()
    oho = nc.dram_tensor("oho", [1, 8], F32, kind="ExternalOutput").ap()
    from contextlib import ExitStack
    with ExitStack() as stack:
        e = stack.enter_context
        zp = e(nc.sbuf_tensor("zp", [128, 32], F32))
        wrc = e(nc.sbuf_tensor("wrc", [128, 64], F32))
        wrb = e(nc.sbuf_tensor("wrb", [128, 64], BF16))
        brb = e(nc.sbuf_tensor("brb", [128, 8], F32))
        zm = e(nc.sbuf_tensor("zm", [128, 8], F32))
        zh = e(nc.sbuf_tensor("zh", [128, 8], BF16))
        zhf = e(nc.sbuf_tensor("zhf", [128, 8], F32))
        zr = e(nc.sbuf_tensor("zr", [128, 8], F32))
        onesR = e(nc.sbuf_tensor("onesR", [128, 128], F32))
        zmB = e(nc.sbuf_tensor("zmB", [128, 16 * 128], BF16))
        sc8 = e(nc.sbuf_tensor("sc8", [128, 8], F32))
        mx = e(nc.sbuf_tensor("mx", [128, 1], F32))
        oh = e(nc.sbuf_tensor("oh", [128, 8], F32))
        pR = e(nc.psum_tensor("pR", [128, 8], F32))
        sg = e(nc.semaphore("sg"))
        sv = e(nc.semaphore("sv"))
        st = e(nc.semaphore("st"))
        so = e(nc.semaphore("so"))
        block = e(nc.Block())

        @block.sync
        def _(sync):
            sync.dma_start(out=zp[:], in_=zparts).then_inc(sg, 16)
            sync.dma_start(out=wrc[:], in_=WrC).then_inc(sg, 16)
            sync.dma_start(out=brb[:], in_=brB).then_inc(sg, 16)
            sync.wait_ge(sv, 3)
            sync.dma_start(out=oho, in_=oh[0:1, 0:8]).then_inc(so, 16)

        @block.vector
        def _(v):
            v.memset(onesR[:], 1.0)
            v.wait_ge(sg, 48)
            v.tensor_tensor(zm[:], zp[:, 0:8], zp[:, 8:16], ALU.add)
            v.tensor_copy(wrb[:], wrc[:])
            v.drain()
            v.tensor_tensor(zm[:], zm[:], zp[:, 16:24], ALU.add)
            v.drain()
            v.tensor_tensor(zm[:], zm[:], zp[:, 24:32], ALU.add)
            v.drain()
            v.tensor_scalar(zm[:], zm[:], 1.0 / float(T), None, ALU.mult)
            v.drain()
            v.tensor_copy(zh[:], zm[:])
            v.drain()
            v.tensor_copy(zhf[:], zh[:])
            v.drain()
            v.tensor_tensor(zr[:], zm[:], zhf[:], ALU.subtract)
            v.drain()
            for c in range(8):
                v.tensor_scalar(zmB[:, 128 * c:128 * (c + 1)], onesR[:],
                                zm[:, c:c + 1], None, ALU.mult)
                v.tensor_scalar(zmB[:, 1024 + 128 * c:1024 + 128 * (c + 1)], onesR[:],
                                zr[:, c:c + 1], None, ALU.mult)
            v.drain().then_inc(sv, 1)
            v.wait_ge(st, 1)
            v.tensor_tensor(sc8[:], pR[:], brb[:], ALU.add)
            v.drain()
            v.tensor_reduce(mx[:], sc8[:], AX, ALU.max)
            v.drain()
            v.tensor_scalar(oh[:], sc8[:], mx[:, 0:1], None, ALU.is_ge)
            v.drain().then_inc(sv, 2)

        @block.tensor
        def _(t):
            t.wait_ge(sv, 1)
            for c in range(16):
                t.matmul(pR[:], zmB[:, 128 * c:128 * (c + 1)], wrb[:, 8 * (c % 8):8 * (c % 8 + 1)],
                         start=(c == 0), stop=(c == 15))
            t.drain().then_inc(st, 1)

        @block.gpsimd
        def _(g):
            g.wait_ge(so, 16)
    return nc


def run_pD2(inputs, zsum):
    nc = build_pD2()
    Wr = np.asarray(inputs["L0_ffn_Wr"], np.float32)
    br = np.asarray(inputs["L0_ffn_br"], np.float32)
    WrC = np.concatenate([Wr.reshape(8, 128, 8)[c] for c in range(8)], axis=1)
    brB = np.ascontiguousarray(np.broadcast_to(br[None, :], (128, 8)))
    maps = []
    for c in range(NC):
        b = c // 4
        zparts = np.concatenate([zsum[4 * b + j].reshape(8, 128).T for j in range(4)], axis=1)
        maps.append({"zparts": np.ascontiguousarray(zparts), "WrC": np.ascontiguousarray(WrC),
                     "brB": brB})
    res = _run(nc, maps)
    ohs = [np.asarray(res.results[c]["oho"])[0] for c in (0, 4)]
    return ohs




def build_pD():
    nc = _bass()
    xresI = nc.dram_tensor("xresI", [TPC, C], F32, kind="ExternalInput").ap()
    z3TI = nc.dram_tensor("z3TI", [C, TPC], BF16, kind="ExternalInput").ap()
    W1e = nc.dram_tensor("W1e", [C, 4096], F32, kind="ExternalInput").ap()
    b1CI = nc.dram_tensor("b1CI", [128, 32], F32, kind="ExternalInput").ap()
    W2e = nc.dram_tensor("W2e", [4096, C], F32, kind="ExternalInput").ap()
    b2BI = nc.dram_tensor("b2BI", [128, C], F32, kind="ExternalInput").ap()
    WqD = nc.dram_tensor("WqD", [C, C], F32, kind="ExternalInput").ap()
    WkD = nc.dram_tensor("WkD", [C, C], F32, kind="ExternalInput").ap()
    WvI = nc.dram_tensor("WvI", [C, C], F32, kind="ExternalInput").ap()
    csQI = nc.dram_tensor("csQI", [128, 8192], F32, kind="ExternalInput").ap()
    snQI = nc.dram_tensor("snQI", [128, 8192], F32, kind="ExternalInput").ap()
    csKI = nc.dram_tensor("csKI", [128, 8192], F32, kind="ExternalInput").ap()
    snKI = nc.dram_tensor("snKI", [128, 8192], F32, kind="ExternalInput").ap()
    identI = nc.dram_tensor("identI", [128, 128], BF16, kind="ExternalInput").ap()
    x2o = nc.dram_tensor("x2o", [TPC, C], F32, kind="ExternalOutput").ap()
    QTo = nc.dram_tensor("QTo", [C, TPC], BF16, kind="ExternalOutput").ap()
    KTo = nc.dram_tensor("KTo", [C, TPC], BF16, kind="ExternalOutput").ap()
    Vo = nc.dram_tensor("Vo", [TPC, C], BF16, kind="ExternalOutput").ap()

    from contextlib import ExitStack
    with ExitStack() as stack:
        e = stack.enter_context
        z3s = e(nc.sbuf_tensor("z3s", [128, 8 * 2048], BF16))
        SCR = e(nc.sbuf_tensor("SCR", [128, 16 * 1024], F32))
        hR = e(nc.sbuf_tensor("hR", [128, 4 * 2048], BF16))
        wf1g = e(nc.sbuf_tensor("wf1g", [128, 4096], BF16))
        wbig = e(nc.sbuf_tensor("wbig", [128, 8192], BF16))
        qkvB = e(nc.sbuf_tensor("qkvB", [128, 16 * 1024], BF16))
        xrt = e(nc.sbuf_tensor("xrt", [128, 2 * 1024], F32))
        z4t = e(nc.sbuf_tensor("z4t", [128, 2 * 1024], BF16))
        rtmp = e(nc.sbuf_tensor("rtmp", [128, 2048], BF16))
        rtm2 = e(nc.sbuf_tensor("rtm2", [128, 2048], BF16))
        csb = e(nc.sbuf_tensor("csb", [128, 2048], BF16))
        snb = e(nc.sbuf_tensor("snb", [128, 2048], BF16))
        b1C = e(nc.sbuf_tensor("b1C", [128, 32], F32))
        b2B = e(nc.sbuf_tensor("b2B", [128, 1024], F32))
        identT = e(nc.sbuf_tensor("identT", [128, 128], BF16))
        scr = e(nc.sbuf_tensor("scr", [128, 1024], F32))
        sm = e(nc.sbuf_tensor("sm", [128, 1], F32))
        mean = e(nc.sbuf_tensor("mean", [128, 1], F32))
        var = e(nc.sbuf_tensor("var", [128, 1], F32))
        ur = e(nc.sbuf_tensor("ur", [128, 1], F32))
        pM = [e(nc.psum_tensor(f"pM{j}", [128, 512], F32)) for j in range(6)]
        pX = [e(nc.psum_tensor("pX0", [128, 128], BF16))]
        sc = Sched(stack, nc, {
            "T": ("T", 1, 800), "V": ("V", 1, 700), "A": ("A", 1, 600),
            "D": ("S", 16, 60), "G": ("G", 16, 40), "O": ("G", 16, 60),
        }, {"S": pM, "X": pX})
        sig, wt, run, grp = sc.sig, sc.wt, sc.run, sc.grp
        block = e(nc.Block())

        run("A", lambda a: a.preload_act_table(AF.Sqrt))
        run("A", lambda a: a.drain())
        mD = sig("D", lambda s: s.dma_start(out=identT[:], in_=identI))
        mD = sig("D", lambda s: s.dma_start(out=b1C[:], in_=b1CI))
        mD = sig("D", lambda s: s.dma_start(out=b2B[:], in_=b2BI))
        mD_small = mD
        mD_z3 = sig("D", lambda s: s.dma_start(
            out=z3s[:].rearrange("p (c t) -> p c t", c=8),
            in_=z3TI.rearrange("(c p) t -> p c t", p=128)))

        mV_init = 0
        for ti in range(16):
            sl = ti % 2
            if ti >= 2:
                wt("S", "V", mV_init - 1)
            mD_x = sig("D", lambda s, ti=ti, sl=sl: s.dma_start(
                out=xrt[:, 1024 * sl:1024 * (sl + 1)], in_=xresI[128 * ti:128 * (ti + 1), :]))
            wt("V", "D", mD_x)
            if ti == 0:
                wt("V", "D", mD_small)

            def init(v, ti=ti, sl=sl):
                v.tensor_tensor(SCR[:, 1024 * ti:1024 * (ti + 1)],
                                xrt[:, 1024 * sl:1024 * (sl + 1)], b2B[:], ALU.add)
                return v.drain()
            mV_init = sig("V", init)

        NG = 8
        mT_w_last = 0
        for g in range(NG):
            if g > 0:
                wt("G", "T", mT_w_last)
            mG_w1 = sig("G", lambda gp, g=g: gp.dma_start(
                out=wf1g[:].rearrange("p (c j) -> p c j", c=8),
                in_=W1e[:, 512 * g:512 * (g + 1)].rearrange("(c p) j -> p c j", p=128)))
            mG_w2 = sig("G", lambda gp, g=g: gp.dma_start(
                out=wbig[:, 0:4096].rearrange("p (c j) -> p c j", c=4),
                in_=W2e[512 * g:512 * (g + 1), :].rearrange("(c p) j -> p c j", p=128)))
            for oc in range(4):
                for tt in range(4):
                    def f1mm(t, bank, oc=oc, tt=tt):
                        inst = None
                        for c in range(8):
                            inst = t.matmul(bank[:, 0:512],
                                            wf1g[:, 512 * c + 128 * oc:512 * c + 128 * (oc + 1)],
                                            z3s[:, 2048 * c + 512 * tt:2048 * c + 512 * (tt + 1)],
                                            start=(c == 0), stop=(c == 7))
                        return inst

                    def f1ev(a, bank, oc=oc, tt=tt, g=g):
                        return a.activation(hR[:, 2048 * oc + 512 * tt:2048 * oc + 512 * (tt + 1)],
                                            bank[:, 0:512], AF.Relu, bias=b1C[:, 4 * g + oc:4 * g + oc + 1])
                    w = []
                    if oc == 0 and tt == 0:
                        w = [("G", mG_w1), ("D", mD_z3), ("D", mD_small)]
                    _, mA_h = grp("S", f1mm, "A", f1ev, waits=w)
            for st in range(16):
                for h in range(2):
                    def f2mm(t, bank, st=st, h=h):
                        inst = None
                        for c4 in range(4):
                            inst = t.matmul(bank[:, 0:512],
                                            hR[:, 2048 * c4 + 128 * st:2048 * c4 + 128 * (st + 1)],
                                            wbig[:, 1024 * c4 + 512 * h:1024 * c4 + 512 * (h + 1)],
                                            start=(c4 == 0), stop=(c4 == 3))
                        return inst

                    def f2ev(v, bank, st=st, h=h):
                        s = SCR[:, 1024 * st + 512 * h:1024 * st + 512 * (h + 1)]
                        return v.tensor_tensor(s, s, bank[:, 0:512], ALU.add)
                    w = []
                    if st == 0 and h == 0:
                        w = [("A", mA_h), ("G", mG_w2), ("V", mV_init)]
                    mT_w_last, mV_x2 = grp("S", f2mm, "V", f2ev, waits=w)

        mO_x2 = 0
        mT_tr = 0
        mT_trs = [0, 0]
        for ti in range(16):
            sl = ti % 2

            def lnstats(v, ti=ti):
                xv = SCR[:, 1024 * ti:1024 * (ti + 1)]
                emit_stats1024(v, xv, scr[:], sm[:], var[:], mean[:])
                emit_inv(v, var[:], ur[:])
                return v.drain()
            if ti == 0:
                wt("V", "V", mV_x2)
            mVs = sig("V", lnstats)
            wt("A", "V", mVs)
            mAs = sig("A", lambda a: a.activation(ur[:], ur[:], AF.Sqrt))
            wt("V", "A", mAs)
            if mT_trs[sl]:
                wt("V", "T", mT_trs[sl])

            def z4scale(v, ti=ti, sl=sl):
                xv = SCR[:, 1024 * ti:1024 * (ti + 1)]
                return v.tensor_scalar(z4t[:, 1024 * sl:1024 * (sl + 1)], xv, mean[:], ur[:],
                                       ALU.subtract, ALU.mult)
            mVz = sig("V", z4scale)
            wt("G", "V", mVz)
            mO_x2 = sig("O", lambda gp, ti=ti: gp.dma_start(
                out=x2o[128 * ti:128 * (ti + 1), :], in_=SCR[:, 1024 * ti:1024 * (ti + 1)]))
            for c in range(8):
                def trmm(t, bank, sl=sl, c=c):
                    return t.transpose(bank[:, 0:128],
                                       z4t[:, 1024 * sl + 128 * c:1024 * sl + 128 * (c + 1)], identT[:])

                def trev(a, bank, c=c, ti=ti):
                    return a.activation(z3s[:, 2048 * c + 128 * ti:2048 * c + 128 * (ti + 1)],
                                        bank[:], AF.Identity)
                w = []
                if c == 0:
                    w = [("V", mVz)]
                    if ti == 0:
                        w.append(("T", mT_w_last))
                mT_tr, _ = grp("X", trmm, "A", trev, waits=w)
            mT_trs[sl] = mT_tr

        wt("G", "O", sc.tick["O"])
        mG_cs = sig("G", lambda gp: gp.dma_start(out=SCR[:, 0:8192], in_=csQI))
        mG_cs = sig("G", lambda gp: gp.dma_start(out=SCR[:, 8192:16384], in_=snQI))
        mT_proj_last = 0
        mV_rot = 0
        for which in range(3):
            WI = (WqD, WkD, WvI)[which]
            if which > 0:
                wt("G", "T", mT_proj_last)
            mG_w = sig("G", lambda gp, WI=WI: gp.dma_start(
                out=wbig[:].rearrange("p (c j) -> p c j", c=8),
                in_=WI.rearrange("(c p) j -> p c j", p=128)))
            if which == 1:
                wt("G", "V", mV_rot)
                mG_cs = sig("G", lambda gp: gp.dma_start(out=SCR[:, 0:8192], in_=csKI))
                mG_cs = sig("G", lambda gp: gp.dma_start(out=SCR[:, 8192:16384], in_=snKI))
            if which < 2:
                for oc in range(8):
                    for tt in range(4):
                        def qmm(t, bank, oc=oc, tt=tt):
                            inst = None
                            for c in range(8):
                                inst = t.matmul(bank[:, 0:512],
                                                wbig[:, 1024 * c + 128 * oc:1024 * c + 128 * (oc + 1)],
                                                z3s[:, 2048 * c + 512 * tt:2048 * c + 512 * (tt + 1)],
                                                start=(c == 0), stop=(c == 7))
                            return inst

                        def qev(a, bank, oc=oc, tt=tt):
                            return a.activation(qkvB[:, 2048 * oc + 512 * tt:2048 * oc + 512 * (tt + 1)],
                                                bank[:, 0:512], AF.Identity)
                        w = []
                        if oc == 0 and tt == 0:
                            w = [("G", mG_w), ("T", mT_tr), ("A", sc.tick["A"])]
                            if which > 0:
                                w.append(("V", sc.tick["V"]))
                                w.append(("O", sc.tick["O"]))
                        mT_proj_last, mA_p = grp("S", qmm, "A", qev, waits=w)
                wt("V", "A", mA_p)
                wt("V", "G", mG_cs)

                def rot(v):
                    for c in range(4):
                        ev = qkvB[:, 2048 * c:2048 * (c + 1)]
                        od = qkvB[:, 2048 * (c + 4):2048 * (c + 5)]
                        v.tensor_copy(csb[:], SCR[:, 2048 * c:2048 * (c + 1)])
                        v.tensor_copy(snb[:], SCR[:, 8192 + 2048 * c:8192 + 2048 * (c + 1)])
                        v.tensor_copy(rtmp[:], ev)
                        v.drain()
                        v.tensor_tensor(ev, ev, csb[:], ALU.mult)
                        v.tensor_tensor(rtm2[:], od, snb[:], ALU.mult)
                        v.drain()
                        v.tensor_tensor(ev, ev, rtm2[:], ALU.subtract)
                        v.tensor_tensor(od, od, csb[:], ALU.mult)
                        v.drain()
                        v.tensor_tensor(rtm2[:], rtmp[:], snb[:], ALU.mult)
                        v.drain()
                        v.tensor_tensor(od, od, rtm2[:], ALU.add)
                        v.drain()
                    return v.drain()
                mV_rot = sig("V", rot)
                wt("G", "V", mV_rot)
                out_t = (QTo, KTo)[which]
                sig("O", lambda gp, out_t=out_t: gp.dma_start(
                    out=out_t.rearrange("(c p) t -> p c t", p=128),
                    in_=qkvB[:].rearrange("p (c t) -> p c t", c=8)))
            else:
                for st in range(16):
                    for h in range(2):
                        def vmm(t, bank, st=st, h=h):
                            inst = None
                            for c in range(8):
                                inst = t.matmul(bank[:, 0:512],
                                                z3s[:, 2048 * c + 128 * st:2048 * c + 128 * (st + 1)],
                                                wbig[:, 1024 * c + 512 * h:1024 * c + 512 * (h + 1)],
                                                start=(c == 0), stop=(c == 7))
                            return inst

                        def vev(a, bank, st=st, h=h):
                            return a.activation(qkvB[:, 1024 * st + 512 * h:1024 * st + 512 * (h + 1)],
                                                bank[:, 0:512], AF.Identity)
                        w = []
                        if st == 0 and h == 0:
                            w = [("G", mG_w), ("V", sc.tick["V"]), ("O", sc.tick["O"])]
                        mT_proj_last, mA_p = grp("S", vmm, "A", vev, waits=w)
                wt("G", "A", mA_p)
                sig("O", lambda gp: gp.dma_start(
                    out=Vo.rearrange("(s p) j -> p s j", p=128),
                    in_=qkvB[:].rearrange("p (s j) -> p s j", s=16)))

        wt("G", "O", sc.tick["O"])
        wt("G", "T", sc.tick["T"])
        wt("G", "V", sc.tick["V"])
        wt("G", "A", sc.tick["A"])
        sig("G", lambda gp: gp.drain())
        sc.finalize(block)
    return nc


def _xpos_tabs(start, downscale):
    half = C // 2
    j = np.arange(half, dtype=np.float64)
    scale_vec = (2.0 * j + 0.4 * C) / (1.4 * C)
    pos = np.arange(start, start + TPC, dtype=np.float64)
    scale = scale_vec[None, :] ** (pos[:, None] / 512.0)
    if downscale:
        scale = 1.0 / scale
    inv_freq = 1.0 / 10000.0 ** (j / half)
    ang = pos[:, None] * inv_freq[None, :]
    gam = GAMMA ** ((pos % 128.0) if not downscale else -(pos % 128.0))
    cs = np.cos(ang) * scale * gam[:, None]
    sn = np.sin(ang) * scale * gam[:, None]
    cs = cs.T.reshape(4, 128, TPC).transpose(1, 0, 2).reshape(128, 4 * TPC)
    sn = sn.T.reshape(4, 128, TPC).transpose(1, 0, 2).reshape(128, 4 * TPC)
    z = np.zeros((128, 8192), np.float32)
    z[:, 0:4 * TPC] = cs
    z2_ = np.zeros((128, 8192), np.float32)
    z2_[:, 0:4 * TPC] = sn
    return np.ascontiguousarray(z), np.ascontiguousarray(z2_)


def host_pD_inputs(inputs, xres, z3, experts):
    maps = []
    deint = lambda W: np.ascontiguousarray(
        np.concatenate([np.asarray(W, np.float32)[:, 0::2], np.asarray(W, np.float32)[:, 1::2]], axis=1))
    WqD = deint(inputs["L1_attn_W_Q"])
    WkD = deint(inputs["L1_attn_W_K"])
    WvI = np.asarray(inputs["L1_attn_W_V"], np.float32)
    identI = np.ascontiguousarray(np.eye(128, dtype=bf))
    for c in range(NC):
        b, q = c // 4, c % 4
        s0 = 2048 * q
        ei = experts[b]
        W1 = np.asarray(inputs["L0_ffn_W1"][ei], np.float32)
        W2 = np.asarray(inputs["L0_ffn_W2"][ei], np.float32)
        b1 = np.asarray(inputs["L0_ffn_b1"][ei], np.float32)
        b2 = np.asarray(inputs["L0_ffn_b2"][ei], np.float32)
        csQ, snQ = _xpos_tabs(s0, False)
        csK, snK = _xpos_tabs(s0, True)
        rows = slice(8192 * b + s0, 8192 * b + s0 + 2048)
        maps.append({
            "xresI": np.ascontiguousarray(np.asarray(xres, np.float32)[rows]),
            "z3TI": np.ascontiguousarray(np.asarray(z3, bf)[rows].T),
            "W1e": W1, "W2e": W2,
            "b1CI": np.ascontiguousarray(b1.reshape(32, 128).T),
            "b2BI": np.ascontiguousarray(np.broadcast_to(b2[None, :], (128, C))),
            "WqD": WqD, "WkD": WkD, "WvI": WvI,
            "csQI": csQ, "snQI": snQ, "csKI": csK, "snKI": snK,
            "identI": identI,
        })
    return maps


def run_pD(inputs, xres, z3, experts):
    nc = build_pD()
    res = _run(nc, host_pD_inputs(inputs, xres, z3, experts))
    x2 = np.concatenate([np.asarray(r["x2o"]) for r in res.results], 0)
    QT = [np.asarray(r["QTo"]) for r in res.results]
    KT = [np.asarray(r["KTo"]) for r in res.results]
    V = [np.asarray(r["Vo"]) for r in res.results]
    return x2, QT, KT, V



EWIN = 27
EKT = 4 * 15 + EWIN


def build_pE():
    nc = _bass()
    QTi = nc.dram_tensor("QTi", [C, TPC], BF16, kind="ExternalInput").ap()
    KTw = nc.dram_tensor("KTw", [C, EKT * 128], BF16, kind="ExternalInput").ap()
    Vw = nc.dram_tensor("Vw", [EKT * 128, C], BF16, kind="ExternalInput").ap()
    mleI = nc.dram_tensor("mleI", [128, 128], F32, kind="ExternalInput").ap()
    Eo = nc.dram_tensor("Eo", [TPC, C], F32, kind="ExternalOutput").ap()

    from contextlib import ExitStack
    with ExitStack() as stack:
        e = stack.enter_context
        qS = e(nc.sbuf_tensor("qS", [128, 8 * 2048], BF16))
        outacc = e(nc.sbuf_tensor("outacc", [128, 16 * 1024], F32))
        kG = [e(nc.sbuf_tensor(f"kG{j}", [128, 8 * 1024], BF16)) for j in range(2)]
        vG = [e(nc.sbuf_tensor(f"vG{j}", [128, 8 * 1024], BF16)) for j in range(2)]
        scB = e(nc.sbuf_tensor("scB", [128, 2 * 8 * 128], BF16))
        mleT = e(nc.sbuf_tensor("mleT", [128, 128], F32))
        pS = [e(nc.psum_tensor(f"pS{j}", [128, 128], F32)) for j in range(2)]
        pA = [e(nc.psum_tensor(f"pA{j}", [128, 512], F32)) for j in range(4)]
        sc = Sched(stack, nc, {
            "T": ("T", 1, 4000), "V": ("V", 1, 4000), "A": ("A", 1, 100),
            "D": ("S", 16, 200), "O": ("G", 16, 40), "G": ("G", 1, 10),
        }, {"P": pS, "AV": pA})
        sig, wt, run, grp = sc.sig, sc.wt, sc.run, sc.grp
        block = e(nc.Block())

        mD = sig("D", lambda s: s.dma_start(out=mleT[:], in_=mleI))
        mD = sig("D", lambda s: s.dma_start(
            out=qS[:].rearrange("p (c t) -> p c t", c=8),
            in_=QTi.rearrange("(c p) t -> p c t", p=128)))
        mD_q = mD

        def vinit(v):
            v.memset(outacc[:], 0.0)
            return v.drain()
        mV_init = sig("V", vinit)

        NKG = (EKT + 7) // 8
        mT_kv = [0, 0]
        mD_kv = [0, 0]
        for kg in range(NKG):
            gsl = kg % 2
            j0 = 8 * kg
            nj = min(8, EKT - j0)
            if mT_kv[gsl]:
                wt("S", "T", mT_kv[gsl])
            for jr in range(nj):
                mD_kv[gsl] = sig("D", lambda s, j=j0 + jr, jr=jr, gsl=gsl: s.dma_start(
                    out=kG[gsl][:, 1024 * jr:1024 * (jr + 1)].rearrange("p (c t) -> p c t", c=8),
                    in_=KTw.rearrange("(c p) t -> p c t", p=128)[:, :, 128 * j:128 * (j + 1)]))
                mD_kv[gsl] = sig("D", lambda s, j=j0 + jr, jr=jr, gsl=gsl: s.dma_start(
                    out=vG[gsl][:, 1024 * jr:1024 * (jr + 1)],
                    in_=Vw[128 * j:128 * (j + 1), :]))
            qlo = max(0, -(-(j0 - 26) // 4))
            qhi = min(15, (j0 + 7) // 4)
            for qi in range(qlo, qhi + 1):
                js = [j for j in range(j0, j0 + nj) if 4 * qi <= j <= 4 * qi + 26]
                if not js:
                    continue
                mV_sc = 0
                for ji, j in enumerate(js):
                    dlt = 4 * qi + 26 - j
                    g64 = float(GAMMA ** (64.0 * dlt))

                    def smm(t, bank, j=j, qi=qi, gsl=gsl, j0=j0):
                        inst = None
                        for c in range(8):
                            inst = t.matmul(bank[:, 0:128],
                                            kG[gsl][:, 1024 * (j - j0) + 128 * c:1024 * (j - j0) + 128 * (c + 1)],
                                            qS[:, 2048 * c + 128 * qi:2048 * c + 128 * (qi + 1)],
                                            start=(c == 0), stop=(c == 7))
                        return inst

                    def sev(v, bank, ji=ji, dlt=dlt, g64=g64, gsl=gsl):
                        dst = scB[:, 1024 * gsl + 128 * ji:1024 * gsl + 128 * (ji + 1)]
                        if dlt == 0:
                            v.tensor_tensor(dst, bank[:], mleT[:], ALU.mult)
                        else:
                            v.tensor_scalar(dst, bank[:], g64, g64, ALU.mult, ALU.mult)
                        return v.drain()
                    w = []
                    if kg == 0 and qi == qlo and ji == 0:
                        w = [("D", mD_q), ("D", mD)]
                    if ji == 0:
                        w = w + [("D", mD_kv[gsl])]
                    _, mV_sc = grp("P", smm, "V", sev, waits=w)
                for h in range(2):
                    def avmm(t, bank, js=js, j0=j0, qi=qi, h=h, gsl=gsl):
                        inst = None
                        for ji in range(len(js)):
                            inst = t.matmul(bank[:, 0:512],
                                            scB[:, 1024 * gsl + 128 * ji:1024 * gsl + 128 * (ji + 1)],
                                            vG[gsl][:, 1024 * (js[ji] - j0) + 512 * h:1024 * (js[ji] - j0) + 512 * (h + 1)],
                                            start=(ji == 0), stop=(ji == len(js) - 1))
                        return inst

                    def avev(v, bank, qi=qi, h=h):
                        s = outacc[:, 1024 * qi + 512 * h:1024 * qi + 512 * (h + 1)]
                        return v.tensor_tensor(s, s, bank[:, 0:512], ALU.add)
                    w = [("V", mV_sc)] if h == 0 else []
                    if kg == 0 and qi == qlo and h == 0:
                        w.append(("V", mV_init))
                    mT_avg, _ = grp("AV", avmm, "V", avev, waits=w)
                mT_kv[gsl] = mT_avg

        wt("G", "V", sc.tick["V"])
        for qi in range(16):
            sig("O", lambda gp, qi=qi: gp.dma_start(
                out=Eo[128 * qi:128 * (qi + 1), :], in_=outacc[:, 1024 * qi:1024 * (qi + 1)]))
        wt("G", "O", sc.tick["O"])
        wt("G", "T", sc.tick["T"])
        sig("G", lambda gp: gp.drain())
        sc.finalize(block)
    return nc


def run_pE(QT, KT, V):
    nc = build_pE()
    mle = (np.arange(128)[:, None] <= np.arange(128)[None, :]).astype(np.float32)
    maps = []
    for c in range(NC):
        b, q = c // 4, c % 4
        KTb = np.concatenate([KT[4 * b + j] for j in range(4)], axis=1)
        Vb = np.concatenate([V[4 * b + j] for j in range(4)], axis=0)
        QTb = np.concatenate([QT[4 * b + j] for j in range(4)], axis=1)
        qcols = np.concatenate([QTb[:, 128 * (q + 4 * i):128 * (q + 4 * i + 1)]
                                for i in range(16)], axis=1)
        KTw = np.zeros((C, EKT * 128), bf)
        Vw = np.zeros((EKT * 128, C), bf)
        for j in range(EKT):
            gt = q - 26 + j
            if 0 <= gt < 64:
                KTw[:, 128 * j:128 * (j + 1)] = KTb[:, 128 * gt:128 * (gt + 1)]
                Vw[128 * j:128 * (j + 1), :] = Vb[128 * gt:128 * (gt + 1), :]
        maps.append({"QTi": np.ascontiguousarray(qcols), "KTw": KTw, "Vw": Vw, "mleI": mle})
    res = _run(nc, maps)
    ret = np.zeros((B, T, C), np.float32)
    for c in range(NC):
        b, q = c // 4, c % 4
        eo = np.asarray(res.results[c]["Eo"])
        for i in range(16):
            gt = q + 4 * i
            ret[b, 128 * gt:128 * (gt + 1)] = eo[128 * i:128 * (i + 1)]
    return ret.reshape(B * T, C)



NEXP = 16


def build_pF():
    nc = _bass()
    x2I = nc.dram_tensor("x2I", [TPC, C], F32, kind="ExternalInput").ap()
    retI = nc.dram_tensor("retI", [TPC, C], F32, kind="ExternalInput").ap()
    eW1s = nc.dram_tensor("eW1s", [NEXP * C, C], F32, kind="ExternalInput").ap()
    eW2s = nc.dram_tensor("eW2s", [NEXP * C, C], F32, kind="ExternalInput").ap()
    eb1C = nc.dram_tensor("eb1C", [128, 8 * NEXP], F32, kind="ExternalInput").ap()
    eb2B = nc.dram_tensor("eb2B", [128, NEXP * C], F32, kind="ExternalInput").ap()
    rWp = nc.dram_tensor("rWp", [C, 16], F32, kind="ExternalInput").ap()
    rbB = nc.dram_tensor("rbB", [128, 16], F32, kind="ExternalInput").ap()
    identI = nc.dram_tensor("identI", [128, 128], BF16, kind="ExternalInput").ap()
    x4o = nc.dram_tensor("x4o", [TPC, C], F32, kind="ExternalOutput").ap()

    from contextlib import ExitStack
    with ExitStack() as stack:
        e = stack.enter_context
        x4acc = e(nc.sbuf_tensor("x4acc", [128, 16 * 1024], F32))
        z5T = e(nc.sbuf_tensor("z5T", [128, 8 * 2048], BF16))
        hE = e(nc.sbuf_tensor("hE", [128, 8 * 2048], BF16))
        ew1b = [e(nc.sbuf_tensor(f"ew1b{j}", [128, 8192], BF16)) for j in range(2)]
        ew2 = e(nc.sbuf_tensor("ew2", [128, 8192], BF16))
        x2t = e(nc.sbuf_tensor("x2t", [128, 2 * 1024], F32))
        rtt = e(nc.sbuf_tensor("rtt", [128, 2 * 1024], F32))
        z5t = e(nc.sbuf_tensor("z5t", [128, 2 * 1024], BF16))
        b1c = e(nc.sbuf_tensor("b1c", [128, 8 * NEXP], F32))
        b2s = e(nc.sbuf_tensor("b2s", [128, 1024], BF16))
        cwA = e(nc.sbuf_tensor("cwA", [128, 256], F32))
        onesPF = e(nc.sbuf_tensor("onesPF", [128, 128], BF16))
        rwb = e(nc.sbuf_tensor("rwb", [128, 8 * 16], BF16))
        rbb = e(nc.sbuf_tensor("rbb", [128, 16], F32))
        lg = e(nc.sbuf_tensor("lg", [128, 16], F32))
        pr = e(nc.sbuf_tensor("pr", [128, 16], F32))
        prm = e(nc.sbuf_tensor("prm", [128, 16], F32))
        ohx = e(nc.sbuf_tensor("ohx", [128, 16], F32))

        mx = e(nc.sbuf_tensor("mx", [128, 1], F32))
        nmx = e(nc.sbuf_tensor("nmx", [128, 1], F32))
        ssum = e(nc.sbuf_tensor("ssum", [128, 1], F32))
        identT = e(nc.sbuf_tensor("identT", [128, 128], BF16))
        scr = e(nc.sbuf_tensor("scr", [128, 1024], F32))
        sm = e(nc.sbuf_tensor("sm", [128, 1], F32))
        mean = e(nc.sbuf_tensor("mean", [128, 1], F32))
        var = e(nc.sbuf_tensor("var", [128, 1], F32))
        ur = e(nc.sbuf_tensor("ur", [128, 1], F32))
        pM = [e(nc.psum_tensor(f"pM{j}", [128, 512], F32)) for j in range(6)]
        pX = [e(nc.psum_tensor("pX0", [128, 128], BF16))]
        pR = [e(nc.psum_tensor("pR0", [128, 16], F32))]
        sc = Sched(stack, nc, {
            "T": ("T", 1, 5200), "V": ("V", 1, 3000), "A": ("A", 1, 4000),
            "D": ("S", 16, 60), "G": ("G", 16, 150), "O": ("G", 16, 60),
        }, {"S": pM, "X": pX, "R": pR})
        sig, wt, run, grp = sc.sig, sc.wt, sc.run, sc.grp
        block = e(nc.Block())

        run("A", lambda a: a.preload_act_table(AF.Sqrt))
        run("A", lambda a: a.drain())
        sig("V", lambda v: (v.memset(onesPF[:], 1.0), v.drain())[1])
        mD = sig("D", lambda s: s.dma_start(out=identT[:], in_=identI))
        mD = sig("D", lambda s: s.dma_start(out=b1c[:], in_=eb1C))
        mD = sig("D", lambda s: s.dma_start(out=rbb[:], in_=rbB))
        mD_small = mD
        mG_rw = sig("G", lambda gp: gp.dma_start(
            out=rwb[:].rearrange("p (c j) -> p c j", c=8),
            in_=rWp.rearrange("(c p) j -> p c j", p=128)))

        mT_tr = 0
        mT_trs = [0, 0]
        mV_ln = 0
        for ti in range(16):
            sl = ti % 2
            if ti >= 2:
                wt("S", "V", mV_ln)
            mD_x = sig("D", lambda s, ti=ti, sl=sl: s.dma_start(
                out=x2t[:, 1024 * sl:1024 * (sl + 1)], in_=x2I[128 * ti:128 * (ti + 1), :]))
            mD_r = sig("D", lambda s, ti=ti, sl=sl: s.dma_start(
                out=rtt[:, 1024 * sl:1024 * (sl + 1)], in_=retI[128 * ti:128 * (ti + 1), :]))
            wt("V", "D", mD_r)

            def x3i(v, ti=ti, sl=sl):
                v.tensor_tensor(x4acc[:, 1024 * ti:1024 * (ti + 1)],
                                x2t[:, 1024 * sl:1024 * (sl + 1)],
                                rtt[:, 1024 * sl:1024 * (sl + 1)], ALU.add)
                return v.drain()
            sig("V", x3i)

            def lnstats(v, ti=ti):
                xv = x4acc[:, 1024 * ti:1024 * (ti + 1)]
                emit_stats1024(v, xv, scr[:], sm[:], var[:], mean[:])
                emit_inv(v, var[:], ur[:])
                return v.drain()
            mVs = sig("V", lnstats)
            wt("A", "V", mVs)
            mAs = sig("A", lambda a: a.activation(ur[:], ur[:], AF.Sqrt))
            wt("V", "A", mAs)
            if mT_trs[sl]:
                wt("V", "T", mT_trs[sl])

            def z5s(v, ti=ti, sl=sl):
                xv = x4acc[:, 1024 * ti:1024 * (ti + 1)]
                return v.tensor_scalar(z5t[:, 1024 * sl:1024 * (sl + 1)], xv, mean[:], ur[:],
                                       ALU.subtract, ALU.mult)
            mV_ln = sig("V", z5s)
            for c in range(8):
                def trmm(t, bank, sl=sl, c=c):
                    return t.transpose(bank[:, 0:128],
                                       z5t[:, 1024 * sl + 128 * c:1024 * sl + 128 * (c + 1)], identT[:])

                def trev(a, bank, c=c, ti=ti):
                    return a.activation(z5T[:, 2048 * c + 128 * ti:2048 * c + 128 * (ti + 1)],
                                        bank[:], AF.Identity)
                w = []
                if c == 0:
                    w = [("V", mV_ln)]
                    if ti == 0:
                        w.append(("D", mD_small))
                mT_tr, mA_tr = grp("X", trmm, "A", trev, waits=w)
            mT_trs[sl] = mT_tr

        run("A", lambda a: a.preload_act_table(AF.Exp))
        run("A", lambda a: a.drain())
        mO_cw = 0
        for ti in range(16):
            def rmm(t, bank, ti=ti):
                inst = None
                for c in range(8):
                    inst = t.matmul(bank[:, 0:16], z5T[:, 2048 * c + 128 * ti:2048 * c + 128 * (ti + 1)],
                                    rwb[:, 16 * c:16 * (c + 1)], start=(c == 0), stop=(c == 7))
                return inst

            def rev(v, bank, ti=ti):
                return v.tensor_tensor(lg[:], bank[:, 0:16], rbb[:], ALU.add)
            w = [("A", mA_tr), ("G", mG_rw)] if ti == 0 else []
            _, mV_lg = grp("R", rmm, "V", rev, waits=w)

            def sfm1(v):
                v.tensor_reduce(mx[:], lg[:], AX, ALU.max)
                v.drain()
                v.tensor_scalar(nmx[:], mx[:], -1.0, None, ALU.mult)
                return v.drain()
            mV_s1 = sig("V", sfm1)
            wt("A", "V", mV_s1)
            mA_e = sig("A", lambda a: a.activation(lg[:], lg[:], AF.Exp, bias=nmx[:, 0:1]))
            wt("V", "A", mA_e)

            def sfm2(v, ti=ti):
                v.tensor_reduce(ssum[:], lg[:], AX, ALU.add)
                v.drain()
                v.reciprocal(ssum[:], ssum[:])
                v.drain()
                v.tensor_scalar(pr[:], lg[:], ssum[:, 0:1], None, ALU.mult)
                v.drain()
                v.tensor_reduce(mx[:], pr[:], AX, ALU.max)
                v.drain()
                v.tensor_scalar(ohx[:], pr[:], mx[:, 0:1], None, ALU.is_ge)
                v.drain()
                v.tensor_tensor(scr[:, 0:16], ohx[:], pr[:], ALU.mult)
                v.drain()
                v.tensor_tensor(prm[:], pr[:], scr[:, 0:16], ALU.subtract)
                v.drain()
                v.tensor_reduce(mx[:], prm[:], AX, ALU.max)
                v.drain()
                v.tensor_scalar(scr[:, 16:32], prm[:], mx[:, 0:1], None, ALU.is_ge)
                v.drain()
                v.tensor_tensor(ohx[:], ohx[:], scr[:, 16:32], ALU.add)
                v.tensor_tensor(scr[:, 0:16], scr[:, 16:32], prm[:], ALU.mult)
                v.drain()
                v.tensor_tensor(prm[:], prm[:], scr[:, 0:16], ALU.subtract)
                v.drain()
                v.tensor_reduce(mx[:], prm[:], AX, ALU.max)
                v.drain()
                v.tensor_scalar(scr[:, 16:32], prm[:], mx[:, 0:1], None, ALU.is_ge)
                v.drain()
                v.tensor_tensor(ohx[:], ohx[:], scr[:, 16:32], ALU.add)
                v.drain()
                v.tensor_tensor(cwA[:, 16 * ti:16 * (ti + 1)], ohx[:], pr[:], ALU.mult)
                v.drain()
                v.memset(cwA[:, 16 * ti + 15:16 * ti + 16], 1.0)
                return v.drain()
            mV_cw = sig("V", sfm2)

        mT_e_last = 0
        m_f1 = [0, 0]
        for ex in range(NEXP):
            esl = ex % 2
            ew1 = ew1b[esl]
            if m_f1[esl]:
                wt("G", "T", m_f1[esl])
            mG_w1 = sig("G", lambda gp, ex=ex, ew1=ew1: gp.dma_start(
                out=ew1[:].rearrange("p (c j) -> p c j", c=8),
                in_=eW1s[C * ex:C * (ex + 1), :].rearrange("(c p) j -> p c j", p=128)))
            if ex > 0:
                wt("G", "T", mT_e_last)
            mG_w2a = sig("G", lambda gp, ex=ex: gp.dma_start(
                out=ew2[:, 0:4096].rearrange("p (c j) -> p c j", c=4),
                in_=eW2s[C * ex:C * ex + 512, :].rearrange("(c p) j -> p c j", p=128)))
            mG_w2b = sig("G", lambda gp, ex=ex: gp.dma_start(
                out=ew2[:, 4096:8192].rearrange("p (c j) -> p c j", c=4),
                in_=eW2s[C * ex + 512:C * (ex + 1), :].rearrange("(c p) j -> p c j", p=128)))
            mG_b2 = sig("G", lambda gp, ex=ex: gp.dma_start(
                out=b2s[:], in_=eb2B[:, C * ex:C * (ex + 1)]))
            wt("V", "G", mG_b2)
            if ex > 0:
                wt("V", "T", mT_e_last)

            def b2scale(v):
                v.tensor_scalar(b2s[:], b2s[:], 1.0 / 128.0, None, ALU.mult)
                return v.drain()
            mV_b2 = sig("V", b2scale)

            mA_h = 0
            for oc in range(8):
                for tt in range(4):
                    def f1mm(t, bank, oc=oc, tt=tt, ew1=ew1):
                        inst = None
                        for c in range(8):
                            inst = t.matmul(bank[:, 0:512],
                                            ew1[:, 1024 * c + 128 * oc:1024 * c + 128 * (oc + 1)],
                                            z5T[:, 2048 * c + 512 * tt:2048 * c + 512 * (tt + 1)],
                                            start=(c == 0), stop=(c == 7))
                        return inst

                    def f1ev(a, bank, oc=oc, tt=tt, ex=ex):
                        return a.activation(hE[:, 2048 * oc + 512 * tt:2048 * oc + 512 * (tt + 1)],
                                            bank[:, 0:512], AF.Relu, bias=b1c[:, 8 * ex + oc:8 * ex + oc + 1])
                    w = []
                    if oc == 0 and tt == 0:
                        w = [("G", mG_w1)]
                    mT_f1x, mA_h = grp("S", f1mm, "A", f1ev, waits=w)
            m_f1[esl] = mT_f1x
            for half in range(2):
                for st in range(16):
                    for h in range(2):
                        def f2mm_w(t, bank, st=st, h=h, half=half):
                            inst = None
                            for c4 in range(4):
                                c = 4 * half + c4
                                inst = t.matmul(bank[:, 0:512],
                                                hE[:, 2048 * c + 128 * st:2048 * c + 128 * (st + 1)],
                                                ew2[:, 4096 * half + 1024 * c4 + 512 * h:4096 * half + 1024 * c4 + 512 * (h + 1)],
                                                start=(c4 == 0), stop=(half == 0 and c4 == 3))
                            if half == 1:
                                inst = t.matmul(bank[:, 0:512], onesPF[:],
                                                b2s[:, 512 * h:512 * (h + 1)], start=False, stop=True)
                            return inst

                        def f2ev(v, bank, st=st, h=h, ex=ex):
                            s = x4acc[:, 1024 * st + 512 * h:1024 * st + 512 * (h + 1)]
                            return v.scalar_tensor_tensor(s, bank[:, 0:512],
                                                          cwA[:, 16 * st + ex:16 * st + ex + 1],
                                                          s, ALU.mult, ALU.add)
                        w = []
                        if st == 0 and h == 0:
                            if half == 0:
                                w = [("A", mA_h), ("V", mV_b2), ("G", mG_w2a)]
                            else:
                                w = [("G", mG_w2b)]
                        mT_e_last, mV_x4 = grp("S", f2mm_w, "V", f2ev, waits=w)

        wt("G", "V", sc.tick["V"])
        for ti in range(16):
            sig("O", lambda gp, ti=ti: gp.dma_start(
                out=x4o[128 * ti:128 * (ti + 1), :], in_=x4acc[:, 1024 * ti:1024 * (ti + 1)]))

        wt("G", "O", sc.tick["O"])
        wt("G", "T", sc.tick["T"])
        wt("G", "A", sc.tick["A"])
        sig("G", lambda gp: gp.drain())
        sc.finalize(block)
    return nc


def run_pF(inputs, x2, ret):
    nc = build_pF()
    pre = lambda k: np.asarray(inputs["L1_ffn_" + k], np.float32)
    eW1 = np.concatenate([pre("eW1").reshape(15 * C, C), pre("sW1")], axis=0)
    eW2 = np.concatenate([pre("eW2").reshape(15 * C, C), pre("sW2")], axis=0)
    eb1 = np.concatenate([pre("eb1"), pre("sb1")[None]], axis=0)
    eb2 = np.concatenate([pre("eb2"), pre("sb2")[None]], axis=0)
    eb1C = np.concatenate([eb1[e].reshape(8, 128).T for e in range(NEXP)], axis=1)
    eb2B = np.concatenate([np.broadcast_to(eb2[e][None], (128, C)) for e in range(NEXP)], axis=1)
    rWp = np.zeros((C, 16), np.float32)
    rWp[:, 0:15] = pre("router_W")
    rbB = np.zeros((16,), np.float32)
    rbB[0:15] = pre("router_b")
    rbB[15] = -1e30
    rbB = np.ascontiguousarray(np.broadcast_to(rbB[None], (128, 16)))
    identI = np.ascontiguousarray(np.eye(128, dtype=bf))
    maps = []
    for c in range(NC):
        rows = slice(2048 * c, 2048 * (c + 1))
        maps.append({
            "x2I": np.ascontiguousarray(np.asarray(x2, np.float32)[rows]),
            "retI": np.ascontiguousarray(np.asarray(ret, np.float32)[rows]),
            "eW1s": eW1, "eW2s": eW2,
            "eb1C": np.ascontiguousarray(eb1C), "eb2B": np.ascontiguousarray(eb2B),
            "rWp": rWp, "rbB": rbB, "identI": identI,
        })
    res = _run(nc, maps)
    return np.concatenate([np.asarray(r["x4o"]) for r in res.results], 0)


def kernel(**inputs):
    xp, z2 = run_pA(inputs)
    og = run_pB(inputs, z2)
    xres, z3, zsum = run_pC(inputs, xp, og)
    ohs = run_pD2(inputs, zsum)
    experts = [int(np.argmax(np.asarray(oh))) for oh in ohs]
    x2, QT, KT, V = run_pD(inputs, xres, z3, experts)
    ret = run_pE(QT, KT, V)
    x4 = run_pF(inputs, x2, ret)
    return np.ascontiguousarray(x4.reshape(B, T, C).astype(np.float32))


# version 11 (final)
# speedup vs baseline: 1.1898x; 1.0381x over previous
import math
import numpy as np
import ml_dtypes
import concourse.bass as bass
import concourse.bacc as bacc
import concourse.mybir as mybir
from concourse.bass_utils import run_bass_kernel_spmd

F32 = mybir.dt.float32
BF16 = mybir.dt.bfloat16
AF = mybir.ActivationFunctionType
ALU = mybir.AluOpType
AX = mybir.AxisListType.X

B = 2
T = 8192
C = 1024
H = 16
N = 64
NC = 8
TPC = 2048
NT = TPC // 128
LNEPS = 1e-5
GNEPS = 0.00064
GAMMA = 0.96875
LOG_GAMMA = math.log(GAMMA)
CHUNK = 128
NCHUNK = T // CHUNK

bf = ml_dtypes.bfloat16


def _bass():
    return bass.Bass("TRN2", target_bir_lowering=False, detect_race_conditions=False)


def _run(nc, in_maps):
    mybir.codegen_inst_isa_subclasses(nc)
    return run_bass_kernel_spmd(nc, in_maps, core_ids=list(range(NC)))



class SemSeq:

    def __init__(self, stack, nc, name, inc, total):
        self.k = max(1, -(-(inc * total) // 240))
        self.sems = [stack.enter_context(nc.semaphore(f"{name}{j}")) for j in range(self.k)]
        self.inc = inc

    def attach(self, inst, i):
        inst.then_inc(self.sems[i % self.k], self.inc)
        return inst

    def wait(self, eng, i):
        if i < 0:
            return
        for j in range(self.k):
            cnt = (i - j) // self.k + 1 if i >= j else 0
            if cnt > 0:
                eng.wait_ge(self.sems[j], self.inc * cnt)


def emit_stats1024(v, xv, scr, sm, var, mean):
    v.tensor_tensor(scr, xv, xv, ALU.mult)
    v.reduce_sum(sm, xv, AX)
    v.drain()
    v.reduce_sum(var, scr, AX)
    v.tensor_scalar(mean, sm, 1.0 / 1024.0, None, ALU.mult)
    v.drain()
    v.tensor_scalar(var, var, 1.0 / 1024.0, None, ALU.mult)
    v.tensor_tensor(sm, mean, mean, ALU.mult)
    v.drain()
    v.tensor_tensor(var, var, sm, ALU.subtract)
    v.drain()


def emit_invsq(v, var, ur, vt):
    v.tensor_scalar(ur, var, float(LNEPS), None, ALU.add)
    v.drain()
    v.reciprocal(vt, ur)
    v.drain()
    v.tensor_tensor(vt, vt, var, ALU.mult)
    v.drain()
    v.tensor_scalar(vt, vt, float(LNEPS), None, ALU.add)
    v.drain()
    v.tensor_tensor(vt, vt, ur, ALU.mult)
    v.drain()
    v.reciprocal(ur, vt)
    v.drain()


def emit_inv(v, var, ur):
    v.tensor_scalar(ur, var, float(LNEPS), None, ALU.add)
    v.drain()
    v.reciprocal(ur, ur)
    v.drain()




def build_pA():
    nc = _bass()
    x = nc.dram_tensor("x", [TPC, C], F32, kind="ExternalInput").ap()
    lnpw = nc.dram_tensor("lnpw", [128, C], F32, kind="ExternalInput").ap()
    lnpb = nc.dram_tensor("lnpb", [128, C], F32, kind="ExternalInput").ap()
    xp = nc.dram_tensor("xp", [TPC, C], F32, kind="ExternalOutput").ap()
    z2 = nc.dram_tensor("z2", [TPC, C], BF16, kind="ExternalOutput").ap()

    from contextlib import ExitStack
    with ExitStack() as stack:
        e = stack.enter_context
        xt = e(nc.sbuf_tensor("xt", [128, 2 * C], F32))
        wt = e(nc.sbuf_tensor("wt", [128, C], F32))
        bt = e(nc.sbuf_tensor("bt", [128, C], F32))
        xpt = e(nc.sbuf_tensor("xpt", [128, 2 * C], F32))
        z2t = e(nc.sbuf_tensor("z2t", [128, 2 * C], BF16))
        scr = e(nc.sbuf_tensor("scr", [128, C], F32))
        sm = e(nc.sbuf_tensor("sm", [128, 1], F32))
        mean = e(nc.sbuf_tensor("mean", [128, 1], F32))
        var = e(nc.sbuf_tensor("var", [128, 1], F32))
        ur1 = e(nc.sbuf_tensor("ur1", [128, 1], F32))
        ur2 = e(nc.sbuf_tensor("ur2", [128, 1], F32))
        vt = e(nc.sbuf_tensor("vt", [128, 1], F32))
        sl = SemSeq(stack, nc, "sl", 16, NT)
        so = SemSeq(stack, nc, "so", 16, 2 * NT)
        sb = e(nc.semaphore("sb"))
        sv = e(nc.semaphore("sv"))
        sa = e(nc.semaphore("sa"))
        block = e(nc.Block())

        @block.sync
        def _(sync):
            for i in range(NT):
                if i >= 2:
                    sync.wait_ge(sv, 4 * (i - 2) + 3)
                sl.attach(sync.dma_start(out=xt[:, (i % 2) * C:(i % 2 + 1) * C],
                                         in_=x[128 * i:128 * (i + 1), :]), i)

        @block.vector
        def _(vector):
            for i in range(NT):
                sli = slice((i % 2) * C, (i % 2 + 1) * C)
                sl.wait(vector, i)
                if i == 0:
                    vector.wait_ge(sb, 32)
                if i >= 2:
                    so.wait(vector, 2 * (i - 2) + 1)
                xv = xt[:, sli]
                emit_stats1024(vector, xv, scr[:], sm[:], var[:], mean[:])
                emit_invsq(vector, var[:], ur1[:], vt[:])
                vector.engine_nop().then_inc(sv, 1)
                vector.wait_ge(sa, 2 * i + 1)
                vector.tensor_scalar(xpt[:, sli], xv, mean[:], ur1[:], ALU.subtract, ALU.mult)
                vector.drain()
                vector.tensor_tensor(xpt[:, sli], xpt[:, sli], wt[:], ALU.mult)
                vector.drain()
                vector.tensor_tensor(xpt[:, sli], xpt[:, sli], bt[:], ALU.add).then_inc(sv, 1)
                vector.drain()
                emit_stats1024(vector, xpt[:, sli], scr[:], sm[:], var[:], mean[:])
                emit_inv(vector, var[:], ur2[:])
                vector.engine_nop().then_inc(sv, 1)
                vector.wait_ge(sa, 2 * i + 2)
                vector.tensor_scalar(z2t[:, sli], xpt[:, sli], mean[:], ur2[:],
                                     ALU.subtract, ALU.mult).then_inc(sv, 1)

        @block.scalar
        def _(scalar):
            scalar.preload_act_table(AF.Sqrt)
            for i in range(NT):
                scalar.wait_ge(sv, 4 * i + 1)
                scalar.activation(ur1[:], ur1[:], AF.Sqrt).then_inc(sa, 1)
                scalar.wait_ge(sv, 4 * i + 3)
                scalar.activation(ur2[:], ur2[:], AF.Sqrt).then_inc(sa, 1)

        @block.gpsimd
        def _(gpsimd):
            gpsimd.dma_start(out=wt[:], in_=lnpw).then_inc(sb, 16)
            gpsimd.dma_start(out=bt[:], in_=lnpb).then_inc(sb, 16)
            for i in range(NT):
                gpsimd.wait_ge(sv, 4 * i + 2)
                so.attach(gpsimd.dma_start(out=xp[128 * i:128 * (i + 1), :],
                                           in_=xpt[:, (i % 2) * C:(i % 2 + 1) * C]), 2 * i)
                gpsimd.wait_ge(sv, 4 * i + 4)
                so.attach(gpsimd.dma_start(out=z2[128 * i:128 * (i + 1), :],
                                           in_=z2t[:, (i % 2) * C:(i % 2 + 1) * C]), 2 * i + 1)
            so.wait(gpsimd, 2 * NT - 1)
    return nc


def run_pA(inputs):
    nc = build_pA()
    x = np.asarray(inputs["L0_attn_x"], np.float32).reshape(B * T, C)
    lnpw = np.asarray(inputs["L0_attn_ln_pre_w"], np.float32)
    lnpb = np.asarray(inputs["L0_attn_ln_pre_b"], np.float32)
    lnpw = np.ascontiguousarray(np.broadcast_to(lnpw[None, :], (128, C)))
    lnpb = np.ascontiguousarray(np.broadcast_to(lnpb[None, :], (128, C)))
    in_maps = [{"x": np.ascontiguousarray(x[TPC * c:TPC * (c + 1)]),
                "lnpw": lnpw, "lnpb": lnpb} for c in range(NC)]
    res = _run(nc, in_maps)
    xp = np.concatenate([np.asarray(r["xp"]) for r in res.results], 0)
    z2 = np.concatenate([np.asarray(r["z2"]) for r in res.results], 0)
    return xp, z2




SL = 1024
NSTRIP = T // SL
CPS = SL // CHUNK
PB_STAGE = 7




def _finalize_pb(block, prog):
    @block.sync
    def _(sy):
        for f in prog["S"]:
            f(sy)

    @block.tensor
    def _(t):
        for f in prog["T"]:
            f(t)

    @block.vector
    def _(v):
        for f in prog["V"]:
            f(v)

    @block.scalar
    def _(a):
        for f in prog["A"]:
            f(a)

    @block.gpsimd
    def _(g):
        for f in prog["G"]:
            f(g)


def build_pB():
    nc = _bass()
    z2T = nc.dram_tensor("z2T", [C, T + 1], BF16, kind="ExternalInput").ap()
    wcat = nc.dram_tensor("wcat", [C, 1024], F32, kind="ExternalInput").ap()
    mixcat = nc.dram_tensor("mixcat", [128, 48], F32, kind="ExternalInput").ap()
    lnwb = nc.dram_tensor("lnwb", [128, 16], F32, kind="ExternalInput").ap()
    w2sI = nc.dram_tensor("w2sI", [64, 256], F32, kind="ExternalInput").ap()
    a2sI = nc.dram_tensor("a2sI", [64, 256], F32, kind="ExternalInput").ap()
    g2sI = nc.dram_tensor("g2sI", [128, 256], F32, kind="ExternalInput").ap()
    smallc = nc.dram_tensor("smallc", [128, 10], F32, kind="ExternalInput").ap()
    gnwb = nc.dram_tensor("gnwb", [128, 512], F32, kind="ExternalInput").ap()
    og = nc.dram_tensor("og", [T, 256], BF16, kind="ExternalOutput").ap()
    vbx = nc.dram_tensor("vbx", [1, 256], F32, kind="Internal").ap()

    SLH = SL // 2
    NTT = SL // 128
    from contextlib import ExitStack
    with ExitStack() as stack:
        e = stack.enter_context
        mixc = e(nc.sbuf_tensor("mixc", [128, 48], F32))
        lnt = e(nc.sbuf_tensor("lnt", [128, 16], F32))
        smt = e(nc.sbuf_tensor("smt", [128, 10], F32))
        onemka = e(nc.sbuf_tensor("onemka", [128, 2], F32))
        gnt = e(nc.sbuf_tensor("gnt", [128, 512], F32))
        sab = e(nc.sbuf_tensor("sab", [128, 96], F32))
        w2t = e(nc.sbuf_tensor("w2t", [64, 256], BF16))
        a2t = e(nc.sbuf_tensor("a2t", [128, 256], BF16))
        g2t = e(nc.sbuf_tensor("g2t", [128, 256], BF16))
        biasc = e(nc.sbuf_tensor("biasc", [128, 6], F32))
        vbrow = e(nc.sbuf_tensor("vbrow", [1, 256], F32))
        vbrowB = e(nc.sbuf_tensor("vbrowB", [128, 256], F32))
        vbrowS = e(nc.sbuf_tensor("vbrowS", [128, 256], BF16))
        lnbbB = e(nc.sbuf_tensor("lnbbB", [128, 8 * 128], BF16))
        onesF = e(nc.sbuf_tensor("onesF", [128, 128], F32))
        onesB = e(nc.sbuf_tensor("onesB", [128, 128], BF16))
        blkones = e(nc.sbuf_tensor("blkones", [128, 128], BF16))
        ident = e(nc.sbuf_tensor("ident", [128, 128], BF16))
        mlt = e(nc.sbuf_tensor("mlt", [128, 128], F32))
        mgt = e(nc.sbuf_tensor("mgt", [128, 128], F32))
        mle = e(nc.sbuf_tensor("mle", [128, 128], F32))
        ind64 = e(nc.sbuf_tensor("ind64", [128, 64], BF16))
        wtmpb = e(nc.sbuf_tensor("wtmpb", [128, 8 * 1024], BF16))
        wzs = e(nc.sbuf_tensor("wzs", [128, 8 * 1024], BF16))
        wz = wtmpb
        s0f = e(nc.sbuf_tensor("s0f", [128, 128], F32))
        s0b = e(nc.sbuf_tensor("s0b", [128, 128], BF16))
        sTmp = e(nc.sbuf_tensor("sTmp", [128, 128], F32))
        zst = e(nc.sbuf_tensor("zst", [128, 8 * (SL + 1)], BF16))
        rS = e(nc.sbuf_tensor("rS", [128, 2 * SL], F32))
        kS = e(nc.sbuf_tensor("kS", [128, 2 * SL], F32))
        dsig = e(nc.sbuf_tensor("dsig", [128, 2 * SL], F32))
        asig = e(nc.sbuf_tensor("asig", [128, 2 * SL], F32))
        zwza = e(nc.sbuf_tensor("zwza", [128, SL], BF16))
        zg = e(nc.sbuf_tensor("zg", [128, SL], BF16))
        vS = e(nc.sbuf_tensor("vS", [128, 8 * 256], BF16))
        gS = e(nc.sbuf_tensor("gS", [128, 8 * 256], BF16))
        ca = e(nc.sbuf_tensor("ca", [128, 2 * SL], F32))
        cprev = e(nc.sbuf_tensor("cprev", [128, 2 * SL], F32))
        cinv = e(nc.sbuf_tensor("cinv", [128, 2 * SL], F32))
        SCR1 = e(nc.sbuf_tensor("SCR1", [128, 2 * SL], F32))
        SCR2 = e(nc.sbuf_tensor("SCR2", [128, 2 * SL], F32))
        SCB1 = e(nc.sbuf_tensor("SCB1", [128, 2 * SL], BF16))
        atil = e(nc.sbuf_tensor("atil", [128, 2 * SL], BF16))
        btil = e(nc.sbuf_tensor("btil", [128, 2 * SL], BF16))
        ktil = e(nc.sbuf_tensor("ktil", [128, 2 * SL], BF16))
        rtil = e(nc.sbuf_tensor("rtil", [128, 2 * SL], BF16))
        rkpb = e(nc.sbuf_tensor("rkpb", [128, 2 * SL], BF16))
        ogS = e(nc.sbuf_tensor("ogS", [128, 8 * 256], BF16))
        mtS = e(nc.sbuf_tensor("mtS", [128, 512], BF16))
        mS = e(nc.sbuf_tensor("mS", [128, 512], BF16))
        gtS = e(nc.sbuf_tensor("gtS", [128, 512], BF16))
        d1S = e(nc.sbuf_tensor("d1S", [128, 512], BF16))
        d2S = e(nc.sbuf_tensor("d2S", [128, 512], BF16))
        pwT = [e(nc.sbuf_tensor(f"pwT{j}", [128, 512], BF16)) for j in range(2)]
        pwU = [e(nc.sbuf_tensor(f"pwU{j}", [128, 512], BF16)) for j in range(2)]
        uB = e(nc.sbuf_tensor("uB", [128, 256], BF16))
        bkT = e(nc.sbuf_tensor("bkT", [128, 512], BF16))
        gnm = e(nc.sbuf_tensor("gnm", [128, 4], F32))
        gns = e(nc.sbuf_tensor("gns", [128, 4], F32))
        gnr = e(nc.sbuf_tensor("gnr", [128, 4], F32))
        rksum = e(nc.sbuf_tensor("rksum", [128, 4], F32))
        ogchunk = e(nc.sbuf_tensor("ogchunk", [128, 256], F32))
        oq = e(nc.sbuf_tensor("oq", [128, 256], F32))
        pbS = [e(nc.psum_tensor(f"pbS{j}", [128, 512], F32)) for j in range(2)]
        pbV = [e(nc.psum_tensor("pbV0", [128, 256], F32))]
        pbP = [e(nc.psum_tensor(f"pbP{j}", [128, 128], F32)) for j in range(2)]
        pbQ = [e(nc.psum_tensor(f"pbQ{j}", [128, 64], F32)) for j in range(2)]
        pbX = [e(nc.psum_tensor("pbX0", [128, 128], BF16))]

        sqD = SemSeq(stack, nc, "sqD", 16, NSTRIP + 8)
        sqO = SemSeq(stack, nc, "sqO", 16, NSTRIP + 2)
        sqT = SemSeq(stack, nc, "sqT", 1, 8500)
        sqV = SemSeq(stack, nc, "sqV", 1, 6500)
        sqA = SemSeq(stack, nc, "sqA", 1, 3500)
        sqG = SemSeq(stack, nc, "sqG", 16, 12)
        sqC = SemSeq(stack, nc, "sqC", 1, 8)
        block = e(nc.Block())

        tick = {"T": 0, "V": 0, "A": 0, "G": 0, "D": 0, "O": 0, "C": 0}
        seqs = {"T": sqT, "V": sqV, "A": sqA, "G": sqG, "D": sqD, "O": sqO, "C": sqC}
        prog = {"S": [], "T": [], "V": [], "A": [], "G": []}

        def emit(engn, fn):
            prog[engn].append(fn)

        def sig(stream, fn):
            i = tick[stream]
            tick[stream] += 1
            engn = {"D": "S", "O": "G", "C": "G"}.get(stream, stream)
            sq = seqs[stream]

            def g(eng, fn=fn, i=i, sq=sq):
                sq.attach(fn(eng), i)
            emit(engn, g)
            return i + 1

        def wt(engn, stream, count):
            if count <= 0:
                return
            i = count - 1
            sq = seqs[stream]
            if stream in ("D", "O", "G"):
                def g(eng, sq=sq, i=i):
                    sq.wait(eng, i)
            else:
                def g(eng, sq=sq, i=i):
                    eng.wait_ge(sq.sems[i % sq.k], sq.inc * (i // sq.k + 1))
            emit(engn, g)

        def run(engn, fn):
            emit(engn, fn)

        pools = {"S": pbS, "V": pbV, "P": pbP, "Q": pbQ, "X": pbX}
        rot = {k: 0 for k in pools}
        blast = {}

        def grp(pool, mmfn, estream, efn, waits=()):
            bank = pools[pool][rot[pool] % len(pools[pool])]
            rot[pool] += 1
            bl = blast.get(bank.name)
            if bl:
                wt("T", bl[0], bl[1])
            for (stm, m) in waits:
                wt("T", stm, m)
            mT = sig("T", lambda t, mmfn=mmfn, bank=bank: mmfn(t, bank))
            wt(estream, "T", mT)
            mE = sig(estream, lambda v, efn=efn, bank=bank: efn(v, bank))
            blast[bank.name] = (estream, mE)
            return mT, mE

        def grp2(pool, mmfns, estream, efns, waits=()):
            mT = mE = 0
            for i, (f, ef) in enumerate(zip(mmfns, efns)):
                mT, mE = grp(pool, f, estream, ef, waits if i == 0 else ())
            return mT, mE

        mD = 0
        for dst, src in ((mixc, mixcat), (lnt, lnwb), (smt, smallc), (gnt, gnwb)):
            mD = sig("D", lambda s, dst=dst, src=src: s.dma_start(out=dst[:], in_=src))
        mD_small = mD
        mG = sig("G", lambda g: g.dma_start(
            out=wtmpb[:].rearrange("p (c j) -> p c j", c=8),
            in_=wcat.rearrange("(c p) j -> p c j", p=128)))
        mG = sig("G", lambda g: g.dma_start(out=w2t[:], in_=w2sI))
        mG = sig("G", lambda g: g.dma_start(out=a2t[64:128, :], in_=a2sI))
        mG = sig("G", lambda g: g.dma_start(out=g2t[:], in_=g2sI))
        mG_w = mG

        run("A", lambda a: a.preload_act_table(AF.Sigmoid))
        run("A", lambda a: a.drain())

        if PB_STAGE < 0.4:
            wt("G", "D", mD_small)
            sig("C", lambda g: g.drain())
            _finalize_pb(block, prog)
            return nc

        def const_build(v):
            v.memset(onesF[:], 1.0)
            v.memset(onesB[:], 1.0)
            v.memset(blkones[:], 0.0)
            v.memset(ind64[:], 0.0)
            v.memset(s0f[:], 0.0)
            v.memset(s0b[:], 0.0)
            v.drain()
            v.memset(blkones[0:64, 0:64], 1.0)
            v.memset(blkones[64:128, 64:128], 1.0)
            v.memset(ind64[0:64, 0:1], 1.0)
            v.memset(ind64[64:128, 1:2], 1.0)
            return v.drain()
        mV_const = sig("V", const_build)

        if PB_STAGE < 0.6:
            wt("G", "V", mV_const)
            sig("C", lambda g: g.drain())
            _finalize_pb(block, prog)
            return nc

        wt("G", "V", mV_const)

        def mask_build(g):
            g.affine_select(mlt[:], onesF[:], [[1, 128]], ALU.is_ge, 0.0,
                            base=-1, channel_multiplier=-1)
            g.affine_select(mgt[:], onesF[:], [[-1, 128]], ALU.is_ge, 0.0,
                            base=-1, channel_multiplier=1)
            g.affine_select(mle[:], onesF[:], [[1, 128]], ALU.is_ge, 0.0,
                            base=0, channel_multiplier=-1)
            g.affine_select(ident[:], onesB[:], [[-1, 128]], ALU.is_ge, 0.0,
                            base=0, channel_multiplier=1)
            g.drain()
            g.affine_select(ident[:], ident[:], [[1, 128]], ALU.is_ge, 0.0,
                            base=0, channel_multiplier=-1)
            return g.drain()
        mC_mask = sig("C", mask_build)

        if PB_STAGE < 0.8:
            wt("G", "C", mC_mask)
            sig("C", lambda g: g.drain())
            _finalize_pb(block, prog)
            return nc

        wt("V", "D", mD_small)
        wt("V", "G", mG_w)

        def wprep(v):
            for m in range(6):
                for c in range(8):
                    col = (m * 8 + c) * 2
                    v.tensor_tensor(sab[:, col + 1:col + 2], lnt[:, c:c + 1],
                                    mixc[:, m * 8 + c:m * 8 + c + 1], ALU.mult)
            v.drain()
            for m in range(6):
                for c in range(8):
                    col = (m * 8 + c) * 2
                    v.tensor_tensor(sab[:, col:col + 1], lnt[:, c:c + 1],
                                    sab[:, col + 1:col + 2], ALU.subtract)
            v.tensor_scalar(onemka[:], smt[:, 6:8], -1.0, 1.0, ALU.mult, ALU.add)
            for c in range(8):
                v.tensor_scalar(lnbbB[:, 128 * c:128 * (c + 1)], onesF[:],
                                lnt[:, 8 + c:9 + c], None, ALU.mult)
            return v.drain()
        mV_wprep = sig("V", wprep)

        if PB_STAGE < 0.9:
            wt("G", "V", mV_wprep)
            sig("C", lambda g: g.drain())
            _finalize_pb(block, prog)
            return nc

        def bias_mm(oc):
            def f(t, bank, oc=oc):
                inst = None
                for c in range(8):
                    inst = t.matmul(bank[:, 0:64],
                                    wtmpb[:, c * 1024 + 128 * oc:c * 1024 + 128 * (oc + 1)],
                                    lnbbB[:, 128 * c:128 * c + 64],
                                    start=(c == 0), stop=(c == 7))
                return inst
            return f

        def bias_ev(oc):
            def f(a, bank, oc=oc):
                return a.activation(biasc[:, oc:oc + 1], bank[:, 0:1], AF.Identity)
            return f

        mT_bias = 0
        for oc in range(6):
            w = [("V", mV_wprep)] if oc == 0 else []
            mT_bias, mA_bias = grp("Q", bias_mm(oc), "A", bias_ev(oc), waits=w)

        def vb_mm(t, bank):
            inst = None
            for c in range(8):
                inst = t.matmul(bank[:, 0:256], lnbbB[:, 128 * c:128 * (c + 1)],
                                wtmpb[:, c * 1024 + 768:c * 1024 + 1024],
                                start=(c == 0), stop=(c == 7))
            return inst

        def vb_ev(v, bank):
            v.tensor_copy(vbrow[:], bank[0:1, 0:256])
            return v.drain()
        mT_vb, mV_vb = grp("V", vb_mm, "V", vb_ev)

        wt("G", "V", mV_vb)
        mO_vb = sig("O", lambda g: g.dma_start(out=vbx, in_=vbrow[:]))
        wt("G", "O", mO_vb)
        mG_vbB = sig("G", lambda g: g.dma_start(out=vbrowB[:], in_=vbx.partition_broadcast(128)))
        wt("V", "G", mG_vbB)
        wt("V", "T", mT_vb)

        def wscale(v):
            regions = [(0, 256, 0), (256, 512, 1), (512, 576, 2), (576, 640, 3),
                       (640, 768, 4), (768, 1024, 5)]
            v.tensor_scalar(vbrowS[:], vbrowB[:], 1.0 / 128.0, None, ALU.mult)
            for c in range(8):
                base = c * 1024
                for lo, hi, m in regions:
                    col = (m * 8 + c) * 2
                    v.tensor_scalar(wzs[:, base + lo:base + hi], wtmpb[:, base + lo:base + hi],
                                    sab[:, col + 1:col + 2], None, ALU.mult)
            v.drain()
            for c in range(8):
                base = c * 1024
                for lo, hi, m in regions:
                    col = (m * 8 + c) * 2
                    v.tensor_scalar(wtmpb[:, base + lo:base + hi], wtmpb[:, base + lo:base + hi],
                                    sab[:, col:col + 1], None, ALU.mult)
            return v.drain()
        mV_wz = sig("V", wscale)

        if PB_STAGE < 0.99:
            wt("G", "V", mV_wz)
            sig("C", lambda g: g.drain())
            _finalize_pb(block, prog)
            return nc

        OCD = [0, 128, 256, 384, 512, 640]
        mD_z = [0] * NSTRIP
        zdone = [0] * NSTRIP
        ogall = [0] * NSTRIP
        m_chT_last = 0
        mSb = mV_const
        m_v7 = 0
        for s in range(NSTRIP if PB_STAGE >= 1 else 0):
            t0 = SL * s
            if s > 0:
                wt("S", "T", zdone[s - 1])
            mD_z[s] = sig("D", lambda sy, t0=t0: sy.dma_start(
                out=zst[:].rearrange("p (c j) -> p c j", c=8),
                in_=z2T.rearrange("(c p) t -> p c t", p=128)[:, :, t0:t0 + SL + 1]))

            if s > 0:
                run("A", lambda a: a.preload_act_table(AF.Sigmoid))
                run("A", lambda a: a.drain())

            ev_zwza = ev_zg = 0
            for oc in range(6):
                for nt in range(2):
                    off = OCD[oc]

                    def s1mm(t, bank, off=off, nt=nt):
                        for c in range(8):
                            t.matmul(bank[:, 0:SLH], wz[:, c * 1024 + off:c * 1024 + off + 128],
                                     zst[:, c * (SL + 1) + 1 + SLH * nt:c * (SL + 1) + 1 + SLH * nt + SLH],
                                     start=(c == 0), stop=False)
                        inst = None
                        for c in range(8):
                            inst = t.matmul(bank[:, 0:SLH], wzs[:, c * 1024 + off:c * 1024 + off + 128],
                                            zst[:, c * (SL + 1) + SLH * nt:c * (SL + 1) + SLH * nt + SLH],
                                            start=False, stop=(c == 7))
                        return inst

                    if oc < 2:
                        def s1ev(a, bank, oc=oc, nt=nt):
                            return a.activation(rS[:, oc * SL + SLH * nt:oc * SL + SLH * nt + SLH],
                                                bank[:, 0:SLH], AF.Identity, bias=biasc[:, oc:oc + 1])
                    elif oc < 4:
                        def s1ev(a, bank, oc=oc, nt=nt):
                            return a.activation(kS[:, (oc - 2) * SL + SLH * nt:(oc - 2) * SL + SLH * nt + SLH],
                                                bank[:, 0:SLH], AF.Identity, bias=biasc[:, oc:oc + 1])
                    elif oc == 4:
                        def s1ev(a, bank, nt=nt):
                            a.activation(zwza[0:64, SLH * nt:SLH * nt + SLH], bank[0:64, 0:SLH],
                                         AF.Tanh, bias=biasc[0:64, 4:5])
                            return a.activation(zwza[64:128, SLH * nt:SLH * nt + SLH], bank[64:128, 0:SLH],
                                                AF.Identity, bias=biasc[64:128, 4:5])
                    else:
                        def s1ev(a, bank, nt=nt):
                            return a.activation(zg[:, SLH * nt:SLH * nt + SLH], bank[:, 0:SLH],
                                                AF.Sigmoid, bias=biasc[:, 5:6])
                    w = []
                    if oc == 0 and nt == 0:
                        w = [("D", mD_z[s]), ("V", mV_wz)]
                        if s > 0:
                            w.append(("V", ogall[s - 1]))
                    mT1g, mE1g = grp("S", s1mm, "A", s1ev, waits=w)
                    if oc == 4 and nt == 1:
                        ev_zwza = mE1g
                    if oc == 5 and nt == 1:
                        ev_zg = mE1g
            for tt in range(NTT):
                def vmm(t, bank, tt=tt):
                    for c in range(8):
                        t.matmul(bank[:, 0:256], zst[:, c * (SL + 1) + 1 + 128 * tt:c * (SL + 1) + 1 + 128 * tt + 128],
                                 wz[:, c * 1024 + 768:c * 1024 + 1024], start=(c == 0), stop=False)
                    for c in range(8):
                        t.matmul(bank[:, 0:256], zst[:, c * (SL + 1) + 128 * tt:c * (SL + 1) + 128 * tt + 128],
                                 wzs[:, c * 1024 + 768:c * 1024 + 1024], start=False, stop=False)
                    return t.matmul(bank[:, 0:256], onesB[:], vbrowS[:], start=False, stop=True)

                def vev(a, bank, tt=tt):
                    return a.activation(vS[:, 256 * tt:256 * (tt + 1)], bank[:, 0:256], AF.Identity)
                mTv, _ = grp("V", vmm, "A", vev)
                if tt == NTT - 1:
                    zdone[s] = mTv
            for nt in range(2):
                for mh in range(2):
                    def w2mm(t, bank, nt=nt, mh=mh):
                        return t.matmul(bank[:, 0:SLH], w2t[:, 128 * mh:128 * (mh + 1)],
                                        zwza[0:64, SLH * nt:SLH * (nt + 1)], start=True, stop=True)

                    def w2ev(a, bank, nt=nt, mh=mh):
                        return a.activation(dsig[:, mh * SL + SLH * nt:mh * SL + SLH * nt + SLH],
                                            bank[:, 0:SLH], AF.Sigmoid, bias=smt[:, mh:mh + 1])
                    w = [("A", ev_zwza)] if (nt == 0 and mh == 0) else []
                    grp("S", w2mm, "A", w2ev, waits=w)
            for nt in range(2):
                for mh in range(2):
                    def a2mm(t, bank, nt=nt, mh=mh):
                        return t.matmul(bank[:, 0:SLH], a2t[64:128, 128 * mh:128 * (mh + 1)],
                                        zwza[64:128, SLH * nt:SLH * (nt + 1)], start=True, stop=True)

                    def a2ev(a, bank, nt=nt, mh=mh):
                        return a.activation(asig[:, mh * SL + SLH * nt:mh * SL + SLH * nt + SLH],
                                            bank[:, 0:SLH], AF.Sigmoid, bias=smt[:, 2 + mh:3 + mh])
                    grp("S", a2mm, "A", a2ev)
            for tt in range(NTT):
                def gmm(t, bank, tt=tt):
                    return t.matmul(bank[:, 0:256], zg[:, 128 * tt:128 * (tt + 1)],
                                    g2t[:], start=True, stop=True)

                def gev(a, bank, tt=tt):
                    return a.activation(gS[:, 256 * tt:256 * (tt + 1)], bank[:, 0:256], AF.Identity)
                w = [("A", ev_zg)] if tt == 0 else []
                grp("V", gmm, "A", gev, waits=w)

            if PB_STAGE < 2:
                continue
            run("A", lambda a: a.preload_act_table(AF.Exp))
            run("A", lambda a: a.drain())

            def p2_act(a):
                a.drain()
                return a.activation(SCR1[:], dsig[:], AF.Exp, scale=-0.60653066)
            mA_d = sig("A", p2_act)
            run("A", lambda a: a.preload_act_table(AF.Sqrt))
            run("A", lambda a: a.drain())

            wt("V", "A", mA_d)
            if s > 0:
                wt("V", "T", m_chT_last)

            def p2_v1(v):
                bufs = [SCR1, ca]
                cur = 0
                for k in (1, 2, 4, 8, 16, 32, 64):
                    a4 = bufs[cur][:].rearrange("p (t ch i) -> p t ch i", t=2, i=CHUNK)
                    b4 = bufs[1 - cur][:].rearrange("p (t ch i) -> p t ch i", t=2, i=CHUNK)
                    v.tensor_copy(b4[:, :, :, 0:k], a4[:, :, :, 0:k])
                    v.tensor_tensor(b4[:, :, :, k:CHUNK], a4[:, :, :, k:CHUNK],
                                    a4[:, :, :, 0:CHUNK - k], ALU.mult)
                    v.drain()
                    cur = 1 - cur
                a4 = ca[:].rearrange("p (t ch i) -> p t ch i", t=2, i=CHUNK)
                c4 = cprev[:].rearrange("p (t ch i) -> p t ch i", t=2, i=CHUNK)
                v.tensor_copy(c4[:, :, :, 1:CHUNK], a4[:, :, :, 0:CHUNK - 1])
                v.memset(c4[:, :, :, 0:1], 1.0)
                v.reciprocal(cinv[:], ca[:])
                v.drain()
                for ti in range(2):
                    v.tensor_scalar(SCR2[:, ti * SL:(ti + 1) * SL], kS[:, ti * SL:(ti + 1) * SL],
                                    smt[:, 4 + ti:5 + ti], None, ALU.mult)
                v.drain()
                v.tensor_tensor(SCB1[:], SCR2[:], SCR2[:], ALU.mult)
                return v.drain()
            mV_v1 = sig("V", p2_v1)

            for ti in range(2):
                for nt in range(2):
                    def ssmm(t, bank, ti=ti, nt=nt):
                        return t.matmul(bank[:, 0:SLH], blkones[:],
                                        SCB1[:, ti * SL + SLH * nt:ti * SL + SLH * nt + SLH],
                                        start=True, stop=True)

                    def ssev(a, bank, ti=ti, nt=nt):
                        return a.activation(SCR1[:, ti * SL + SLH * nt:ti * SL + SLH * nt + SLH],
                                            bank[:, 0:SLH], AF.Sqrt)
                    w = [("V", mV_v1)] if (ti == 0 and nt == 0) else []
                    _, mA_ss = grp("S", ssmm, "A", ssev, waits=w)

            wt("V", "A", mA_ss)

            def p2_v2(v):
                v.tensor_scalar(SCR1[:], SCR1[:], 1e-12, None, ALU.max)
                v.drain()
                v.reciprocal(SCR1[:], SCR1[:])
                v.drain()
                v.tensor_tensor(SCR2[:], SCR2[:], SCR1[:], ALU.mult)
                v.drain()
                v.scalar_tensor_tensor(atil[:], SCR2[:], -1.0, cprev[:], ALU.mult, ALU.mult)
                v.tensor_tensor(SCR1[:], SCR2[:], asig[:], ALU.mult)
                v.drain()
                v.tensor_tensor(btil[:], SCR1[:], cinv[:], ALU.mult)
                for ti in range(2):
                    v.tensor_scalar(SCR1[:, ti * SL:(ti + 1) * SL], asig[:, ti * SL:(ti + 1) * SL],
                                    smt[:, 6 + ti:7 + ti], onemka[:, ti:ti + 1], ALU.mult, ALU.add)
                v.drain()
                v.tensor_tensor(SCR1[:], kS[:], SCR1[:], ALU.mult)
                v.drain()
                v.tensor_tensor(ktil[:], SCR1[:], cinv[:], ALU.mult)
                v.tensor_tensor(SCR2[:], rS[:], SCR1[:], ALU.mult)
                v.drain()
                for ti in range(2):
                    v.tensor_scalar(rkpb[:, ti * SL:(ti + 1) * SL], SCR2[:, ti * SL:(ti + 1) * SL],
                                    smt[:, 8 + ti:9 + ti], None, ALU.mult)
                v.tensor_tensor(rtil[:], rS[:], ca[:], ALU.mult)
                return v.drain()
            mV_prep = sig("V", p2_v2)

            t1_done = {}
            for ch in range(CPS if PB_STAGE >= 3 else 0):
                def fsl(buf, p, ch=ch):
                    r0 = 64 * (p % 2)
                    cc = (p // 2) * SL + ch * CHUNK
                    return buf[r0:r0 + 64, cc:cc + CHUNK]

                def t1_bundles(ch, first_waits):
                    specs = [(btil, atil, mlt, mtS), (atil, btil, mgt, mS), (ktil, atil, mlt, gtS)]
                    out = []
                    for ki, (sb_, mv_, msk, dst) in enumerate(specs):
                        for p2_ in range(2):
                            mms, evs = [], []
                            for p in (2 * p2_, 2 * p2_ + 1):
                                def t1mm(t, bank, sb_=sb_, mv_=mv_, p=p, ch=ch, fsl=fsl):
                                    return t.matmul(bank[:, 0:128], fsl(sb_, p, ch), fsl(mv_, p, ch),
                                                    start=True, stop=True)

                                def t1ev(v, bank, msk=msk, dst=dst, p=p):
                                    return v.tensor_tensor(dst[:, 128 * p:128 * (p + 1)], bank[:], msk[:], ALU.mult)
                                mms.append(t1mm)
                                evs.append(t1ev)
                            w = first_waits if (ki == 0 and p2_ == 0) else []
                            out.append((mms, evs, w))
                    return out

                if ch in t1_done:
                    mE_gtS = t1_done.pop(ch)
                else:
                    w0 = [("V", mV_prep)]
                    if s == 0 and ch == 0:
                        w0.append(("C", mC_mask))
                    mE_gtS = 0
                    for mms, evs, w in t1_bundles(ch, w0):
                        _, mE_gtS = grp2("P", mms, "V", evs, waits=w)

                if PB_STAGE < 4:
                    continue
                mA_t2 = 0
                for p2_ in range(2):
                    mms, evs = [], []
                    for p in (2 * p2_, 2 * p2_ + 1):
                        def t2mm(t, bank, p=p, ch=ch, fsl=fsl):
                            r0 = 64 * (p % 2)
                            c0 = 64 * (p // 2)
                            t.matmul(bank[:, 0:64], fsl(atil, p, ch), s0b[r0:r0 + 64, c0:c0 + 64],
                                     start=True, stop=False)
                            return t.matmul(bank[:, 0:64], gtS[:, 128 * p:128 * (p + 1)],
                                            vS[:, ch * 256 + 64 * p:ch * 256 + 64 * (p + 1)],
                                            start=False, stop=True)

                        def t2ev(a, bank, p=p):
                            return a.activation(uB[:, 64 * p:64 * (p + 1)], bank[:, 0:64], AF.Identity)
                        mms.append(t2mm)
                        evs.append(t2ev)
                    w = [("V", mSb), ("V", mE_gtS)] if p2_ == 0 else []
                    _, mA_t2 = grp2("Q", mms, "A", evs, waits=w)

                fill_state = {"d2": 0, "tr": 0}
                fillers = []
                for ki, (sb_, dst) in enumerate([(btil, d1S), (ktil, d2S)]):
                    for p2_ in range(2):
                        mms, evs = [], []
                        for p in (2 * p2_, 2 * p2_ + 1):
                            def t3mm(t, bank, sb_=sb_, p=p, ch=ch, fsl=fsl):
                                return t.matmul(bank[:, 0:128], fsl(sb_, p, ch), fsl(rtil, p, ch),
                                                start=True, stop=True)

                            def t3ev(v, bank, dst=dst, p=p):
                                return v.tensor_tensor(dst[:, 128 * p:128 * (p + 1)], bank[:], mle[:], ALU.mult)
                            mms.append(t3mm)
                            evs.append(t3ev)

                        def emit_t3(mms=mms, evs=evs):
                            _, m = grp2("P", mms, "V", evs)
                            fill_state["d2"] = m
                        fillers.append(emit_t3)
                for ti in range(2):
                    for bi, src in enumerate((btil, ktil)):
                        cc = ti * SL + ch * CHUNK

                        def trmm(t, bank, src=src, cc=cc):
                            return t.transpose(bank[:, 0:128], src[:, cc:cc + CHUNK], ident[:])

                        def trev(a, bank, ti=ti, bi=bi):
                            return a.activation(bkT[:, 256 * ti + 128 * bi:256 * ti + 128 * bi + 128],
                                                bank[:], AF.Identity)

                        def emit_tr(trmm=trmm, trev=trev):
                            _, m = grp("X", trmm, "A", trev)
                            fill_state["tr"] = m
                        fillers.append(emit_tr)
                if ch + 1 < CPS and PB_STAGE >= 4:
                    for mms, evs, w in t1_bundles(ch + 1, []):
                        def emit_n1(mms=mms, evs=evs, w=w, nch=ch + 1):
                            _, m = grp2("P", mms, "V", evs, waits=w)
                            t1_done[nch] = m
                        fillers.append(emit_n1)

                cur_t, cur_u = mtS, mS
                mE_pwT_last = 0
                mE_pwU_last = 0
                mE_apply_last = 0
                for lvl in range(7):
                    entry_w = [("A", mA_t2)] if lvl == 0 else [("A", mE_pwT_last), ("V", mE_pwU_last)]
                    ap_th = []
                    for p2_ in range(2):
                        mms, evs = [], []
                        for p in (2 * p2_, 2 * p2_ + 1):
                            def apmm(t, bank, cur_t=cur_t, p=p):
                                return t.matmul(bank[:, 0:64], cur_t[:, 128 * p:128 * (p + 1)],
                                                uB[:, 64 * p:64 * (p + 1)], start=True, stop=True)

                            def apev(v, bank, p=p):
                                return v.tensor_tensor(uB[:, 64 * p:64 * (p + 1)], uB[:, 64 * p:64 * (p + 1)],
                                                       bank[:, 0:64], ALU.add)
                            mms.append(apmm)
                            evs.append(apev)
                        ap_th.append((mms, evs))
                    if lvl < 6:
                        nx_u, nx_t = pwU[lvl % 2], pwT[lvl % 2]
                        pw_th = []
                        for p2_ in range(2):
                            umms, uevs, tmms, tevs = [], [], [], []
                            for p in (2 * p2_, 2 * p2_ + 1):
                                def pumm(t, bank, cur_t=cur_t, cur_u=cur_u, p=p):
                                    return t.matmul(bank[:, 0:128], cur_t[:, 128 * p:128 * (p + 1)],
                                                    cur_u[:, 128 * p:128 * (p + 1)], start=True, stop=True)

                                def puev(v, bank, nx_u=nx_u, p=p):
                                    return v.tensor_copy(nx_u[:, 128 * p:128 * (p + 1)], bank[:])

                                def ptmm(t, bank, cur_t=cur_t, cur_u=cur_u, p=p):
                                    return t.matmul(bank[:, 0:128], cur_u[:, 128 * p:128 * (p + 1)],
                                                    cur_t[:, 128 * p:128 * (p + 1)], start=True, stop=True)

                                def ptev(a, bank, nx_t=nx_t, p=p):
                                    return a.activation(nx_t[:, 128 * p:128 * (p + 1)], bank[:], AF.Identity)
                                umms.append(pumm)
                                uevs.append(puev)
                                tmms.append(ptmm)
                                tevs.append(ptev)
                            pw_th.append((umms, uevs, tmms, tevs))
                        _, mE_pwU_last = grp2("P", pw_th[0][0], "V", pw_th[0][1], waits=entry_w)
                        _, mE_apply_last = grp2("Q", ap_th[0][0], "V", ap_th[0][1])
                        _, mE_pwT_last = grp2("P", pw_th[0][2], "A", pw_th[0][3])
                        _, mE_pwU_last = grp2("P", pw_th[1][0], "V", pw_th[1][1])
                        _, mE_apply_last = grp2("Q", ap_th[1][0], "V", ap_th[1][1])
                        _, mE_pwT_last = grp2("P", pw_th[1][2], "A", pw_th[1][3])
                        cur_t, cur_u = nx_t, nx_u
                    else:
                        _, mE_apply_last = grp2("Q", ap_th[0][0], "V", ap_th[0][1], waits=entry_w)
                        _, mE_apply_last = grp2("Q", ap_th[1][0], "V", ap_th[1][1])
                    for _f in range(2 if lvl < 6 else 0):
                        if fillers:
                            fillers.pop(0)()

                if PB_STAGE < 5:
                    continue

                for f in fillers:
                    f()
                fillers = []
                mE_d2 = fill_state["d2"]
                mA_tr = fill_state["tr"]

                mA_t4 = 0
                for p2_ in range(2):
                    mms, evs = [], []
                    for p in (2 * p2_, 2 * p2_ + 1):
                        def t4mm(t, bank, p=p, ch=ch, fsl=fsl):
                            r0 = 64 * (p % 2)
                            c0 = 64 * (p // 2)
                            t.matmul(bank[:, 0:64], fsl(rtil, p, ch), s0b[r0:r0 + 64, c0:c0 + 64],
                                     start=True, stop=False)
                            t.matmul(bank[:, 0:64], d1S[:, 128 * p:128 * (p + 1)],
                                     uB[:, 64 * p:64 * (p + 1)], start=False, stop=False)
                            return t.matmul(bank[:, 0:64], d2S[:, 128 * p:128 * (p + 1)],
                                            vS[:, ch * 256 + 64 * p:ch * 256 + 64 * (p + 1)],
                                            start=False, stop=True)

                        def t4ev(a, bank, p=p):
                            return a.activation(ogchunk[:, 64 * p:64 * (p + 1)], bank[:, 0:64], AF.Identity)
                        mms.append(t4mm)
                        evs.append(t4ev)
                    w = []
                    if p2_ == 0:
                        w = [("V", mE_d2), ("V", mE_apply_last)]
                        if m_v7:
                            w.append(("V", m_v7))
                    _, mA_t4 = grp2("Q", mms, "A", evs, waits=w)


                if PB_STAGE < 6:
                    continue
                for g2 in range(2):
                    def t5mm(t, bank, g2=g2, ch=ch):
                        t.matmul(bank[:, 0:128], bkT[:, 256 * g2:256 * g2 + 128],
                                 uB[:, 128 * g2:128 * (g2 + 1)], start=True, stop=False)
                        return t.matmul(bank[:, 0:128], bkT[:, 256 * g2 + 128:256 * g2 + 256],
                                        vS[:, ch * 256 + 128 * g2:ch * 256 + 128 * (g2 + 1)],
                                        start=False, stop=True)

                    def t5ev(v, bank, g2=g2, ch=ch):
                        cc = g2 * SL + ch * CHUNK + CHUNK - 1
                        v.tensor_tensor(sTmp[0:64, 64 * g2:64 * g2 + 64],
                                        s0f[0:64, 64 * g2:64 * g2 + 64], bank[0:64, 0:64], ALU.add)
                        v.tensor_tensor(sTmp[64:128, 64 * g2:64 * g2 + 64],
                                        s0f[64:128, 64 * g2:64 * g2 + 64], bank[64:128, 64:128], ALU.add)
                        v.drain()
                        v.tensor_scalar(s0f[0:64, 64 * g2:64 * g2 + 64],
                                        sTmp[0:64, 64 * g2:64 * g2 + 64],
                                        ca[0:64, cc:cc + 1], None, ALU.mult)
                        v.tensor_scalar(s0f[64:128, 64 * g2:64 * g2 + 64],
                                        sTmp[64:128, 64 * g2:64 * g2 + 64],
                                        ca[64:128, cc:cc + 1], None, ALU.mult)
                        v.drain()
                        v.tensor_copy(s0b[:, 64 * g2:64 * g2 + 64], s0f[:, 64 * g2:64 * g2 + 64])
                        return v.drain()
                    w = [("A", mA_tr)] if g2 == 0 else []
                    _, mSb = grp("P", t5mm, "V", t5ev, waits=w)

                if PB_STAGE < 7:
                    continue
                mms, evs = [], []
                for ti in range(2):
                    def t6mm(t, bank, ti=ti, ch=ch):
                        return t.matmul(bank[:, 0:64],
                                        rkpb[:, ti * SL + ch * CHUNK:ti * SL + (ch + 1) * CHUNK],
                                        ind64[:], start=True, stop=True)

                    def t6ev(v, bank, ti=ti):
                        return v.tensor_copy(rksum[:, 2 * ti:2 * ti + 2], bank[:, 0:2])
                    mms.append(t6mm)
                    evs.append(t6ev)
                _, mE_rk = grp2("Q", mms, "V", evs)

                wt("V", "A", mA_t4)

                def v6(v):
                    o4m = ogchunk[:].rearrange("p (h i) -> p h i", h=4)
                    v.tensor_reduce(gnm[:], o4m, AX, ALU.add)
                    v.tensor_tensor(oq[:], ogchunk[:], ogchunk[:], ALU.mult)
                    v.drain()
                    o4 = oq[:].rearrange("p (h i) -> p h i", h=4)
                    v.tensor_reduce(gns[:], o4, AX, ALU.add)
                    v.tensor_scalar(gnm[:], gnm[:], 1.0 / 64.0, None, ALU.mult)
                    v.drain()
                    v.tensor_scalar(gnr[:], gns[:], 1.0 / 64.0, None, ALU.mult)
                    v.tensor_tensor(gns[:], gnm[:], gnm[:], ALU.mult)
                    v.drain()
                    v.tensor_tensor(gnr[:], gnr[:], gns[:], ALU.subtract)
                    v.drain()
                    v.tensor_scalar(gnr[:], gnr[:], float(GNEPS), None, ALU.add)
                    v.drain()
                    v.reciprocal(gnr[:], gnr[:])
                    return v.drain()
                mV6 = sig("V", v6)
                wt("A", "V", mV6)

                def a6(a):
                    return a.activation(gnr[:], gnr[:], AF.Sqrt)
                mA6 = sig("A", a6)
                wt("V", "A", mA6)
                if ch == 0 and s > 0:
                    wt("V", "O", s + 1)

                def v7(v, ch=ch):
                    for h in range(4):
                        v.tensor_scalar(ogchunk[:, 64 * h:64 * (h + 1)], ogchunk[:, 64 * h:64 * (h + 1)],
                                        gnm[:, h:h + 1], gnr[:, h:h + 1], ALU.subtract, ALU.mult)
                    v.drain()
                    v.tensor_tensor(ogchunk[:], ogchunk[:], gnt[:, 0:256], ALU.mult)
                    v.drain()
                    v.tensor_tensor(ogchunk[:], ogchunk[:], gnt[:, 256:512], ALU.add)
                    v.drain()
                    for h in range(4):
                        v.scalar_tensor_tensor(ogchunk[:, 64 * h:64 * (h + 1)],
                                               vS[:, ch * 256 + 64 * h:ch * 256 + 64 * (h + 1)],
                                               rksum[:, h:h + 1], ogchunk[:, 64 * h:64 * (h + 1)],
                                               ALU.mult, ALU.add)
                    v.drain()
                    v.tensor_tensor(ogS[:, ch * 256:(ch + 1) * 256], ogchunk[:],
                                    gS[:, ch * 256:(ch + 1) * 256], ALU.mult)
                    return v.drain()
                m_v7 = sig("V", v7)
                m_chT_last = tick["T"]
            ogall[s] = m_v7 if m_v7 else tick["V"]

            if PB_STAGE < 7:
                continue
            wt("G", "V", m_v7)
            sig("O", lambda g, t0=t0: g.dma_start(
                out=og[t0:t0 + SL, :].rearrange("(tt p) j -> p tt j", p=128),
                in_=ogS[:, 0:NTT * 256].rearrange("p (tt j) -> p tt j", tt=NTT)))

        if PB_STAGE >= 7:
            wt("G", "O", tick["O"])
        wt("G", "V", tick["V"])
        wt("G", "T", tick["T"])
        wt("G", "A", tick["A"])
        sig("C", lambda g: g.drain())
        _finalize_pb(block, prog)
    return nc


def host_pB_inputs(inputs, z2):
    pre = lambda k: np.asarray(inputs["L0_attn_" + k], np.float32)
    z2 = np.asarray(z2).reshape(B, T, C)
    maps = []
    for c in range(NC):
        b, g = c // 4, c % 4
        cs = slice(256 * g, 256 * (g + 1))
        z2T = np.zeros((C, T + 1), bf)
        z2T[:, 1:] = np.ascontiguousarray(z2[b].T)
        wcat = np.concatenate([pre("W_r")[:, cs], pre("W_k")[:, cs], pre("w1"),
                               pre("a1"), pre("g1"), pre("W_v")[:, cs]], axis=1)
        mixl = [pre("x_r"), pre("x_k"), pre("x_w"), pre("x_a"), pre("x_g"), pre("x_v")]
        mixcat = np.concatenate([m.reshape(8, 128).T for m in mixl], axis=1)
        lnwb = np.concatenate([pre("ln_attn_w").reshape(8, 128).T,
                               pre("ln_attn_b").reshape(8, 128).T], axis=1)
        smallc = np.concatenate([pre("w0")[cs].reshape(2, 128).T,
                                 pre("a0")[cs].reshape(2, 128).T,
                                 pre("k_k")[cs].reshape(2, 128).T,
                                 pre("k_a")[cs].reshape(2, 128).T,
                                 pre("r_k").reshape(C)[cs].reshape(2, 128).T], axis=1)
        gnwb = np.concatenate([np.broadcast_to(pre("gn_w")[cs][None], (128, 256)),
                               np.broadcast_to(pre("gn_b")[cs][None], (128, 256))], axis=1)
        maps.append({
            "z2T": z2T,
            "wcat": np.ascontiguousarray(wcat),
            "mixcat": np.ascontiguousarray(mixcat),
            "lnwb": np.ascontiguousarray(lnwb),
            "w2sI": np.ascontiguousarray(pre("w2")[:, cs]),
            "a2sI": np.ascontiguousarray(pre("a2")[:, cs]),
            "g2sI": np.ascontiguousarray(pre("g2")[:, cs]),
            "smallc": np.ascontiguousarray(smallc),
            "gnwb": np.ascontiguousarray(gnwb),
        })
    return maps


def run_pB(inputs, z2):
    nc = build_pB()
    maps = host_pB_inputs(inputs, z2)
    res = _run(nc, maps)
    og = np.zeros((B, T, C), bf)
    for c in range(NC):
        og[c // 4, :, 256 * (c % 4):256 * (c % 4 + 1)] = np.asarray(res.results[c]["og"])
    return og.reshape(B * T, C)




class Sched:
    def __init__(self, stack, nc, streams, pools):
        self.nc = nc
        self.seqs = {k: SemSeq(stack, nc, "s" + k, inc, total)
                     for k, (_, inc, total) in streams.items()}
        self.eng = {k: v[0] for k, v in streams.items()}
        self.tick = {k: 0 for k in streams}
        self.prog = {"S": [], "T": [], "V": [], "A": [], "G": []}
        self.pools = pools
        self.rot = {k: 0 for k in pools}
        self.blast = {}

    def emit(self, engn, fn):
        self.prog[engn].append(fn)

    def run(self, engn, fn):
        self.prog[engn].append(fn)

    def sig(self, stream, fn):
        i = self.tick[stream]
        self.tick[stream] += 1
        sq = self.seqs[stream]

        def g(eng, fn=fn, i=i, sq=sq):
            sq.attach(fn(eng), i)
        self.emit(self.eng[stream], g)
        return i + 1

    def wt(self, engn, stream, count):
        if count <= 0:
            return
        i = count - 1
        sq = self.seqs[stream]
        if sq.inc == 16:
            def g(eng, sq=sq, i=i):
                sq.wait(eng, i)
        else:
            def g(eng, sq=sq, i=i):
                eng.wait_ge(sq.sems[i % sq.k], sq.inc * (i // sq.k + 1))
        self.emit(engn, g)

    def grp(self, pool, mmfn, estream, efn, waits=()):
        bank = self.pools[pool][self.rot[pool] % len(self.pools[pool])]
        self.rot[pool] += 1
        bl = self.blast.get(bank.name)
        if bl:
            self.wt("T", bl[0], bl[1])
        for (stm, m) in waits:
            self.wt("T", stm, m)
        mT = self.sig("T", lambda t, mmfn=mmfn, bank=bank: mmfn(t, bank))
        self.wt(self.eng[estream], "T", mT)
        if self.eng[estream] == "V":
            mE = self.sig(estream, lambda v, efn=efn, bank=bank: (efn(v, bank), v.drain())[1])
        else:
            mE = self.sig(estream, lambda v, efn=efn, bank=bank: efn(v, bank))
        self.blast[bank.name] = (estream, mE)
        return mT, mE

    def finalize(self, block):
        prog = self.prog

        @block.sync
        def _(sy):
            for f in prog["S"]:
                f(sy)

        @block.tensor
        def _(t):
            for f in prog["T"]:
                f(t)

        @block.vector
        def _(v):
            for f in prog["V"]:
                f(v)

        @block.scalar
        def _(a):
            for f in prog["A"]:
                f(a)

        @block.gpsimd
        def _(g):
            for f in prog["G"]:
                f(g)



CT = TPC + 128
CNT = CT // 128


def build_pC():
    nc = _bass()
    ogT = nc.dram_tensor("ogT", [C, CT], BF16, kind="ExternalInput").ap()
    xpt = nc.dram_tensor("xpt", [CT, C], F32, kind="ExternalInput").ap()
    x0t = nc.dram_tensor("x0t", [TPC, C], F32, kind="ExternalInput").ap()
    WoI = nc.dram_tensor("WoI", [C, C], F32, kind="ExternalInput").ap()
    Wf1I = nc.dram_tensor("Wf1I", [C, 4096], F32, kind="ExternalInput").ap()
    Wf2I = nc.dram_tensor("Wf2I", [4096, C], F32, kind="ExternalInput").ap()
    lnfc = nc.dram_tensor("lnfc", [128, 24], F32, kind="ExternalInput").ap()
    bmaskI = nc.dram_tensor("bmaskI", [128, 1], F32, kind="ExternalInput").ap()
    identI = nc.dram_tensor("identI", [128, 128], BF16, kind="ExternalInput").ap()
    xres = nc.dram_tensor("xres", [TPC, C], F32, kind="ExternalOutput").ap()
    z3o = nc.dram_tensor("z3o", [TPC, C], BF16, kind="ExternalOutput").ap()
    zsumo = nc.dram_tensor("zsumo", [1, C], F32, kind="ExternalOutput").ap()

    from contextlib import ExitStack
    with ExitStack() as stack:
        e = stack.enter_context
        wbig = e(nc.sbuf_tensor("wbig", [128, 8192], BF16))
        wf1b = [e(nc.sbuf_tensor(f"wf1b{j}", [128, 4096], BF16)) for j in range(2)]
        w1bg = e(nc.sbuf_tensor("w1bg", [128, 4096], BF16))
        zfT = e(nc.sbuf_tensor("zfT", [128, 8 * CT], BF16))
        x1acc = e(nc.sbuf_tensor("x1acc", [128, 16 * 1024], F32))
        hR = e(nc.sbuf_tensor("hR", [128, 4 * 2048], BF16))
        ogst = e(nc.sbuf_tensor("ogst", [128, 2 * 1024], BF16))
        xt2 = e(nc.sbuf_tensor("xt2", [128, 2 * 1024], F32))
        xpt2 = e(nc.sbuf_tensor("xpt2", [128, 2 * 1024], F32))
        x0t2 = e(nc.sbuf_tensor("x0t2", [128, 2 * 1024], F32))
        zft = e(nc.sbuf_tensor("zft", [128, 2 * 1024], BF16))
        z3t = e(nc.sbuf_tensor("z3t", [128, 2 * 1024], BF16))
        lnc = e(nc.sbuf_tensor("lnc", [128, 24], F32))
        bmk = e(nc.sbuf_tensor("bmk", [128, 1], F32))
        sabF = e(nc.sbuf_tensor("sabF", [128, 16], F32))
        lnfbB = e(nc.sbuf_tensor("lnfbB", [128, 8 * 64], BF16))
        onesFc = e(nc.sbuf_tensor("onesFc", [128, 64], F32))
        ones128b = e(nc.sbuf_tensor("ones128b", [128, 128], BF16))
        identT = e(nc.sbuf_tensor("identT", [128, 128], BF16))
        biasF = e(nc.sbuf_tensor("biasF", [128, 4], F32))
        zsumacc = e(nc.sbuf_tensor("zsumacc", [1, 1024], F32))
        scr = e(nc.sbuf_tensor("scr", [128, 1024], F32))
        sm = e(nc.sbuf_tensor("sm", [128, 1], F32))
        mean = e(nc.sbuf_tensor("mean", [128, 1], F32))
        var = e(nc.sbuf_tensor("var", [128, 1], F32))
        ur = e(nc.sbuf_tensor("ur", [128, 1], F32))
        pM = [e(nc.psum_tensor(f"pM{j}", [128, 512], F32)) for j in range(6)]
        pQ = [e(nc.psum_tensor("pQ0", [128, 64], F32))]
        pX = [e(nc.psum_tensor("pX0", [128, 128], BF16))]
        block_streams = {
            "T": ("T", 1, 700), "V": ("V", 1, 700), "A": ("A", 1, 450),
            "D": ("S", 16, 60), "G": ("G", 16, 24), "O": ("G", 16, 40),
        }
        sc = Sched(stack, nc, block_streams, {"S": pM, "Q": pQ, "X": pX})
        sig, wt, run, grp = sc.sig, sc.wt, sc.run, sc.grp
        block = e(nc.Block())

        run("A", lambda a: a.preload_act_table(AF.Sqrt))
        run("A", lambda a: a.drain())
        mG_wo = sig("G", lambda g: g.dma_start(
            out=wbig[:].rearrange("p (c j) -> p c j", c=8),
            in_=WoI.rearrange("(c p) j -> p c j", p=128)))
        mD = sig("D", lambda s: s.dma_start(out=lnc[:], in_=lnfc))
        mD = sig("D", lambda s: s.dma_start(out=bmk[:], in_=bmaskI))
        mD = sig("D", lambda s: s.dma_start(out=identT[:], in_=identI))
        mD_small = mD

        def const_build(v):
            v.memset(onesFc[:], 1.0)
            v.memset(ones128b[:], 1.0)
            v.memset(zsumacc[:], 0.0)
            v.drain()
            return v.drain()
        mV_const = sig("V", const_build)
        wt("V", "D", mD_small)

        def prep(v):
            for c in range(8):
                v.tensor_tensor(sabF[:, 2 * c + 1:2 * c + 2], lnc[:, c:c + 1],
                                lnc[:, 16 + c:17 + c], ALU.mult)
            v.drain()
            for c in range(8):
                v.tensor_tensor(sabF[:, 2 * c:2 * c + 1], lnc[:, c:c + 1],
                                sabF[:, 2 * c + 1:2 * c + 2], ALU.subtract)
                v.tensor_scalar(lnfbB[:, 64 * c:64 * (c + 1)], onesFc[:],
                                lnc[:, 8 + c:9 + c], None, ALU.mult)
            return v.drain()
        mV_prep = sig("V", prep)

        mT_tr_last = [0] * 2
        mV_x_last = [0] * 2
        for ti in range(CNT):
            sl = ti % 2
            if mT_tr_last[sl]:
                wt("S", "T", mT_tr_last[sl])
            if mV_x_last[sl]:
                wt("S", "V", mV_x_last[sl])
            mD_og = sig("D", lambda s, ti=ti, sl=sl: s.dma_start(
                out=ogst[:, 1024 * sl:1024 * (sl + 1)].rearrange("p (c j) -> p c j", c=8),
                in_=ogT.rearrange("(c p) t -> p c t", p=128)[:, :, 128 * ti:128 * (ti + 1)]))
            mD_xp = sig("D", lambda s, ti=ti, sl=sl: s.dma_start(
                out=xpt2[:, 1024 * sl:1024 * (sl + 1)],
                in_=xpt[128 * ti:128 * (ti + 1), :]))
            if ti >= 1:
                mD_x0 = sig("D", lambda s, ti=ti, sl=sl: s.dma_start(
                    out=x0t2[:, 1024 * sl:1024 * (sl + 1)],
                    in_=x0t[128 * (ti - 1):128 * ti, :]))
            mE_half = 0
            for h in range(2):
                def womm(t, bank, sl=sl, h=h):
                    inst = None
                    for c in range(8):
                        inst = t.matmul(bank[:, 0:512], ogst[:, 1024 * sl + 128 * c:1024 * sl + 128 * (c + 1)],
                                        wbig[:, 1024 * c + 512 * h:1024 * c + 512 * (h + 1)],
                                        start=(c == 0), stop=(c == 7))
                    return inst

                def woev(v, bank, sl=sl, h=h):
                    return v.tensor_tensor(xt2[:, 1024 * sl + 512 * h:1024 * sl + 512 * (h + 1)],
                                           bank[:, 0:512],
                                           xpt2[:, 1024 * sl + 512 * h:1024 * sl + 512 * (h + 1)], ALU.add)
                w = [("D", mD_og), ("D", mD_xp)] if h == 0 else []
                if ti == 0 and h == 0:
                    w.append(("G", mG_wo))
                _, mE_half = grp("S", womm, "V", woev, waits=w)

            def lnstats(v, sl=sl):
                xv = xt2[:, 1024 * sl:1024 * (sl + 1)]
                emit_stats1024(v, xv, scr[:], sm[:], var[:], mean[:])
                emit_inv(v, var[:], ur[:])
                return v.drain()
            mVs = sig("V", lnstats)
            wt("A", "V", mVs)
            mAs = sig("A", lambda a: a.activation(ur[:], ur[:], AF.Sqrt))
            wt("V", "A", mAs)
            if ti >= 1:
                wt("V", "D", mD_x0)
            if mT_tr_last[sl]:
                wt("V", "T", mT_tr_last[sl])

            def zfscale(v, sl=sl, ti=ti):
                xv = xt2[:, 1024 * sl:1024 * (sl + 1)]
                v.tensor_scalar(zft[:, 1024 * sl:1024 * (sl + 1)], xv, mean[:], ur[:],
                                ALU.subtract, ALU.mult)
                if ti >= 1:
                    v.drain()
                    v.tensor_tensor(x1acc[:, 1024 * (ti - 1):1024 * ti], xv,
                                    x0t2[:, 1024 * sl:1024 * (sl + 1)], ALU.add)
                return v.drain()
            mVz = sig("V", zfscale)
            mV_x_last[sl] = mVz

            for c in range(8):
                def trmm(t, bank, sl=sl, c=c):
                    return t.transpose(bank[:, 0:128], zft[:, 1024 * sl + 128 * c:1024 * sl + 128 * (c + 1)],
                                       identT[:])

                def trev(a, bank, c=c, ti=ti):
                    return a.activation(zfT[:, CT * c + 128 * ti:CT * c + 128 * (ti + 1)],
                                        bank[:], AF.Identity)
                w = [("V", mVz)] if c == 0 else []
                if ti == 0 and c == 0:
                    w.append(("D", mD_small))
                mTt, _ = grp("X", trmm, "A", trev, waits=w)
            mT_tr_last[sl] = mTt

        wt("V", "A", sc.tick["A"])

        def bnd(v):
            for c in range(8):
                v.tensor_scalar(zfT[:, CT * c + 127:CT * c + 128],
                                zfT[:, CT * c + 127:CT * c + 128], bmk[:, 0:1], None, ALU.mult)
            return v.drain()
        mV_bnd = sig("V", bnd)

        NG = 8
        mT_w_last = 0
        m_f1T = [0, 0]
        for g in range(NG):
            gsl = g % 2
            wf1g = wf1b[gsl]
            if m_f1T[gsl]:
                wt("G", "T", m_f1T[gsl])
            elif g == 0:
                wt("G", "T", sc.tick["T"])
            mG_w1 = sig("G", lambda gp, g=g, wf1g=wf1g: gp.dma_start(
                out=wf1g[:].rearrange("p (c j) -> p c j", c=8),
                in_=Wf1I[:, 512 * g:512 * (g + 1)].rearrange("(c p) j -> p c j", p=128)))
            if g > 0:
                wt("G", "T", mT_w_last)
            mG_w2 = sig("G", lambda gp, g=g: gp.dma_start(
                out=wbig[:, 0:4096].rearrange("p (c j) -> p c j", c=4),
                in_=Wf2I[512 * g:512 * (g + 1), :].rearrange("(c p) j -> p c j", p=128)))
            mT_bias = 0
            for oc in range(4):
                def bmm(t, bank, oc=oc, wf1g=wf1g):
                    inst = None
                    for c in range(8):
                        inst = t.matmul(bank[:, 0:64], wf1g[:, 512 * c + 128 * oc:512 * c + 128 * (oc + 1)],
                                        lnfbB[:, 64 * c:64 * c + 64], start=(c == 0), stop=(c == 7))
                    return inst

                def bev(a, bank, oc=oc):
                    return a.activation(biasF[:, oc:oc + 1], bank[:, 0:1], AF.Identity)
                w = [("G", mG_w1), ("V", mV_prep)] if oc == 0 else []
                mT_bias, _ = grp("Q", bmm, "A", bev, waits=w)

            wt("V", "T", mT_bias)

            def wsc(v, wf1g=wf1g):
                for c in range(8):
                    v.tensor_scalar(w1bg[:, 512 * c:512 * (c + 1)], wf1g[:, 512 * c:512 * (c + 1)],
                                    sabF[:, 2 * c + 1:2 * c + 2], None, ALU.mult)
                v.drain()
                for c in range(8):
                    v.tensor_scalar(wf1g[:, 512 * c:512 * (c + 1)], wf1g[:, 512 * c:512 * (c + 1)],
                                    sabF[:, 2 * c:2 * c + 1], None, ALU.mult)
                return v.drain()
            mV_wsc = sig("V", wsc)

            mV_sq = 0
            for oc in range(4):
                for tt in range(4):
                    def f1mm(t, bank, oc=oc, tt=tt, wf1g=wf1g):
                        for c in range(8):
                            t.matmul(bank[:, 0:512], wf1g[:, 512 * c + 128 * oc:512 * c + 128 * (oc + 1)],
                                     zfT[:, CT * c + 128 + 512 * tt:CT * c + 128 + 512 * (tt + 1)],
                                     start=(c == 0), stop=False)
                        inst = None
                        for c in range(8):
                            inst = t.matmul(bank[:, 0:512], w1bg[:, 512 * c + 128 * oc:512 * c + 128 * (oc + 1)],
                                            zfT[:, CT * c + 127 + 512 * tt:CT * c + 127 + 512 * (tt + 1)],
                                            start=False, stop=(c == 7))
                        return inst

                    def f1ev(a, bank, oc=oc, tt=tt):
                        return a.activation(hR[:, 2048 * oc + 512 * tt:2048 * oc + 512 * (tt + 1)],
                                            bank[:, 0:512], AF.Relu, bias=biasF[:, oc:oc + 1])
                    w = []
                    if oc == 0 and tt == 0:
                        w = [("V", mV_wsc), ("V", mV_bnd)]
                    mT_f1c, mAr = grp("S", f1mm, "A", f1ev, waits=w)
                    wt("V", "A", mAr)

                    def sq(v, oc=oc, tt=tt):
                        s = hR[:, 2048 * oc + 512 * tt:2048 * oc + 512 * (tt + 1)]
                        return v.tensor_tensor(s, s, s, ALU.mult)
                    mV_sq = sig("V", sq)
            m_f1T[gsl] = mT_f1c

            for st in range(16):
                for h in range(2):
                    def f2mm(t, bank, st=st, h=h):
                        inst = None
                        for c4 in range(4):
                            inst = t.matmul(bank[:, 0:512], hR[:, 2048 * c4 + 128 * st:2048 * c4 + 128 * (st + 1)],
                                            wbig[:, 1024 * c4 + 512 * h:1024 * c4 + 512 * (h + 1)],
                                            start=(c4 == 0), stop=(c4 == 3))
                        return inst

                    def f2ev(v, bank, st=st, h=h):
                        s = x1acc[:, 1024 * st + 512 * h:1024 * st + 512 * (h + 1)]
                        return v.tensor_tensor(s, s, bank[:, 0:512], ALU.add)
                    w = []
                    if st == 0 and h == 0:
                        w = [("V", mV_sq), ("G", mG_w2)]
                    mT_w_last, _ = grp("S", f2mm, "V", f2ev, waits=w)

        mO = 0
        mT_zs = [0, 0]
        for ti in range(16):
            sl = ti % 2
            wt("V", "T", mT_w_last)

            def lnstats2(v, ti=ti):
                xv = x1acc[:, 1024 * ti:1024 * (ti + 1)]
                emit_stats1024(v, xv, scr[:], sm[:], var[:], mean[:])
                emit_inv(v, var[:], ur[:])
                return v.drain()
            mVs = sig("V", lnstats2)
            wt("A", "V", mVs)
            mAs = sig("A", lambda a: a.activation(ur[:], ur[:], AF.Sqrt))
            wt("V", "A", mAs)
            if ti >= 2:
                wt("V", "O", mO - 1)
                wt("V", "T", mT_zs[sl])

            def z3scale(v, ti=ti, sl=sl):
                xv = x1acc[:, 1024 * ti:1024 * (ti + 1)]
                return v.tensor_scalar(z3t[:, 1024 * sl:1024 * (sl + 1)], xv, mean[:], ur[:],
                                       ALU.subtract, ALU.mult)
            mVz = sig("V", z3scale)

            for h in range(2):
                def zsmm(t, bank, sl=sl, h=h):
                    return t.matmul(bank[:, 0:512], ones128b[:],
                                    z3t[:, 1024 * sl + 512 * h:1024 * sl + 512 * (h + 1)],
                                    start=True, stop=True)

                def zsev(v, bank, h=h):
                    return v.tensor_tensor(zsumacc[0:1, 512 * h:512 * (h + 1)],
                                           zsumacc[0:1, 512 * h:512 * (h + 1)], bank[0:1, 0:512], ALU.add)
                w = [("V", mVz)] if h == 0 else []
                mT_zs[sl], _ = grp("S", zsmm, "V", zsev, waits=w)

            wt("G", "V", mVz)
            mO = sig("O", lambda gp, ti=ti: gp.dma_start(
                out=xres[128 * ti:128 * (ti + 1), :], in_=x1acc[:, 1024 * ti:1024 * (ti + 1)]))
            mO = sig("O", lambda gp, ti=ti, sl=sl: gp.dma_start(
                out=z3o[128 * ti:128 * (ti + 1), :], in_=z3t[:, 1024 * sl:1024 * (sl + 1)]))

        wt("G", "V", sc.tick["V"])
        mO = sig("O", lambda gp: gp.dma_start(out=zsumo, in_=zsumacc[:]))
        wt("G", "O", sc.tick["O"])
        wt("G", "T", sc.tick["T"])
        wt("G", "A", sc.tick["A"])
        sig("G", lambda gp: gp.drain())
        sc.finalize(block)
    return nc


def host_pC_inputs(inputs, xp, og):
    xp = np.asarray(xp, np.float32).reshape(B, T, C)
    og = np.asarray(og, bf).reshape(B, T, C)
    x0 = np.asarray(inputs["L0_attn_x"], np.float32).reshape(B, T, C)
    lnf = [np.asarray(inputs["L0_attn_ln_ffn_w"], np.float32),
           np.asarray(inputs["L0_attn_ln_ffn_b"], np.float32),
           np.asarray(inputs["L0_attn_ffn_x_k"], np.float32)]
    lnfc = np.concatenate([v.reshape(8, 128).T for v in lnf], axis=1)
    maps = []
    for c in range(NC):
        b, q = c // 4, c % 4
        s0 = 2048 * q
        ogp = np.zeros((CT, C), bf)
        xpp = np.zeros((CT, C), np.float32)
        if q == 0:
            ogp[128:] = og[b, 0:2048 + 128 - 128]
            xpp[128:] = xp[b, 0:2048]
            bmv = 0.0
        else:
            ogp[:] = og[b, s0 - 128:s0 + 2048]
            xpp[:] = xp[b, s0 - 128:s0 + 2048]
            bmv = 1.0
        maps.append({
            "ogT": np.ascontiguousarray(ogp.T),
            "xpt": xpp,
            "x0t": np.ascontiguousarray(x0[b, s0:s0 + 2048]),
            "WoI": np.asarray(inputs["L0_attn_W_o"], np.float32),
            "Wf1I": np.asarray(inputs["L0_attn_W_ffn1"], np.float32),
            "Wf2I": np.asarray(inputs["L0_attn_W_ffn2"], np.float32),
            "lnfc": np.ascontiguousarray(lnfc),
            "bmaskI": np.full((128, 1), bmv, np.float32),
            "identI": np.ascontiguousarray(np.eye(128, dtype=bf)),
        })
    return maps


def run_pC(inputs, xp, og):
    nc = build_pC()
    maps = host_pC_inputs(inputs, xp, og)
    res = _run(nc, maps)
    xres = np.concatenate([np.asarray(r["xres"]) for r in res.results], 0)
    z3 = np.concatenate([np.asarray(r["z3o"]) for r in res.results], 0)
    zsum = np.stack([np.asarray(r["zsumo"])[0] for r in res.results], 0)
    return xres, z3, zsum




def build_pD2():
    nc = _bass()
    zparts = nc.dram_tensor("zparts", [128, 32], F32, kind="ExternalInput").ap()
    WrC = nc.dram_tensor("WrC", [128, 64], F32, kind="ExternalInput").ap()
    brB = nc.dram_tensor("brB", [128, 8], F32, kind="ExternalInput").ap()
    oho = nc.dram_tensor("oho", [1, 8], F32, kind="ExternalOutput").ap()
    from contextlib import ExitStack
    with ExitStack() as stack:
        e = stack.enter_context
        zp = e(nc.sbuf_tensor("zp", [128, 32], F32))
        wrc = e(nc.sbuf_tensor("wrc", [128, 64], F32))
        wrb = e(nc.sbuf_tensor("wrb", [128, 64], BF16))
        brb = e(nc.sbuf_tensor("brb", [128, 8], F32))
        zm = e(nc.sbuf_tensor("zm", [128, 8], F32))
        zh = e(nc.sbuf_tensor("zh", [128, 8], BF16))
        zhf = e(nc.sbuf_tensor("zhf", [128, 8], F32))
        zr = e(nc.sbuf_tensor("zr", [128, 8], F32))
        onesR = e(nc.sbuf_tensor("onesR", [128, 128], F32))
        zmB = e(nc.sbuf_tensor("zmB", [128, 16 * 128], BF16))
        sc8 = e(nc.sbuf_tensor("sc8", [128, 8], F32))
        mx = e(nc.sbuf_tensor("mx", [128, 1], F32))
        oh = e(nc.sbuf_tensor("oh", [128, 8], F32))
        pR = e(nc.psum_tensor("pR", [128, 8], F32))
        sg = e(nc.semaphore("sg"))
        sv = e(nc.semaphore("sv"))
        st = e(nc.semaphore("st"))
        so = e(nc.semaphore("so"))
        block = e(nc.Block())

        @block.sync
        def _(sync):
            sync.dma_start(out=zp[:], in_=zparts).then_inc(sg, 16)
            sync.dma_start(out=wrc[:], in_=WrC).then_inc(sg, 16)
            sync.dma_start(out=brb[:], in_=brB).then_inc(sg, 16)
            sync.wait_ge(sv, 3)
            sync.dma_start(out=oho, in_=oh[0:1, 0:8]).then_inc(so, 16)

        @block.vector
        def _(v):
            v.memset(onesR[:], 1.0)
            v.wait_ge(sg, 48)
            v.tensor_tensor(zm[:], zp[:, 0:8], zp[:, 8:16], ALU.add)
            v.tensor_copy(wrb[:], wrc[:])
            v.drain()
            v.tensor_tensor(zm[:], zm[:], zp[:, 16:24], ALU.add)
            v.drain()
            v.tensor_tensor(zm[:], zm[:], zp[:, 24:32], ALU.add)
            v.drain()
            v.tensor_scalar(zm[:], zm[:], 1.0 / float(T), None, ALU.mult)
            v.drain()
            v.tensor_copy(zh[:], zm[:])
            v.drain()
            v.tensor_copy(zhf[:], zh[:])
            v.drain()
            v.tensor_tensor(zr[:], zm[:], zhf[:], ALU.subtract)
            v.drain()
            for c in range(8):
                v.tensor_scalar(zmB[:, 128 * c:128 * (c + 1)], onesR[:],
                                zm[:, c:c + 1], None, ALU.mult)
                v.tensor_scalar(zmB[:, 1024 + 128 * c:1024 + 128 * (c + 1)], onesR[:],
                                zr[:, c:c + 1], None, ALU.mult)
            v.drain().then_inc(sv, 1)
            v.wait_ge(st, 1)
            v.tensor_tensor(sc8[:], pR[:], brb[:], ALU.add)
            v.drain()
            v.tensor_reduce(mx[:], sc8[:], AX, ALU.max)
            v.drain()
            v.tensor_scalar(oh[:], sc8[:], mx[:, 0:1], None, ALU.is_ge)
            v.drain().then_inc(sv, 2)

        @block.tensor
        def _(t):
            t.wait_ge(sv, 1)
            for c in range(16):
                t.matmul(pR[:], zmB[:, 128 * c:128 * (c + 1)], wrb[:, 8 * (c % 8):8 * (c % 8 + 1)],
                         start=(c == 0), stop=(c == 15))
            t.drain().then_inc(st, 1)

        @block.gpsimd
        def _(g):
            g.wait_ge(so, 16)
    return nc


def run_pD2(inputs, zsum):
    nc = build_pD2()
    Wr = np.asarray(inputs["L0_ffn_Wr"], np.float32)
    br = np.asarray(inputs["L0_ffn_br"], np.float32)
    WrC = np.concatenate([Wr.reshape(8, 128, 8)[c] for c in range(8)], axis=1)
    brB = np.ascontiguousarray(np.broadcast_to(br[None, :], (128, 8)))
    maps = []
    for c in range(NC):
        b = c // 4
        zparts = np.concatenate([zsum[4 * b + j].reshape(8, 128).T for j in range(4)], axis=1)
        maps.append({"zparts": np.ascontiguousarray(zparts), "WrC": np.ascontiguousarray(WrC),
                     "brB": brB})
    res = _run(nc, maps)
    ohs = [np.asarray(res.results[c]["oho"])[0] for c in (0, 4)]
    return ohs




def build_pD():
    nc = _bass()
    xresI = nc.dram_tensor("xresI", [TPC, C], F32, kind="ExternalInput").ap()
    z3TI = nc.dram_tensor("z3TI", [C, TPC], BF16, kind="ExternalInput").ap()
    W1e = nc.dram_tensor("W1e", [C, 4096], F32, kind="ExternalInput").ap()
    b1CI = nc.dram_tensor("b1CI", [128, 32], F32, kind="ExternalInput").ap()
    W2e = nc.dram_tensor("W2e", [4096, C], F32, kind="ExternalInput").ap()
    b2BI = nc.dram_tensor("b2BI", [128, C], F32, kind="ExternalInput").ap()
    WqD = nc.dram_tensor("WqD", [C, C], F32, kind="ExternalInput").ap()
    WkD = nc.dram_tensor("WkD", [C, C], F32, kind="ExternalInput").ap()
    WvI = nc.dram_tensor("WvI", [C, C], F32, kind="ExternalInput").ap()
    csQI = nc.dram_tensor("csQI", [128, 8192], F32, kind="ExternalInput").ap()
    snQI = nc.dram_tensor("snQI", [128, 8192], F32, kind="ExternalInput").ap()
    csKI = nc.dram_tensor("csKI", [128, 8192], F32, kind="ExternalInput").ap()
    snKI = nc.dram_tensor("snKI", [128, 8192], F32, kind="ExternalInput").ap()
    identI = nc.dram_tensor("identI", [128, 128], BF16, kind="ExternalInput").ap()
    x2o = nc.dram_tensor("x2o", [TPC, C], F32, kind="ExternalOutput").ap()
    QTo = nc.dram_tensor("QTo", [C, TPC], BF16, kind="ExternalOutput").ap()
    KTo = nc.dram_tensor("KTo", [C, TPC], BF16, kind="ExternalOutput").ap()
    Vo = nc.dram_tensor("Vo", [TPC, C], BF16, kind="ExternalOutput").ap()

    from contextlib import ExitStack
    with ExitStack() as stack:
        e = stack.enter_context
        z3s = e(nc.sbuf_tensor("z3s", [128, 8 * 2048], BF16))
        SCR = e(nc.sbuf_tensor("SCR", [128, 16 * 1024], F32))
        hR = e(nc.sbuf_tensor("hR", [128, 4 * 2048], BF16))
        wf1g = e(nc.sbuf_tensor("wf1g", [128, 4096], BF16))
        wbig = e(nc.sbuf_tensor("wbig", [128, 8192], BF16))
        qkvB = e(nc.sbuf_tensor("qkvB", [128, 16 * 1024], BF16))
        xrt = e(nc.sbuf_tensor("xrt", [128, 2 * 1024], F32))
        z4t = e(nc.sbuf_tensor("z4t", [128, 2 * 1024], BF16))
        rtmp = e(nc.sbuf_tensor("rtmp", [128, 2048], BF16))
        rtm2 = e(nc.sbuf_tensor("rtm2", [128, 2048], BF16))
        csb = e(nc.sbuf_tensor("csb", [128, 2048], BF16))
        snb = e(nc.sbuf_tensor("snb", [128, 2048], BF16))
        b1C = e(nc.sbuf_tensor("b1C", [128, 32], F32))
        b2B = e(nc.sbuf_tensor("b2B", [128, 1024], F32))
        identT = e(nc.sbuf_tensor("identT", [128, 128], BF16))
        scr = e(nc.sbuf_tensor("scr", [128, 1024], F32))
        sm = e(nc.sbuf_tensor("sm", [128, 1], F32))
        mean = e(nc.sbuf_tensor("mean", [128, 1], F32))
        var = e(nc.sbuf_tensor("var", [128, 1], F32))
        ur = e(nc.sbuf_tensor("ur", [128, 1], F32))
        pM = [e(nc.psum_tensor(f"pM{j}", [128, 512], F32)) for j in range(6)]
        pX = [e(nc.psum_tensor("pX0", [128, 128], BF16))]
        sc = Sched(stack, nc, {
            "T": ("T", 1, 800), "V": ("V", 1, 700), "A": ("A", 1, 600),
            "D": ("S", 16, 60), "G": ("G", 16, 40), "O": ("G", 16, 60),
        }, {"S": pM, "X": pX})
        sig, wt, run, grp = sc.sig, sc.wt, sc.run, sc.grp
        block = e(nc.Block())

        run("A", lambda a: a.preload_act_table(AF.Sqrt))
        run("A", lambda a: a.drain())
        mD = sig("D", lambda s: s.dma_start(out=identT[:], in_=identI))
        mD = sig("D", lambda s: s.dma_start(out=b1C[:], in_=b1CI))
        mD = sig("D", lambda s: s.dma_start(out=b2B[:], in_=b2BI))
        mD_small = mD
        mD_z3 = sig("D", lambda s: s.dma_start(
            out=z3s[:].rearrange("p (c t) -> p c t", c=8),
            in_=z3TI.rearrange("(c p) t -> p c t", p=128)))

        mV_init = 0
        for ti in range(16):
            sl = ti % 2
            if ti >= 2:
                wt("S", "V", mV_init - 1)
            mD_x = sig("D", lambda s, ti=ti, sl=sl: s.dma_start(
                out=xrt[:, 1024 * sl:1024 * (sl + 1)], in_=xresI[128 * ti:128 * (ti + 1), :]))
            wt("V", "D", mD_x)
            if ti == 0:
                wt("V", "D", mD_small)

            def init(v, ti=ti, sl=sl):
                v.tensor_tensor(SCR[:, 1024 * ti:1024 * (ti + 1)],
                                xrt[:, 1024 * sl:1024 * (sl + 1)], b2B[:], ALU.add)
                return v.drain()
            mV_init = sig("V", init)

        NG = 8
        mT_w_last = 0
        for g in range(NG):
            if g > 0:
                wt("G", "T", mT_w_last)
            mG_w1 = sig("G", lambda gp, g=g: gp.dma_start(
                out=wf1g[:].rearrange("p (c j) -> p c j", c=8),
                in_=W1e[:, 512 * g:512 * (g + 1)].rearrange("(c p) j -> p c j", p=128)))
            mG_w2 = sig("G", lambda gp, g=g: gp.dma_start(
                out=wbig[:, 0:4096].rearrange("p (c j) -> p c j", c=4),
                in_=W2e[512 * g:512 * (g + 1), :].rearrange("(c p) j -> p c j", p=128)))
            for oc in range(4):
                for tt in range(4):
                    def f1mm(t, bank, oc=oc, tt=tt):
                        inst = None
                        for c in range(8):
                            inst = t.matmul(bank[:, 0:512],
                                            wf1g[:, 512 * c + 128 * oc:512 * c + 128 * (oc + 1)],
                                            z3s[:, 2048 * c + 512 * tt:2048 * c + 512 * (tt + 1)],
                                            start=(c == 0), stop=(c == 7))
                        return inst

                    def f1ev(a, bank, oc=oc, tt=tt, g=g):
                        return a.activation(hR[:, 2048 * oc + 512 * tt:2048 * oc + 512 * (tt + 1)],
                                            bank[:, 0:512], AF.Relu, bias=b1C[:, 4 * g + oc:4 * g + oc + 1])
                    w = []
                    if oc == 0 and tt == 0:
                        w = [("G", mG_w1), ("D", mD_z3), ("D", mD_small)]
                    _, mA_h = grp("S", f1mm, "A", f1ev, waits=w)
            for st in range(16):
                for h in range(2):
                    def f2mm(t, bank, st=st, h=h):
                        inst = None
                        for c4 in range(4):
                            inst = t.matmul(bank[:, 0:512],
                                            hR[:, 2048 * c4 + 128 * st:2048 * c4 + 128 * (st + 1)],
                                            wbig[:, 1024 * c4 + 512 * h:1024 * c4 + 512 * (h + 1)],
                                            start=(c4 == 0), stop=(c4 == 3))
                        return inst

                    def f2ev(v, bank, st=st, h=h):
                        s = SCR[:, 1024 * st + 512 * h:1024 * st + 512 * (h + 1)]
                        return v.tensor_tensor(s, s, bank[:, 0:512], ALU.add)
                    w = []
                    if st == 0 and h == 0:
                        w = [("A", mA_h), ("G", mG_w2), ("V", mV_init)]
                    mT_w_last, mV_x2 = grp("S", f2mm, "V", f2ev, waits=w)

        mO_x2 = 0
        mT_tr = 0
        mT_trs = [0, 0]
        for ti in range(16):
            sl = ti % 2

            def lnstats(v, ti=ti):
                xv = SCR[:, 1024 * ti:1024 * (ti + 1)]
                emit_stats1024(v, xv, scr[:], sm[:], var[:], mean[:])
                emit_inv(v, var[:], ur[:])
                return v.drain()
            if ti == 0:
                wt("V", "V", mV_x2)
            mVs = sig("V", lnstats)
            wt("A", "V", mVs)
            mAs = sig("A", lambda a: a.activation(ur[:], ur[:], AF.Sqrt))
            wt("V", "A", mAs)
            if mT_trs[sl]:
                wt("V", "T", mT_trs[sl])

            def z4scale(v, ti=ti, sl=sl):
                xv = SCR[:, 1024 * ti:1024 * (ti + 1)]
                return v.tensor_scalar(z4t[:, 1024 * sl:1024 * (sl + 1)], xv, mean[:], ur[:],
                                       ALU.subtract, ALU.mult)
            mVz = sig("V", z4scale)
            wt("G", "V", mVz)
            mO_x2 = sig("O", lambda gp, ti=ti: gp.dma_start(
                out=x2o[128 * ti:128 * (ti + 1), :], in_=SCR[:, 1024 * ti:1024 * (ti + 1)]))
            for c in range(8):
                def trmm(t, bank, sl=sl, c=c):
                    return t.transpose(bank[:, 0:128],
                                       z4t[:, 1024 * sl + 128 * c:1024 * sl + 128 * (c + 1)], identT[:])

                def trev(a, bank, c=c, ti=ti):
                    return a.activation(z3s[:, 2048 * c + 128 * ti:2048 * c + 128 * (ti + 1)],
                                        bank[:], AF.Identity)
                w = []
                if c == 0:
                    w = [("V", mVz)]
                    if ti == 0:
                        w.append(("T", mT_w_last))
                mT_tr, _ = grp("X", trmm, "A", trev, waits=w)
            mT_trs[sl] = mT_tr

        wt("G", "O", sc.tick["O"])
        mG_cs = sig("G", lambda gp: gp.dma_start(out=SCR[:, 0:8192], in_=csQI))
        mG_cs = sig("G", lambda gp: gp.dma_start(out=SCR[:, 8192:16384], in_=snQI))
        mT_proj_last = 0
        mV_rot = 0
        for which in range(3):
            WI = (WqD, WkD, WvI)[which]
            if which > 0:
                wt("G", "T", mT_proj_last)
            mG_w = sig("G", lambda gp, WI=WI: gp.dma_start(
                out=wbig[:].rearrange("p (c j) -> p c j", c=8),
                in_=WI.rearrange("(c p) j -> p c j", p=128)))
            if which == 1:
                wt("G", "V", mV_rot)
                mG_cs = sig("G", lambda gp: gp.dma_start(out=SCR[:, 0:8192], in_=csKI))
                mG_cs = sig("G", lambda gp: gp.dma_start(out=SCR[:, 8192:16384], in_=snKI))
            if which < 2:
                for oc in range(8):
                    for tt in range(4):
                        def qmm(t, bank, oc=oc, tt=tt):
                            inst = None
                            for c in range(8):
                                inst = t.matmul(bank[:, 0:512],
                                                wbig[:, 1024 * c + 128 * oc:1024 * c + 128 * (oc + 1)],
                                                z3s[:, 2048 * c + 512 * tt:2048 * c + 512 * (tt + 1)],
                                                start=(c == 0), stop=(c == 7))
                            return inst

                        def qev(a, bank, oc=oc, tt=tt):
                            return a.activation(qkvB[:, 2048 * oc + 512 * tt:2048 * oc + 512 * (tt + 1)],
                                                bank[:, 0:512], AF.Identity)
                        w = []
                        if oc == 0 and tt == 0:
                            w = [("G", mG_w), ("T", mT_tr), ("A", sc.tick["A"])]
                            if which > 0:
                                w.append(("V", sc.tick["V"]))
                                w.append(("O", sc.tick["O"]))
                        mT_proj_last, mA_p = grp("S", qmm, "A", qev, waits=w)
                wt("V", "A", mA_p)
                wt("V", "G", mG_cs)

                def rot(v):
                    for c in range(4):
                        ev = qkvB[:, 2048 * c:2048 * (c + 1)]
                        od = qkvB[:, 2048 * (c + 4):2048 * (c + 5)]
                        v.tensor_copy(csb[:], SCR[:, 2048 * c:2048 * (c + 1)])
                        v.tensor_copy(snb[:], SCR[:, 8192 + 2048 * c:8192 + 2048 * (c + 1)])
                        v.tensor_copy(rtmp[:], ev)
                        v.drain()
                        v.tensor_tensor(ev, ev, csb[:], ALU.mult)
                        v.tensor_tensor(rtm2[:], od, snb[:], ALU.mult)
                        v.drain()
                        v.tensor_tensor(ev, ev, rtm2[:], ALU.subtract)
                        v.tensor_tensor(od, od, csb[:], ALU.mult)
                        v.drain()
                        v.tensor_tensor(rtm2[:], rtmp[:], snb[:], ALU.mult)
                        v.drain()
                        v.tensor_tensor(od, od, rtm2[:], ALU.add)
                        v.drain()
                    return v.drain()
                mV_rot = sig("V", rot)
                wt("G", "V", mV_rot)
                out_t = (QTo, KTo)[which]
                sig("O", lambda gp, out_t=out_t: gp.dma_start(
                    out=out_t.rearrange("(c p) t -> p c t", p=128),
                    in_=qkvB[:].rearrange("p (c t) -> p c t", c=8)))
            else:
                for st in range(16):
                    for h in range(2):
                        def vmm(t, bank, st=st, h=h):
                            inst = None
                            for c in range(8):
                                inst = t.matmul(bank[:, 0:512],
                                                z3s[:, 2048 * c + 128 * st:2048 * c + 128 * (st + 1)],
                                                wbig[:, 1024 * c + 512 * h:1024 * c + 512 * (h + 1)],
                                                start=(c == 0), stop=(c == 7))
                            return inst

                        def vev(a, bank, st=st, h=h):
                            return a.activation(qkvB[:, 1024 * st + 512 * h:1024 * st + 512 * (h + 1)],
                                                bank[:, 0:512], AF.Identity)
                        w = []
                        if st == 0 and h == 0:
                            w = [("G", mG_w), ("V", sc.tick["V"]), ("O", sc.tick["O"])]
                        mT_proj_last, mA_p = grp("S", vmm, "A", vev, waits=w)
                wt("G", "A", mA_p)
                sig("O", lambda gp: gp.dma_start(
                    out=Vo.rearrange("(s p) j -> p s j", p=128),
                    in_=qkvB[:].rearrange("p (s j) -> p s j", s=16)))

        wt("G", "O", sc.tick["O"])
        wt("G", "T", sc.tick["T"])
        wt("G", "V", sc.tick["V"])
        wt("G", "A", sc.tick["A"])
        sig("G", lambda gp: gp.drain())
        sc.finalize(block)
    return nc


def _xpos_tabs(start, downscale):
    half = C // 2
    j = np.arange(half, dtype=np.float64)
    scale_vec = (2.0 * j + 0.4 * C) / (1.4 * C)
    pos = np.arange(start, start + TPC, dtype=np.float64)
    scale = scale_vec[None, :] ** (pos[:, None] / 512.0)
    if downscale:
        scale = 1.0 / scale
    inv_freq = 1.0 / 10000.0 ** (j / half)
    ang = pos[:, None] * inv_freq[None, :]
    gam = GAMMA ** ((pos % 128.0) if not downscale else -(pos % 128.0))
    cs = np.cos(ang) * scale * gam[:, None]
    sn = np.sin(ang) * scale * gam[:, None]
    cs = cs.T.reshape(4, 128, TPC).transpose(1, 0, 2).reshape(128, 4 * TPC)
    sn = sn.T.reshape(4, 128, TPC).transpose(1, 0, 2).reshape(128, 4 * TPC)
    z = np.zeros((128, 8192), np.float32)
    z[:, 0:4 * TPC] = cs
    z2_ = np.zeros((128, 8192), np.float32)
    z2_[:, 0:4 * TPC] = sn
    return np.ascontiguousarray(z), np.ascontiguousarray(z2_)


def host_pD_inputs(inputs, xres, z3, experts):
    maps = []
    deint = lambda W: np.ascontiguousarray(
        np.concatenate([np.asarray(W, np.float32)[:, 0::2], np.asarray(W, np.float32)[:, 1::2]], axis=1))
    WqD = deint(inputs["L1_attn_W_Q"])
    WkD = deint(inputs["L1_attn_W_K"])
    WvI = np.asarray(inputs["L1_attn_W_V"], np.float32)
    identI = np.ascontiguousarray(np.eye(128, dtype=bf))
    for c in range(NC):
        b, q = c // 4, c % 4
        s0 = 2048 * q
        ei = experts[b]
        W1 = np.asarray(inputs["L0_ffn_W1"][ei], np.float32)
        W2 = np.asarray(inputs["L0_ffn_W2"][ei], np.float32)
        b1 = np.asarray(inputs["L0_ffn_b1"][ei], np.float32)
        b2 = np.asarray(inputs["L0_ffn_b2"][ei], np.float32)
        csQ, snQ = _xpos_tabs(s0, False)
        csK, snK = _xpos_tabs(s0, True)
        rows = slice(8192 * b + s0, 8192 * b + s0 + 2048)
        maps.append({
            "xresI": np.ascontiguousarray(np.asarray(xres, np.float32)[rows]),
            "z3TI": np.ascontiguousarray(np.asarray(z3, bf)[rows].T),
            "W1e": W1, "W2e": W2,
            "b1CI": np.ascontiguousarray(b1.reshape(32, 128).T),
            "b2BI": np.ascontiguousarray(np.broadcast_to(b2[None, :], (128, C))),
            "WqD": WqD, "WkD": WkD, "WvI": WvI,
            "csQI": csQ, "snQI": snQ, "csKI": csK, "snKI": snK,
            "identI": identI,
        })
    return maps


def run_pD(inputs, xres, z3, experts):
    nc = build_pD()
    res = _run(nc, host_pD_inputs(inputs, xres, z3, experts))
    x2 = np.concatenate([np.asarray(r["x2o"]) for r in res.results], 0)
    QT = [np.asarray(r["QTo"]) for r in res.results]
    KT = [np.asarray(r["KTo"]) for r in res.results]
    V = [np.asarray(r["Vo"]) for r in res.results]
    return x2, QT, KT, V



EWIN = 27
EKT = 4 * 15 + EWIN


def build_pE():
    nc = _bass()
    QTi = nc.dram_tensor("QTi", [C, TPC], BF16, kind="ExternalInput").ap()
    KTw = nc.dram_tensor("KTw", [C, EKT * 128], BF16, kind="ExternalInput").ap()
    Vw = nc.dram_tensor("Vw", [EKT * 128, C], BF16, kind="ExternalInput").ap()
    mleI = nc.dram_tensor("mleI", [128, 128], F32, kind="ExternalInput").ap()
    Eo = nc.dram_tensor("Eo", [TPC, C], F32, kind="ExternalOutput").ap()

    from contextlib import ExitStack
    with ExitStack() as stack:
        e = stack.enter_context
        qS = e(nc.sbuf_tensor("qS", [128, 8 * 2048], BF16))
        outacc = e(nc.sbuf_tensor("outacc", [128, 16 * 1024], F32))
        kG = [e(nc.sbuf_tensor(f"kG{j}", [128, 8 * 1024], BF16)) for j in range(2)]
        vG = [e(nc.sbuf_tensor(f"vG{j}", [128, 8 * 1024], BF16)) for j in range(2)]
        scB = e(nc.sbuf_tensor("scB", [128, 2 * 8 * 128], BF16))
        mleT = e(nc.sbuf_tensor("mleT", [128, 128], F32))
        pS = [e(nc.psum_tensor(f"pS{j}", [128, 128], F32)) for j in range(2)]
        pA = [e(nc.psum_tensor(f"pA{j}", [128, 512], F32)) for j in range(4)]
        sc = Sched(stack, nc, {
            "T": ("T", 1, 4000), "V": ("V", 1, 4000), "A": ("A", 1, 100),
            "D": ("S", 16, 200), "O": ("G", 16, 40), "G": ("G", 1, 10),
        }, {"P": pS, "AV": pA})
        sig, wt, run, grp = sc.sig, sc.wt, sc.run, sc.grp
        block = e(nc.Block())

        mD = sig("D", lambda s: s.dma_start(out=mleT[:], in_=mleI))
        mD = sig("D", lambda s: s.dma_start(
            out=qS[:].rearrange("p (c t) -> p c t", c=8),
            in_=QTi.rearrange("(c p) t -> p c t", p=128)))
        mD_q = mD

        def vinit(v):
            v.memset(outacc[:], 0.0)
            return v.drain()
        mV_init = sig("V", vinit)

        NKG = (EKT + 7) // 8
        mT_kv = [0, 0]
        mD_kv = [0, 0]
        for kg in range(NKG):
            gsl = kg % 2
            j0 = 8 * kg
            nj = min(8, EKT - j0)
            if mT_kv[gsl]:
                wt("S", "T", mT_kv[gsl])
            for jr in range(nj):
                mD_kv[gsl] = sig("D", lambda s, j=j0 + jr, jr=jr, gsl=gsl: s.dma_start(
                    out=kG[gsl][:, 1024 * jr:1024 * (jr + 1)].rearrange("p (c t) -> p c t", c=8),
                    in_=KTw.rearrange("(c p) t -> p c t", p=128)[:, :, 128 * j:128 * (j + 1)]))
                mD_kv[gsl] = sig("D", lambda s, j=j0 + jr, jr=jr, gsl=gsl: s.dma_start(
                    out=vG[gsl][:, 1024 * jr:1024 * (jr + 1)],
                    in_=Vw[128 * j:128 * (j + 1), :]))
            qlo = max(0, -(-(j0 - 26) // 4))
            qhi = min(15, (j0 + 7) // 4)
            for qi in range(qlo, qhi + 1):
                js = [j for j in range(j0, j0 + nj) if 4 * qi <= j <= 4 * qi + 26]
                if not js:
                    continue
                mV_sc = 0
                for ji, j in enumerate(js):
                    dlt = 4 * qi + 26 - j
                    g64 = float(GAMMA ** (64.0 * dlt))

                    def smm(t, bank, j=j, qi=qi, gsl=gsl, j0=j0):
                        inst = None
                        for c in range(8):
                            inst = t.matmul(bank[:, 0:128],
                                            kG[gsl][:, 1024 * (j - j0) + 128 * c:1024 * (j - j0) + 128 * (c + 1)],
                                            qS[:, 2048 * c + 128 * qi:2048 * c + 128 * (qi + 1)],
                                            start=(c == 0), stop=(c == 7))
                        return inst

                    def sev(v, bank, ji=ji, dlt=dlt, g64=g64, gsl=gsl):
                        dst = scB[:, 1024 * gsl + 128 * ji:1024 * gsl + 128 * (ji + 1)]
                        if dlt == 0:
                            v.tensor_tensor(dst, bank[:], mleT[:], ALU.mult)
                        else:
                            v.tensor_scalar(dst, bank[:], g64, g64, ALU.mult, ALU.mult)
                        return v.drain()
                    w = []
                    if kg == 0 and qi == qlo and ji == 0:
                        w = [("D", mD_q), ("D", mD)]
                    if ji == 0:
                        w = w + [("D", mD_kv[gsl])]
                    _, mV_sc = grp("P", smm, "V", sev, waits=w)
                for h in range(2):
                    def avmm(t, bank, js=js, j0=j0, qi=qi, h=h, gsl=gsl):
                        inst = None
                        for ji in range(len(js)):
                            inst = t.matmul(bank[:, 0:512],
                                            scB[:, 1024 * gsl + 128 * ji:1024 * gsl + 128 * (ji + 1)],
                                            vG[gsl][:, 1024 * (js[ji] - j0) + 512 * h:1024 * (js[ji] - j0) + 512 * (h + 1)],
                                            start=(ji == 0), stop=(ji == len(js) - 1))
                        return inst

                    def avev(v, bank, qi=qi, h=h):
                        s = outacc[:, 1024 * qi + 512 * h:1024 * qi + 512 * (h + 1)]
                        return v.tensor_tensor(s, s, bank[:, 0:512], ALU.add)
                    w = [("V", mV_sc)] if h == 0 else []
                    if kg == 0 and qi == qlo and h == 0:
                        w.append(("V", mV_init))
                    mT_avg, _ = grp("AV", avmm, "V", avev, waits=w)
                mT_kv[gsl] = mT_avg

        wt("G", "V", sc.tick["V"])
        for qi in range(16):
            sig("O", lambda gp, qi=qi: gp.dma_start(
                out=Eo[128 * qi:128 * (qi + 1), :], in_=outacc[:, 1024 * qi:1024 * (qi + 1)]))
        wt("G", "O", sc.tick["O"])
        wt("G", "T", sc.tick["T"])
        sig("G", lambda gp: gp.drain())
        sc.finalize(block)
    return nc


def run_pE(QT, KT, V):
    nc = build_pE()
    mle = (np.arange(128)[:, None] <= np.arange(128)[None, :]).astype(np.float32)
    maps = []
    for c in range(NC):
        b, q = c // 4, c % 4
        KTb = np.concatenate([KT[4 * b + j] for j in range(4)], axis=1)
        Vb = np.concatenate([V[4 * b + j] for j in range(4)], axis=0)
        QTb = np.concatenate([QT[4 * b + j] for j in range(4)], axis=1)
        qcols = np.concatenate([QTb[:, 128 * (q + 4 * i):128 * (q + 4 * i + 1)]
                                for i in range(16)], axis=1)
        KTw = np.zeros((C, EKT * 128), bf)
        Vw = np.zeros((EKT * 128, C), bf)
        for j in range(EKT):
            gt = q - 26 + j
            if 0 <= gt < 64:
                KTw[:, 128 * j:128 * (j + 1)] = KTb[:, 128 * gt:128 * (gt + 1)]
                Vw[128 * j:128 * (j + 1), :] = Vb[128 * gt:128 * (gt + 1), :]
        maps.append({"QTi": np.ascontiguousarray(qcols), "KTw": KTw, "Vw": Vw, "mleI": mle})
    res = _run(nc, maps)
    ret = np.zeros((B, T, C), np.float32)
    for c in range(NC):
        b, q = c // 4, c % 4
        eo = np.asarray(res.results[c]["Eo"])
        for i in range(16):
            gt = q + 4 * i
            ret[b, 128 * gt:128 * (gt + 1)] = eo[128 * i:128 * (i + 1)]
    return ret.reshape(B * T, C)



NEXP = 16


def build_pF():
    nc = _bass()
    x2I = nc.dram_tensor("x2I", [TPC, C], F32, kind="ExternalInput").ap()
    retI = nc.dram_tensor("retI", [TPC, C], F32, kind="ExternalInput").ap()
    eW1s = nc.dram_tensor("eW1s", [NEXP * C, C], F32, kind="ExternalInput").ap()
    eW2s = nc.dram_tensor("eW2s", [NEXP * C, C], F32, kind="ExternalInput").ap()
    eb1C = nc.dram_tensor("eb1C", [128, 8 * NEXP], F32, kind="ExternalInput").ap()
    eb2B = nc.dram_tensor("eb2B", [128, NEXP * C], F32, kind="ExternalInput").ap()
    rWp = nc.dram_tensor("rWp", [C, 16], F32, kind="ExternalInput").ap()
    rbB = nc.dram_tensor("rbB", [128, 16], F32, kind="ExternalInput").ap()
    identI = nc.dram_tensor("identI", [128, 128], BF16, kind="ExternalInput").ap()
    x4o = nc.dram_tensor("x4o", [TPC, C], F32, kind="ExternalOutput").ap()

    from contextlib import ExitStack
    with ExitStack() as stack:
        e = stack.enter_context
        x4acc = e(nc.sbuf_tensor("x4acc", [128, 16 * 1024], F32))
        z5T = e(nc.sbuf_tensor("z5T", [128, 8 * 2048], BF16))
        hE = e(nc.sbuf_tensor("hE", [128, 8 * 2048], BF16))
        ew1b = [e(nc.sbuf_tensor(f"ew1b{j}", [128, 8192], BF16)) for j in range(2)]
        ew2 = e(nc.sbuf_tensor("ew2", [128, 8192], BF16))
        x2t = e(nc.sbuf_tensor("x2t", [128, 2 * 1024], F32))
        rtt = e(nc.sbuf_tensor("rtt", [128, 2 * 1024], F32))
        z5t = e(nc.sbuf_tensor("z5t", [128, 2 * 1024], BF16))
        b1c = e(nc.sbuf_tensor("b1c", [128, 8 * NEXP], F32))
        b2s = e(nc.sbuf_tensor("b2s", [128, 1024], BF16))
        cwA = e(nc.sbuf_tensor("cwA", [128, 256], F32))
        onesPF = e(nc.sbuf_tensor("onesPF", [128, 128], BF16))
        rwb = e(nc.sbuf_tensor("rwb", [128, 8 * 16], BF16))
        rbb = e(nc.sbuf_tensor("rbb", [128, 16], F32))
        lg = e(nc.sbuf_tensor("lg", [128, 16], F32))
        pr = e(nc.sbuf_tensor("pr", [128, 16], F32))
        prm = e(nc.sbuf_tensor("prm", [128, 16], F32))
        ohx = e(nc.sbuf_tensor("ohx", [128, 16], F32))

        mx = e(nc.sbuf_tensor("mx", [128, 1], F32))
        nmx = e(nc.sbuf_tensor("nmx", [128, 1], F32))
        ssum = e(nc.sbuf_tensor("ssum", [128, 1], F32))
        identT = e(nc.sbuf_tensor("identT", [128, 128], BF16))
        scr = e(nc.sbuf_tensor("scr", [128, 1024], F32))
        sm = e(nc.sbuf_tensor("sm", [128, 1], F32))
        mean = e(nc.sbuf_tensor("mean", [128, 1], F32))
        var = e(nc.sbuf_tensor("var", [128, 1], F32))
        ur = e(nc.sbuf_tensor("ur", [128, 1], F32))
        pM = [e(nc.psum_tensor(f"pM{j}", [128, 512], F32)) for j in range(6)]
        pX = [e(nc.psum_tensor("pX0", [128, 128], BF16))]
        pR = [e(nc.psum_tensor("pR0", [128, 16], F32))]
        sc = Sched(stack, nc, {
            "T": ("T", 1, 5200), "V": ("V", 1, 3000), "A": ("A", 1, 4000),
            "D": ("S", 16, 60), "G": ("G", 16, 150), "O": ("G", 16, 60),
        }, {"S": pM, "X": pX, "R": pR})
        sig, wt, run, grp = sc.sig, sc.wt, sc.run, sc.grp
        block = e(nc.Block())

        run("A", lambda a: a.preload_act_table(AF.Sqrt))
        run("A", lambda a: a.drain())
        sig("V", lambda v: (v.memset(onesPF[:], 1.0), v.drain())[1])
        mD = sig("D", lambda s: s.dma_start(out=identT[:], in_=identI))
        mD = sig("D", lambda s: s.dma_start(out=b1c[:], in_=eb1C))
        mD = sig("D", lambda s: s.dma_start(out=rbb[:], in_=rbB))
        mD_small = mD
        mG_rw = sig("G", lambda gp: gp.dma_start(
            out=rwb[:].rearrange("p (c j) -> p c j", c=8),
            in_=rWp.rearrange("(c p) j -> p c j", p=128)))

        mT_tr = 0
        mT_trs = [0, 0]
        mV_ln = 0
        for ti in range(16):
            sl = ti % 2
            if ti >= 2:
                wt("S", "V", mV_ln)
            mD_x = sig("D", lambda s, ti=ti, sl=sl: s.dma_start(
                out=x2t[:, 1024 * sl:1024 * (sl + 1)], in_=x2I[128 * ti:128 * (ti + 1), :]))
            mD_r = sig("D", lambda s, ti=ti, sl=sl: s.dma_start(
                out=rtt[:, 1024 * sl:1024 * (sl + 1)], in_=retI[128 * ti:128 * (ti + 1), :]))
            wt("V", "D", mD_r)

            def x3i(v, ti=ti, sl=sl):
                v.tensor_tensor(x4acc[:, 1024 * ti:1024 * (ti + 1)],
                                x2t[:, 1024 * sl:1024 * (sl + 1)],
                                rtt[:, 1024 * sl:1024 * (sl + 1)], ALU.add)
                return v.drain()
            sig("V", x3i)

            def lnstats(v, ti=ti):
                xv = x4acc[:, 1024 * ti:1024 * (ti + 1)]
                emit_stats1024(v, xv, scr[:], sm[:], var[:], mean[:])
                emit_inv(v, var[:], ur[:])
                return v.drain()
            mVs = sig("V", lnstats)
            wt("A", "V", mVs)
            mAs = sig("A", lambda a: a.activation(ur[:], ur[:], AF.Sqrt))
            wt("V", "A", mAs)
            if mT_trs[sl]:
                wt("V", "T", mT_trs[sl])

            def z5s(v, ti=ti, sl=sl):
                xv = x4acc[:, 1024 * ti:1024 * (ti + 1)]
                return v.tensor_scalar(z5t[:, 1024 * sl:1024 * (sl + 1)], xv, mean[:], ur[:],
                                       ALU.subtract, ALU.mult)
            mV_ln = sig("V", z5s)
            for c in range(8):
                def trmm(t, bank, sl=sl, c=c):
                    return t.transpose(bank[:, 0:128],
                                       z5t[:, 1024 * sl + 128 * c:1024 * sl + 128 * (c + 1)], identT[:])

                def trev(a, bank, c=c, ti=ti):
                    return a.activation(z5T[:, 2048 * c + 128 * ti:2048 * c + 128 * (ti + 1)],
                                        bank[:], AF.Identity)
                w = []
                if c == 0:
                    w = [("V", mV_ln)]
                    if ti == 0:
                        w.append(("D", mD_small))
                mT_tr, mA_tr = grp("X", trmm, "A", trev, waits=w)
            mT_trs[sl] = mT_tr

        run("A", lambda a: a.preload_act_table(AF.Exp))
        run("A", lambda a: a.drain())
        mO_cw = 0
        for ti in range(16):
            def rmm(t, bank, ti=ti):
                inst = None
                for c in range(8):
                    inst = t.matmul(bank[:, 0:16], z5T[:, 2048 * c + 128 * ti:2048 * c + 128 * (ti + 1)],
                                    rwb[:, 16 * c:16 * (c + 1)], start=(c == 0), stop=(c == 7))
                return inst

            def rev(v, bank, ti=ti):
                return v.tensor_tensor(lg[:], bank[:, 0:16], rbb[:], ALU.add)
            w = [("A", mA_tr), ("G", mG_rw)] if ti == 0 else []
            _, mV_lg = grp("R", rmm, "V", rev, waits=w)

            def sfm1(v):
                v.tensor_reduce(mx[:], lg[:], AX, ALU.max)
                v.drain()
                v.tensor_scalar(nmx[:], mx[:], -1.0, None, ALU.mult)
                return v.drain()
            mV_s1 = sig("V", sfm1)
            wt("A", "V", mV_s1)
            mA_e = sig("A", lambda a: a.activation(lg[:], lg[:], AF.Exp, bias=nmx[:, 0:1]))
            wt("V", "A", mA_e)

            def sfm2(v, ti=ti):
                v.tensor_reduce(ssum[:], lg[:], AX, ALU.add)
                v.drain()
                v.reciprocal(ssum[:], ssum[:])
                v.drain()
                v.tensor_scalar(pr[:], lg[:], ssum[:, 0:1], None, ALU.mult)
                v.drain()
                v.tensor_reduce(mx[:], pr[:], AX, ALU.max)
                v.drain()
                v.tensor_scalar(ohx[:], pr[:], mx[:, 0:1], None, ALU.is_ge)
                v.drain()
                v.tensor_tensor(scr[:, 0:16], ohx[:], pr[:], ALU.mult)
                v.drain()
                v.tensor_tensor(prm[:], pr[:], scr[:, 0:16], ALU.subtract)
                v.drain()
                v.tensor_reduce(mx[:], prm[:], AX, ALU.max)
                v.drain()
                v.tensor_scalar(scr[:, 16:32], prm[:], mx[:, 0:1], None, ALU.is_ge)
                v.drain()
                v.tensor_tensor(ohx[:], ohx[:], scr[:, 16:32], ALU.add)
                v.tensor_tensor(scr[:, 0:16], scr[:, 16:32], prm[:], ALU.mult)
                v.drain()
                v.tensor_tensor(prm[:], prm[:], scr[:, 0:16], ALU.subtract)
                v.drain()
                v.tensor_reduce(mx[:], prm[:], AX, ALU.max)
                v.drain()
                v.tensor_scalar(scr[:, 16:32], prm[:], mx[:, 0:1], None, ALU.is_ge)
                v.drain()
                v.tensor_tensor(ohx[:], ohx[:], scr[:, 16:32], ALU.add)
                v.drain()
                v.tensor_tensor(cwA[:, 16 * ti:16 * (ti + 1)], ohx[:], pr[:], ALU.mult)
                v.drain()
                v.memset(cwA[:, 16 * ti + 15:16 * ti + 16], 1.0)
                return v.drain()
            mV_cw = sig("V", sfm2)

        mT_e_last = 0
        m_f1 = [0, 0]
        for ex in range(NEXP):
            esl = ex % 2
            ew1 = ew1b[esl]
            if m_f1[esl]:
                wt("G", "T", m_f1[esl])
            mG_w1 = sig("G", lambda gp, ex=ex, ew1=ew1: gp.dma_start(
                out=ew1[:].rearrange("p (c j) -> p c j", c=8),
                in_=eW1s[C * ex:C * (ex + 1), :].rearrange("(c p) j -> p c j", p=128)))
            if ex > 0:
                wt("G", "T", mT_e_last)
            mG_w2 = sig("G", lambda gp, ex=ex: gp.dma_start(
                out=ew2[:].rearrange("p (c j) -> p c j", c=8),
                in_=eW2s[C * ex:C * (ex + 1), :].rearrange("(c p) j -> p c j", p=128)))
            mG_b2 = sig("G", lambda gp, ex=ex: gp.dma_start(
                out=b2s[:], in_=eb2B[:, C * ex:C * (ex + 1)]))
            wt("V", "G", mG_b2)
            if ex > 0:
                wt("V", "T", mT_e_last)

            def b2scale(v):
                v.tensor_scalar(b2s[:], b2s[:], 1.0 / 128.0, None, ALU.mult)
                return v.drain()
            mV_b2 = sig("V", b2scale)

            mA_h = 0
            for oc in range(8):
                for tt in range(4):
                    def f1mm(t, bank, oc=oc, tt=tt, ew1=ew1):
                        inst = None
                        for c in range(8):
                            inst = t.matmul(bank[:, 0:512],
                                            ew1[:, 1024 * c + 128 * oc:1024 * c + 128 * (oc + 1)],
                                            z5T[:, 2048 * c + 512 * tt:2048 * c + 512 * (tt + 1)],
                                            start=(c == 0), stop=(c == 7))
                        return inst

                    def f1ev(a, bank, oc=oc, tt=tt, ex=ex):
                        return a.activation(hE[:, 2048 * oc + 512 * tt:2048 * oc + 512 * (tt + 1)],
                                            bank[:, 0:512], AF.Relu, bias=b1c[:, 8 * ex + oc:8 * ex + oc + 1])
                    w = []
                    if oc == 0 and tt == 0:
                        w = [("G", mG_w1)]
                    mT_f1x, mA_h = grp("S", f1mm, "A", f1ev, waits=w)
            m_f1[esl] = mT_f1x
            for st in range(16):
                for h in range(2):
                    def f2mm(t, bank, st=st, h=h):
                        for c in range(8):
                            t.matmul(bank[:, 0:512],
                                     hE[:, 2048 * c + 128 * st:2048 * c + 128 * (st + 1)],
                                     ew2[:, 1024 * c + 512 * h:1024 * c + 512 * (h + 1)],
                                     start=(c == 0), stop=False)
                        return t.matmul(bank[:, 0:512], onesPF[:],
                                        b2s[:, 512 * h:512 * (h + 1)], start=False, stop=True)

                    def f2ev(v, bank, st=st, h=h, ex=ex):
                        s = x4acc[:, 1024 * st + 512 * h:1024 * st + 512 * (h + 1)]
                        return v.scalar_tensor_tensor(s, bank[:, 0:512],
                                                      cwA[:, 16 * st + ex:16 * st + ex + 1],
                                                      s, ALU.mult, ALU.add)
                    w = []
                    if st == 0 and h == 0:
                        w = [("A", mA_h), ("V", mV_b2), ("G", mG_w2)]
                    mT_e_last, mV_x4 = grp("S", f2mm, "V", f2ev, waits=w)

        wt("G", "V", sc.tick["V"])
        for ti in range(16):
            sig("O", lambda gp, ti=ti: gp.dma_start(
                out=x4o[128 * ti:128 * (ti + 1), :], in_=x4acc[:, 1024 * ti:1024 * (ti + 1)]))

        wt("G", "O", sc.tick["O"])
        wt("G", "T", sc.tick["T"])
        wt("G", "A", sc.tick["A"])
        sig("G", lambda gp: gp.drain())
        sc.finalize(block)
    return nc


def run_pF(inputs, x2, ret):
    nc = build_pF()
    pre = lambda k: np.asarray(inputs["L1_ffn_" + k], np.float32)
    eW1 = np.concatenate([pre("eW1").reshape(15 * C, C), pre("sW1")], axis=0)
    eW2 = np.concatenate([pre("eW2").reshape(15 * C, C), pre("sW2")], axis=0)
    eb1 = np.concatenate([pre("eb1"), pre("sb1")[None]], axis=0)
    eb2 = np.concatenate([pre("eb2"), pre("sb2")[None]], axis=0)
    eb1C = np.concatenate([eb1[e].reshape(8, 128).T for e in range(NEXP)], axis=1)
    eb2B = np.concatenate([np.broadcast_to(eb2[e][None], (128, C)) for e in range(NEXP)], axis=1)
    rWp = np.zeros((C, 16), np.float32)
    rWp[:, 0:15] = pre("router_W")
    rbB = np.zeros((16,), np.float32)
    rbB[0:15] = pre("router_b")
    rbB[15] = -1e30
    rbB = np.ascontiguousarray(np.broadcast_to(rbB[None], (128, 16)))
    identI = np.ascontiguousarray(np.eye(128, dtype=bf))
    maps = []
    for c in range(NC):
        rows = slice(2048 * c, 2048 * (c + 1))
        maps.append({
            "x2I": np.ascontiguousarray(np.asarray(x2, np.float32)[rows]),
            "retI": np.ascontiguousarray(np.asarray(ret, np.float32)[rows]),
            "eW1s": eW1, "eW2s": eW2,
            "eb1C": np.ascontiguousarray(eb1C), "eb2B": np.ascontiguousarray(eb2B),
            "rWp": rWp, "rbB": rbB, "identI": identI,
        })
    res = _run(nc, maps)
    return np.concatenate([np.asarray(r["x4o"]) for r in res.results], 0)


def kernel(**inputs):
    xp, z2 = run_pA(inputs)
    og = run_pB(inputs, z2)
    xres, z3, zsum = run_pC(inputs, xp, og)
    ohs = run_pD2(inputs, zsum)
    experts = [int(np.argmax(np.asarray(oh))) for oh in ohs]
    x2, QT, KT, V = run_pD(inputs, xres, z3, experts)
    ret = run_pE(QT, KT, V)
    x4 = run_pF(inputs, x2, ret)
    return np.ascontiguousarray(x4.reshape(B, T, C).astype(np.float32))
